# version 09
# baseline (speedup 1.0000x reference)
import os
import numpy as np
import ml_dtypes
import concourse.bass as bass
import concourse.mybir as mybir
from concourse.bass_utils import run_bass_kernel_spmd

F32 = mybir.dt.float32
BF16 = mybir.dt.bfloat16
AF = mybir.ActivationFunctionType
ALU = mybir.AluOpType
AX = mybir.AxisListType

S = 2048
D = 1024
NT = 16
H = 16
HD = 64
NP = 8
ED = 4096
NE = 8
NSEQ = 2
C0 = 1920
KB = 1024
G65 = H * 65


class Sem:
    def __init__(self, nc, name):
        self.cm = nc.semaphore(name)
        self.h = self.cm.__enter__()
        self.n = 0


class Plan:
    ENGS = ("sync", "gpsimd", "tensor", "scalar", "vector")

    def __init__(self, nc):
        self.nc = nc
        self.q = {e: [] for e in self.ENGS}

    def add(self, eng, waits, fn, inc=None, k=1):
        wcap = [(sm.h, v) for (sm, v) in waits]
        if inc is not None:
            inc.n += k
        ih = None if inc is None else inc.h

        def run(e, wcap=wcap, fn=fn, ih=ih, k=k):
            for (h, v) in wcap:
                if v > 0:
                    e.wait_ge(h, v)
            ins = fn(e)
            if ih is not None:
                ins.then_inc(ih, k)
        self.q[eng].append(run)

    def flush(self, name=None):
        q = self.q
        self.q = {e: [] for e in self.ENGS}
        with self.nc.Block(name) as blk:
            for ename in self.ENGS:
                fns = q[ename]
                if not fns:
                    continue

                def body(eng, fns=fns):
                    for f in fns:
                        f(eng)
                getattr(blk, ename)(body)


def build_program(upto="full"):
    nc = bass.Bass("TRN2", target_bir_lowering=False)

    x_d = nc.dram_tensor("x", [NSEQ, S, D], F32, kind="ExternalInput").ap()
    wqk_d = nc.dram_tensor("wqk", [D, D], F32, kind="ExternalInput").ap()
    wvT_d = nc.dram_tensor("wvT", [D, D], F32, kind="ExternalInput").ap()
    slope_d = nc.dram_tensor("slope", [1, 1], F32, kind="ExternalInput").ap()
    n1_d = nc.dram_tensor("n1", [1, D], F32, kind="ExternalInput").ap()
    n2_d = nc.dram_tensor("n2", [1, D], F32, kind="ExternalInput").ap()
    qkvT_d = nc.dram_tensor("qkvT", [D, 3 * D], F32, kind="ExternalInput").ap()
    owT_d = nc.dram_tensor("owT", [D, D], F32, kind="ExternalInput").ap()
    gateT_d = nc.dram_tensor("gateT", [D, NE], F32, kind="ExternalInput").ap()
    upT_d = nc.dram_tensor("upT", [D, ED], F32, kind="ExternalInput").ap()
    downT_d = nc.dram_tensor("downT", [ED, D], F32, kind="ExternalInput").ap()
    identbf_d = nc.dram_tensor("identbf", [128, 128], BF16, kind="ExternalInput").ap()
    identf_d = nc.dram_tensor("identf", [128, 128], F32, kind="ExternalInput").ap()
    onescol_d = nc.dram_tensor("onescol", [1, 128], F32, kind="ExternalInput").ap()
    onescolb_d = nc.dram_tensor("onescolb", [128, 128], BF16, kind="ExternalInput").ap()
    ones32_d = nc.dram_tensor("ones32", [128, 32], BF16, kind="ExternalInput").ap()
    rampA_d = nc.dram_tensor("rampA", [128, 2048], F32, kind="ExternalInput").ap()
    maskA_d = nc.dram_tensor("maskA", [128, 128], F32, kind="ExternalInput").ap()
    maskB_d = nc.dram_tensor("maskB", [128, 128], F32, kind="ExternalInput").ap()
    y_d = nc.dram_tensor("y", [NSEQ, S, D], F32, kind="ExternalOutput").ap()
    dbg_d = nc.dram_tensor("dbg", [128, 4096], F32, kind="ExternalOutput").ap()

    X1R = nc.alloc_sbuf_tensor("X1R", [128, NT * D], F32).ap()
    factor = nc.alloc_sbuf_tensor("factor", [128, NT], F32).ap()
    maskA = nc.alloc_sbuf_tensor("maskA_s", [128, 128], F32).ap()
    maskB = nc.alloc_sbuf_tensor("maskB_s", [128, 128], F32).ap()
    identbf = nc.alloc_sbuf_tensor("identbf_s", [128, 128], BF16).ap()
    identf = nc.alloc_sbuf_tensor("identf_s", [128, 128], F32).ap()
    onescol = nc.alloc_sbuf_tensor("onescol_s", [1, 128], F32).ap()
    onescolb = nc.alloc_sbuf_tensor("onescolb_s", [128, 128], BF16).ap()
    ones32 = nc.alloc_sbuf_tensor("ones32_s", [128, 32], BF16).ap()
    slope_sb = nc.alloc_sbuf_tensor("slope_sb", [1, 32], F32).ap()
    slope_bc = nc.alloc_sbuf_tensor("slope_bc", [128, 1], F32).ap()
    epsA = nc.alloc_sbuf_tensor("epsA", [128, 8], F32).ap()
    eps1 = nc.alloc_sbuf_tensor("eps1", [1, 8], F32).ap()
    stat = nc.alloc_sbuf_tensor("stat", [128, 64], F32).ap()
    mu1 = nc.alloc_sbuf_tensor("mu1", [1, 16], F32).ap()
    mu1b = nc.alloc_sbuf_tensor("mu1b", [8, 8], BF16).ap()
    nbr = nc.alloc_sbuf_tensor("nbr", [128, 8], F32).ap()
    nbrb = nc.alloc_sbuf_tensor("nbrb", [128, 8], BF16).ap()
    biasH = nc.alloc_sbuf_tensor("biasH", [128, 2], F32).ap()

    TOP = 229344
    ARENA = 138 * KB
    AB = TOP - ARENA

    def at(name, off, shape, dtype):
        esz = 4 if dtype == F32 else 2
        assert off + shape[1] * esz <= ARENA, (name, off, shape)
        return nc.alloc_sbuf_tensor_at(name, shape, dtype, offset=AB + off).ap()

    XT = [at(f"XT{c}", c * 4 * KB, [128, S], BF16) for c in range(8)]
    Vsb = at("Vsb", 32 * KB, [128, NT * D], BF16)
    wslot = at("wslot", 64 * KB, [128, 8 * D], BF16)
    Sstrip = [at(f"Sstrip{i}", 80 * KB + i * 8 * KB, [128, S], F32) for i in range(2)]
    Pstrip = [at(f"Pstrip{i}", 96 * KB + i * 4 * KB, [128, S], BF16) for i in range(2)]
    PTring = at("PTring", 104 * KB, [128, 4 * 128], BF16)
    qpt = [at(f"qpt{m}", 105 * KB + m * 512, [128, 256], BF16) for m in range(8)]
    xres = [at(f"xres{i}", 109 * KB + i * 4 * KB, [128, D], F32) for i in range(2)]
    xbf = [at(f"xbf{i}", 117 * KB + i * 2 * KB, [128, D], BF16) for i in range(2)]
    Gfull = at("Gfull", 121 * KB, [128, 2048], F32)
    sqscrA = at("sqscrA", 129 * KB, [128, D], BF16)

    h1T = [at(f"h1T{c}", c * 4 * KB, [128, S], BF16) for c in range(8)]
    Vplus = at("Vplus", 32 * KB, [128, NT * G65], BF16)
    qkpT = at("qkpT", 65 * KB, [128, 2 * S], BF16)
    negMrowA = at("negMrowA", 73 * KB, [65, S], BF16)
    qkvw = [at(f"qkvw{i}", 89 * KB + i * 4 * KB, [128, 2048], BF16) for i in range(2)]
    Ebuf2 = [at(f"Ebuf2_{i}", 97 * KB + i * KB, [128, 512], BF16) for i in range(3)]
    n1rep = at("n1rep", 100 * KB, [128, D], F32)
    bscr = at("bscr", 104 * KB, [128, D], F32)
    n1row = at("n1row", 108 * KB, [1, D], F32)
    vwslot = at("vwslot", 65 * KB, [128, 8 * D], BF16)
    rsrow = at("rsrow", 100 * KB, [1, 512], F32)
    rsb = at("rsb", 102 * KB, [64, 512], F32)
    todd = at("todd", 104 * KB, [64, 2 * 256], BF16)
    negrow = at("negrow", 105 * KB, [32, 128], BF16)
    mnegbf = at("mnegbf", 105 * KB + 512, [128, 32], BF16)
    attnOT = [at(f"attnOT{c}", 106 * KB + c * 4 * KB, [128, S], BF16) for c in range(8)]
    dbgscr = at("dbgscr", 100 * KB, [128, 1024], F32)

    X2 = at("X2", 0, [128, NT * D], F32)
    owT = at("owT", 65 * KB, [128, 8 * D], BF16)
    n2rep = at("n2rep", 81 * KB, [128, D], F32)
    bscr2 = at("bscr2", 85 * KB, [128, D], F32)
    gbf = at("gbf", 89 * KB, [128, D], BF16)
    gateT_sb = at("gateT_sb", 91 * KB, [128, NE * 8], BF16)
    n2row = at("n2row", 92 * KB, [1, D], F32)
    gT = [at(f"gT{c}", 106 * KB + c * 4 * KB, [128, S], BF16) for c in range(8)]

    upw = [at(f"upw{i}", i * 16 * KB, [128, 8 * KB], BF16) for i in range(2)]
    dww = [at(f"dww{i}", 32 * KB + i * 16 * KB, [128, 8 * KB], BF16) for i in range(2)]
    siluT = [at(f"siluT{i}", 65 * KB + i * 8 * KB, [128, 8 * 512], BF16) for i in range(2)]
    lnscr = at("lnscr", 81 * KB, [128, D], BF16)
    outst = at("outst", 85 * KB, [128, 2 * D], F32)

    assert nc.sbuf_base <= AB, (nc.sbuf_base, AB)

    pA = nc.alloc_psum_tensor("pA", [128, 2048], F32).ap()
    pB = nc.alloc_psum_tensor("pB", [128, 1024], F32).ap()
    pC = nc.alloc_psum_tensor("pC", [128, 512], F32).ap()
    pT = nc.alloc_psum_tensor("pT", [128, 512], BF16).ap()

    def a(i):
        return pA[:, i * 512:(i + 1) * 512]

    sG = Sem(nc, "sG")
    sS = Sem(nc, "sS")
    sO = Sem(nc, "sO")
    sPE = Sem(nc, "sPE")
    sACT = Sem(nc, "sACT")
    sDVE = Sem(nc, "sDVE")
    sCP = Sem(nc, "sCP")
    dX = [Sem(nc, "dX0"), Sem(nc, "dX1")]
    dQW = [Sem(nc, "dQW0"), Sem(nc, "dQW1")]
    dUW = [Sem(nc, "dUW0"), Sem(nc, "dUW1")]
    dTD = [Sem(nc, "dTD0"), Sem(nc, "dTD1")]
    dWS = Sem(nc, "dWS")

    pl = Plan(nc)
    A = pl.add

    A("sync", [], lambda e: e.dma_start(maskA, maskA_d), inc=sS, k=16)
    A("sync", [], lambda e: e.dma_start(maskB, maskB_d), inc=sS, k=16)
    A("sync", [], lambda e: e.dma_start(identbf, identbf_d), inc=sS, k=16)
    A("sync", [], lambda e: e.dma_start(identf, identf_d), inc=sS, k=16)
    A("sync", [], lambda e: e.dma_start(onescol, onescol_d), inc=sS, k=16)
    A("sync", [], lambda e: e.dma_start(onescolb, onescolb_d), inc=sS, k=16)
    A("sync", [], lambda e: e.dma_start(ones32, ones32_d), inc=sS, k=16)
    A("sync", [], lambda e: e.dma_start(slope_sb[0:1, 0:1], slope_d), inc=sS, k=16)
    A("vector", [], lambda e: e.memset(epsA[:, 0:1], 1e-5))
    A("vector", [], lambda e: e.memset(epsA[:, 1:2], 1e-6))
    A("vector", [], lambda e: e.memset(eps1[0:1, 0:1], 1e-38), inc=sDVE)
    A("tensor", [(sS, sS.n)],
      lambda e: e.matmul(pC[:, 0:1], onescol[0:1, 0:128], slope_sb[0:1, 0:1],
                         start=True, stop=True), inc=sPE)
    A("vector", [(sPE, sPE.n)], lambda e: e.tensor_copy(slope_bc, pC[:, 0:1]), inc=sDVE)
    A("vector", [(sS, sS.n)], lambda e: e.memset(epsA[:, 3:4], 0.0), inc=sDVE)
    pl.flush("setup")

    if upto == "setup":
        A("vector", [], lambda e: e.tensor_copy(outst[:, 0:1], slope_bc), inc=sDVE)
        A("vector", [], lambda e: e.tensor_copy(outst[:, 1:9], epsA), inc=sDVE)
        A("vector", [], lambda e: e.tensor_copy(outst[:, 9:137], maskA), inc=sDVE)
        A("sync", [(sDVE, sDVE.n)], lambda e: e.dma_start(dbg_d[:, 0:137], outst[:, 0:137]),
          inc=sO, k=16)
        pl.flush("dbgsetup")

    for s in range(NSEQ if upto != "setup" else 0):
        A("sync", [], lambda e: e.dma_start(Gfull, rampA_d), inc=sS, k=16)
        A("scalar", [(sS, sS.n), (sDVE, sDVE.n)],
          lambda e: e.activation(Gfull, Gfull, AF.Copy, bias=0.0, scale=slope_bc[:, 0:1]),
          inc=sACT)
        A("gpsimd", [],
          lambda e: e.dma_start(wslot.rearrange("p (c j) -> p c j", c=8),
                                wvT_d.rearrange("(c p) j -> p c j", p=128)),
          inc=dWS, k=16)
        wv_ready = dWS.n

        _tiles = int(os.environ.get("KDBG_TILES", NT))
        _nov = int(os.environ.get("KDBG_NOV", "0"))
        _notr = int(os.environ.get("KDBG_NOTR", "0"))
        trslot = [pA[:, 1024:1152], pA[:, 1536:1664]]
        tr_done = {0: 0, 1: 0}
        vv_free = {0: 0, 1: 0}
        for st in range(_tiles):
            w = [(sPE, tr_done[st % 2])] if st >= 2 else []
            A("gpsimd", w,
              lambda e, s=s, st=st: e.dma_start(xres[st % 2], x_d[s, st * 128:(st + 1) * 128, :]),
              inc=dX[st % 2], k=16)
            xw = dX[st % 2].n
            for c in range(0 if _notr else 8):
                wts = [(sCP, max(0, sCP.n - 1))]
                if c == 0:
                    wts.append((dX[st % 2], xw))
                A("tensor", wts,
                  lambda e, st=st, c=c: e.transpose(
                      trslot[c % 2], xres[st % 2][:, c * 128:(c + 1) * 128], identf),
                  inc=sPE)
                A("vector", [(sPE, sPE.n)],
                  lambda e, st=st, c=c: e.tensor_copy(
                      XT[c][:, st * 128:(st + 1) * 128], trslot[c % 2]),
                  inc=sCP)
            tr_done[st % 2] = sPE.n
            vbank = pB if st % 2 == 0 else pA[:, 0:1024]
            for c in range(0 if _nov else 8):
                wts = []
                if c == 0:
                    wts = [(sCP, sCP.n), (dWS, wv_ready)]
                    if st >= 2:
                        wts.append((sACT, vv_free[st % 2]))
                for half in range(2):
                    A("tensor", wts if half == 0 else [],
                      lambda e, c=c, half=half, st=st, vbank=vbank: e.matmul(
                          vbank[:, half * 512:(half + 1) * 512],
                          XT[c][:, st * 128:(st + 1) * 128],
                          wslot[:, c * D + half * 512: c * D + (half + 1) * 512],
                          start=(c == 0), stop=(c == 7)),
                      inc=(sPE if (c == 7 and half == 1) else None))
            if not _nov:
                A("scalar", [(sPE, sPE.n)],
                  lambda e, st=st, vbank=vbank: e.activation(Vsb[:, st * D:(st + 1) * D], vbank, AF.Copy),
                  inc=sACT)
                vv_free[st % 2] = sACT.n
        pl.flush(f"A01_s{s}")

        if upto == "A01" and s == 0:
            A("vector", [], lambda e: e.tensor_copy(outst[:, 0:1024], Vsb[:, 0:1024]), inc=sDVE)
            A("sync", [(sDVE, sDVE.n)], lambda e: e.dma_start(dbg_d[:, 0:1024], outst[:, 0:1024]),
              inc=sO, k=16)
            A("vector", [], lambda e: e.tensor_copy(outst[:, 1024:2048], XT[0][:, 0:1024]), inc=sDVE)
            A("sync", [(sDVE, sDVE.n)], lambda e: e.dma_start(dbg_d[:, 1024:2048], outst[:, 1024:2048]),
              inc=sO, k=16)
            A("sync", [(sACT, sACT.n)], lambda e: e.dma_start(dbg_d[:, 2048:4096], Gfull), inc=sO, k=16)
            pl.flush("dbgA01")
            break

        A("gpsimd", [],
          lambda e: e.dma_start(wslot.rearrange("p (c j) -> p c j", c=8),
                                wqk_d.rearrange("(c p) j -> p c j", p=128)),
          inc=dWS, k=16)
        wq_ready = dWS.n

        qpt_done = 0
        qp_free = {0: 0, 1: 0}
        sc_free = {0: 0, 1: 0}
        xr_free = {0: 0, 1: 0}
        yv_free = 0
        _qis = int(os.environ.get("KDBG_QIS", NT))
        for qi in range(_qis):
            kend = (qi + 1) * 128
            nkc = (kend + 511) // 512
            buf = qi % 2
            A("sync", [(sDVE, xr_free[qi % 2])] if qi >= 2 else [],
              lambda e, s=s, qi=qi: e.dma_start(xres[qi % 2], x_d[s, qi * 128:(qi + 1) * 128, :]),
              inc=dX[qi % 2], k=16)
            xres_rdy = dX[qi % 2].n

            if qi % 2 == 0:
                for m in range(8):
                    bank = a(0) if m % 2 == 0 else a(1)
                    for c in range(8):
                        wts = []
                        if c == 0:
                            wts = [(dWS, wq_ready), (sACT, qp_free[m % 2])]
                        A("tensor", wts,
                          lambda e, m=m, c=c, qi=qi, bank=bank: e.matmul(
                              bank[:, 0:256],
                              wslot[:, c * D + m * 128: c * D + (m + 1) * 128],
                              XT[c][:, qi * 128: qi * 128 + 256],
                              start=(c == 0), stop=(c == 7)),
                          inc=(sPE if c == 7 else None))
                    A("scalar", [(sPE, sPE.n)],
                      lambda e, m=m, bank=bank: e.activation(
                          qpt[m], bank[:, 0:256], AF.Copy, bias=0.0, scale=1.0 / 32.0),
                      inc=sACT)
                    qp_free[m % 2] = sACT.n
                qpt_done = sACT.n

            for kc in range(nkc):
                w = min(512, kend - kc * 512)
                bank = a(2) if kc % 2 == 0 else a(3)
                for c in range(8):
                    wts = []
                    if c == 0:
                        wts = [(sACT, qpt_done), (sDVE, sc_free[kc % 2])]
                    A("tensor", wts,
                      lambda e, c=c, kc=kc, qi=qi, w=w, bank=bank: e.matmul(
                          bank[:, 0:w],
                          qpt[c][:, (qi % 2) * 128:(qi % 2) * 128 + 128],
                          XT[c][:, kc * 512: kc * 512 + w],
                          start=(c == 0), stop=(c == 7)),
                      inc=(sPE if c == 7 else None))
                off = kc * 512 - qi * 128 + C0
                A("vector", [(sPE, sPE.n)],
                  lambda e, kc=kc, w=w, off=off, buf=buf, bank=bank: e.tensor_add(
                      Sstrip[buf][:, kc * 512: kc * 512 + w], bank[:, 0:w],
                      Gfull[:, off: off + w]),
                  inc=sDVE)
                sc_free[kc % 2] = sDVE.n
                if kc == nkc - 1:
                    A("vector", [],
                      lambda e, qi=qi, buf=buf: e.tensor_add(
                          Sstrip[buf][:, qi * 128: qi * 128 + 128],
                          Sstrip[buf][:, qi * 128: qi * 128 + 128], maskA),
                      inc=sDVE)
            A("vector", [(sDVE, sDVE.n)],
              lambda e, buf=buf, kend=kend: e.reduce_max(
                  stat[:, 4:5], Sstrip[buf][:, 0:kend], axis=AX.X, negate=True),
              inc=sDVE)
            A("scalar", [(sDVE, sDVE.n)],
              lambda e, buf=buf, kend=kend: e.activation(
                  Pstrip[buf][:, 0:kend], Sstrip[buf][:, 0:kend], AF.Exp,
                  bias=stat[:, 4:5], accum_out=stat[:, 5:6]),
              inc=sACT)
            exp_done = sACT.n
            A("vector", [(sACT, exp_done)],
              lambda e: e.reciprocal(stat[:, 6:7], stat[:, 5:6]), inc=sDVE)

            ntk = qi + 1
            cps = []

            def emit_tr(kt, buf=buf):
                A("tensor", [(sACT, exp_done), (sCP, sCP.n)],
                  lambda e, kt=kt, buf=buf: e.transpose(
                      pT[:, (kt % 4) * 128:(kt % 4 + 1) * 128],
                      Pstrip[buf][:, kt * 128:(kt + 1) * 128], identbf),
                  inc=sPE)
                A("vector", [(sPE, sPE.n)],
                  lambda e, kt=kt: e.tensor_copy(
                      PTring[:, (kt % 4) * 128:(kt % 4 + 1) * 128],
                      pT[:, (kt % 4) * 128:(kt % 4 + 1) * 128]),
                  inc=sCP)
                cps.append(sCP.n)

            def emit_av(kt, qi=qi):
                for half in range(2):
                    wts = []
                    if half == 0:
                        wts = [(sCP, cps[kt])]
                        if kt == 0:
                            wts.append((sDVE, yv_free))
                    A("tensor", wts,
                      lambda e, kt=kt, half=half, qi=qi: e.matmul(
                          pB[:, half * 512:(half + 1) * 512],
                          PTring[:, (kt % 4) * 128:(kt % 4 + 1) * 128],
                          Vsb[:, kt * D + half * 512: kt * D + (half + 1) * 512],
                          start=(kt == 0), stop=(kt == qi)),
                      inc=(sPE if (kt == qi and half == 1) else None))

            for kt in range(ntk):
                emit_tr(kt)
                if kt >= 1:
                    emit_av(kt - 1)
            emit_av(ntk - 1)
            av_done = sPE.n

            A("vector", [(sPE, av_done), (dX[qi % 2], xres_rdy)],
              lambda e, qi=qi: e.scalar_tensor_tensor(
                  xres[qi % 2], pB, stat[:, 6:7], xres[qi % 2], op0=ALU.mult, op1=ALU.add),
              inc=sDVE)
            yv_free = sDVE.n
            ydone = sDVE.n
            A("scalar", [(sDVE, ydone)],
              lambda e, qi=qi: e.activation(sqscrA, xres[qi % 2], AF.Square, accum_out=stat[:, 8:9]),
              inc=sACT)
            A("vector", [],
              lambda e, qi=qi: e.reduce_sum(stat[:, 7:8], xres[qi % 2], axis=AX.X), inc=sDVE)
            A("vector", [(sDVE, sDVE.n)],
              lambda e: e.tensor_scalar(stat[:, 9:10], stat[:, 7:8], 1.0 / D, None, op0=ALU.mult),
              inc=sDVE)
            A("vector", [(sACT, sACT.n), (sDVE, sDVE.n)],
              lambda e: e.tensor_scalar(stat[:, 10:11], stat[:, 8:9], 1.0 / D, None, op0=ALU.mult),
              inc=sDVE)
            A("vector", [(sDVE, sDVE.n)],
              lambda e: e.tensor_mul(stat[:, 11:12], stat[:, 9:10], stat[:, 9:10]),
              inc=sDVE)
            A("vector", [(sDVE, sDVE.n)],
              lambda e: e.tensor_sub(stat[:, 12:13], stat[:, 10:11], stat[:, 11:12]),
              inc=sDVE)
            A("scalar", [(sDVE, sDVE.n)],
              lambda e: e.activation(stat[:, 13:14], stat[:, 12:13], AF.Ln, bias=epsA[:, 0:1]))
            A("scalar", [],
              lambda e: e.activation(stat[:, 14:15], stat[:, 13:14], AF.Exp, bias=0.0, scale=-0.5),
              inc=sACT)
            A("vector", [(sACT, sACT.n), (sDVE, sDVE.n)],
              lambda e, qi=qi: e.tensor_scalar(
                  X1R[:, qi * D:(qi + 1) * D], xres[qi % 2],
                  stat[:, 9:10], stat[:, 14:15], op0=ALU.subtract, op1=ALU.mult),
              inc=sDVE)
            xr_free[qi % 2] = sDVE.n
        pl.flush(f"A2_s{s}")

        if upto == "A2d" and s == 0:
            A("sync", [(sDVE, sDVE.n), (sACT, sACT.n)],
              lambda e: e.dma_start(dbg_d[:, 0:2048], Sstrip[0]), inc=sO, k=16)
            A("vector", [(sO, sO.n)], lambda e: e.tensor_copy(Sstrip[1], Pstrip[0]), inc=sDVE)
            A("vector", [], lambda e: e.tensor_copy(Sstrip[1][:, 0:64], stat), inc=sDVE)
            A("sync", [(sDVE, sDVE.n)], lambda e: e.dma_start(dbg_d[:, 2048:4096], Sstrip[1]), inc=sO, k=16)
            pl.flush("dbgA2")
            break

        if upto == "X1":
            for st in range(NT):
                A("sync", [(sDVE, sDVE.n)],
                  lambda e, s=s, st=st: e.dma_start(
                      y_d[s, st * 128:(st + 1) * 128, :], X1R[:, st * D:(st + 1) * D]),
                  inc=sO, k=16)
            A("vector", [(sO, sO.n)], lambda e: e.memset(epsA[:, 4:5], 0.0), inc=sDVE)
            pl.flush(f"dbgX1_s{s}")
            continue

        A("sync", [], lambda e: e.dma_start(n1row, n1_d), inc=sS, k=16)
        n1_rdy = sS.n
        A("tensor", [(sS, n1_rdy)],
          lambda e: e.matmul(pC, onescol[0:1, 0:128], n1row[0:1, 0:512], start=True, stop=True))
        A("tensor", [],
          lambda e: e.matmul(a(0), onescol[0:1, 0:128], n1row[0:1, 512:1024], start=True, stop=True),
          inc=sPE)
        A("vector", [(sPE, sPE.n)], lambda e: e.tensor_copy(n1rep[:, 0:512], pC), inc=sDVE)
        A("vector", [], lambda e: e.tensor_copy(n1rep[:, 512:1024], a(0)), inc=sDVE)

        for st in range(NT):
            A("scalar", [(sDVE, sDVE.n)],
              lambda e, st=st: e.activation(
                  gbf, X1R[:, st * D:(st + 1) * D], AF.Square, accum_out=stat[:, 16:17]),
              inc=sACT)
            A("scalar", [],
              lambda e: e.activation(stat[:, 17:18], stat[:, 16:17], AF.Ln,
                                     bias=epsA[:, 1:2], scale=1.0 / D))
            A("scalar", [],
              lambda e: e.activation(stat[:, 18:19], stat[:, 17:18], AF.Exp, bias=0.0, scale=-0.5),
              inc=sACT)
            A("vector", [(sACT, sACT.n)],
              lambda e, st=st: e.tensor_scalar(
                  bscr, X1R[:, st * D:(st + 1) * D], stat[:, 18:19], None, op0=ALU.mult),
              inc=sDVE)
            A("vector", [], lambda e: e.tensor_mul(bscr, bscr, n1rep), inc=sDVE)
            h1done = sDVE.n
            trslot = [pA[:, 1024:1152], pA[:, 1536:1664]]
            for c in range(8):
                wts = [(sCP, max(0, sCP.n - 1))]
                if c == 0:
                    wts.append((sDVE, h1done))
                A("tensor", wts,
                  lambda e, c=c, trslot=trslot: e.transpose(
                      trslot[c % 2], bscr[:, c * 128:(c + 1) * 128], identf),
                  inc=sPE)
                A("vector", [(sPE, sPE.n)],
                  lambda e, st=st, c=c, trslot=trslot: e.tensor_copy(
                      h1T[c][:, st * 128:(st + 1) * 128], trslot[c % 2]),
                  inc=sCP)
        pl.flush(f"B0_s{s}")

        A("gpsimd", [],
          lambda e: e.dma_start(vwslot.rearrange("p (c j) -> p c j", c=8),
                                qkvT_d[:, 2 * D:3 * D].rearrange("(c p) j -> p c j", p=128)),
          inc=dWS, k=16)
        vw_rdy = dWS.n
        b1_free = {0: 0, 1: 0}
        for st in range(NT):
            vbank = pB if st % 2 == 0 else pA[:, 0:1024]
            for c in range(8):
                wts = []
                if c == 0:
                    wts = [(dWS, vw_rdy), (sCP, sCP.n)]
                    if st >= 2:
                        wts.append((sDVE, b1_free[st % 2]))
                for half in range(2):
                    A("tensor", wts if half == 0 else [],
                      lambda e, c=c, half=half, st=st, vbank=vbank: e.matmul(
                          vbank[:, half * 512:(half + 1) * 512],
                          h1T[c][:, st * 128:(st + 1) * 128],
                          vwslot[:, c * D + half * 512: c * D + (half + 1) * 512],
                          start=(c == 0), stop=(c == 7)),
                      inc=(sPE if (c == 7 and half == 1) else None))
            A("vector", [(sPE, sPE.n)],
              lambda e, st=st, vbank=vbank: e.tensor_copy(
                  Vplus[:, st * G65:(st + 1) * G65].rearrange("p (g w) -> p g w", w=65)[:, :, 0:64],
                  vbank.rearrange("p (g w) -> p g w", w=64)),
              inc=sDVE)
            A("vector", [],
              lambda e, st=st: e.memset(
                  Vplus[:, st * G65:(st + 1) * G65].rearrange("p (g w) -> p g w", w=65)[:, :, 64:65],
                  1.0),
              inc=sDVE)
            b1_free[st % 2] = sDVE.n
        pl.flush(f"B1_s{s}")

        ebuf_ready = [sPE.n, sPE.n, sPE.n]
        stb_free = {0: 0, 1: 0}
        av_free = {0: 0, 1: 0}
        rsb_free = 0
        todd_free = {0: 0, 1: 0}
        for p in range(NP):
            pb = p % 2
            for c in range(8):
                wts = [(sACT, sACT.n)] if (c == 0 and p >= 2) else []
                A("gpsimd", wts,
                  lambda e, c=c, p=p, pb=pb: e.dma_start(
                      qkvw[pb][:, c * 256: c * 256 + 128],
                      qkvT_d[c * 128:(c + 1) * 128, p * 128:(p + 1) * 128]),
                  inc=dQW[pb], k=16)
                A("gpsimd", [],
                  lambda e, c=c, p=p, pb=pb: e.dma_start(
                      qkvw[pb][:, c * 256 + 128: c * 256 + 256],
                      qkvT_d[c * 128:(c + 1) * 128, D + p * 128: D + (p + 1) * 128]),
                  inc=dQW[pb], k=16)
            wrdy = dQW[pb].n
            for tgt in range(2):
                for qq in range(4):
                    bank = a(0) if (tgt * 4 + qq) % 2 == 0 else a(1)
                    for c in range(8):
                        wts = [(dQW[pb], wrdy), (sACT, sACT.n), (sDVE, sDVE.n)] if c == 0 else []
                        A("tensor", wts,
                          lambda e, c=c, tgt=tgt, qq=qq, bank=bank, pb=pb: e.matmul(
                              bank,
                              qkvw[pb][:, c * 256 + tgt * 128: c * 256 + tgt * 128 + 128],
                              h1T[c][:, qq * 512:(qq + 1) * 512],
                              start=(c == 0), stop=(c == 7)),
                          inc=(sPE if c == 7 else None))
                    A("scalar", [(sPE, sPE.n), (sS, sS.n)],
                      lambda e, tgt=tgt, qq=qq, bank=bank: e.activation(
                          qkpT[:, tgt * S + qq * 512: tgt * S + (qq + 1) * 512],
                          bank, AF.Copy, bias=0.0, scale=(0.125 if tgt == 0 else 1.0)),
                      inc=sACT)
            qk_done = sACT.n
            tm_free = {0: 0, 1: 0}
            for qt in range(NT):
                kend2 = (qt + 1) * 128
                nkc2 = (kend2 + 511) // 512
                for kc in range(nkc2):
                    w2 = min(512, kend2 - kc * 512)
                    base = 0 if kc % 2 == 0 else 1024
                    bnk = [pA[:, base:base + 512], pA[:, base + 512:base + 1024]]
                    for hh in range(2):
                        wts = []
                        if hh == 0:
                            wts = [(sACT, qk_done), (sDVE, tm_free[kc % 2])]
                        A("tensor", wts,
                          lambda e, hh=hh, qt=qt, kc=kc, w2=w2, bnk=bnk: e.matmul(
                              bnk[hh][:, 0:w2],
                              qkpT[hh * 64:(hh + 1) * 64, qt * 128:(qt + 1) * 128],
                              qkpT[hh * 64:(hh + 1) * 64, S + kc * 512: S + kc * 512 + w2],
                              start=True, stop=True, tile_position=(hh * 64, 0)),
                          inc=(sPE if hh == 1 else None))
                    mm2 = sPE.n
                    if kc == nkc2 - 1:
                        doff = qt * 128 - kc * 512
                        for hh in range(2):
                            A("vector", [(sPE, mm2)] if hh == 0 else [],
                              lambda e, hh=hh, doff=doff, bnk=bnk: e.tensor_add(
                                  bnk[hh][:, doff:doff + 128],
                                  bnk[hh][:, doff:doff + 128], maskA),
                              inc=sDVE)
                    A("vector", [(sPE, mm2), (sDVE, sDVE.n)],
                      lambda e, w2=w2, base=base: e.reduce_max(
                          stat[:, 32:34],
                          pA[:, base:base + 1024].rearrange("p (h w) -> p h w", h=2)[:, :, 0:w2],
                          axis=AX.X),
                      inc=sDVE)
                    tm_free[kc % 2] = sDVE.n
                    if kc == 0:
                        A("vector", [(sDVE, sDVE.n)],
                          lambda e: e.tensor_copy(stat[:, 34:36], stat[:, 32:34]), inc=sDVE)
                    else:
                        A("vector", [(sDVE, sDVE.n)],
                          lambda e: e.tensor_max(stat[:, 34:36], stat[:, 34:36], stat[:, 32:34]),
                          inc=sDVE)
                for hh in range(2):
                    mc = hh * 16 + qt
                    A("vector", [(sDVE, sDVE.n)],
                      lambda e, mc=mc, hh=hh: e.tensor_scalar(
                          mnegbf[:, mc: mc + 1],
                          stat[:, 34 + hh:35 + hh], -1.0, None, op0=ALU.mult),
                      inc=sDVE)
            A("tensor", [(sDVE, sDVE.n), (sS, sS.n)],
              lambda e: e.transpose(pT[0:32, 0:128], mnegbf[:, 0:32], identbf),
              inc=sPE)
            A("vector", [(sPE, sPE.n)],
              lambda e: e.tensor_copy(negrow, pT[0:32, 0:128]), inc=sDVE)
            A("sync", [(sDVE, sDVE.n)],
              lambda e: e.dma_start(negMrowA[0:1, 0:S], negrow[0:16, :]),
              inc=sS, k=16)
            A("sync", [],
              lambda e: e.dma_start(negMrowA[64:65, 0:S], negrow[16:32, :]),
              inc=sS, k=16)
            ops_rdy = sS.n

            for qc in range(8):
                ntk = 2 * qc + 2
                avb = pB[:, 0:512] if qc % 2 == 0 else pB[:, 512:1024]

                def emit_sc(kt, qc=qc):
                    par = kt % 2
                    half = (kt == 2 * qc + 1)
                    qlo = 128 if half else 0
                    sb = [a(0), a(1)] if par == 0 else [a(2), a(3)]
                    cb = 0
                    for hh in range(2):
                        wts = [(sS, ops_rdy), (sACT, stb_free[par])] if hh == 0 else []
                        A("tensor", wts,
                          lambda e, hh=hh, kt=kt, qc=qc, cb=cb, qlo=qlo, sb=sb: e.matmul(
                              sb[hh][:, cb + qlo:cb + 256],
                              qkpT[hh * 64:(hh + 1) * 64, S + kt * 128:S + (kt + 1) * 128],
                              qkpT[hh * 64:(hh + 1) * 64, qc * 256 + qlo:(qc + 1) * 256],
                              start=True, stop=False,
                              tile_position=(hh * 64, 0)))
                    for hh in range(2):
                        A("tensor", [],
                          lambda e, hh=hh, qc=qc, cb=cb, qlo=qlo, sb=sb: e.matmul(
                              sb[hh][:, cb + qlo:cb + 256],
                              onescolb[hh * 64:hh * 64 + 1, 0:128],
                              negMrowA[hh * 64:hh * 64 + 1, qc * 256 + qlo:(qc + 1) * 256],
                              start=False, stop=True,
                              tile_position=(hh * 64, 0)),
                          inc=(sPE if hh == 1 else None))
                    sc_done = sPE.n
                    nmask = 0
                    if kt == 2 * qc:
                        for hh in range(2):
                            A("vector", [(sPE, sc_done)] if hh == 0 else [],
                              lambda e, hh=hh, cb=cb, sb=sb: e.tensor_add(
                                  sb[hh][:, cb: cb + 128],
                                  sb[hh][:, cb: cb + 128], maskB),
                              inc=sDVE)
                        nmask = 2
                    elif half:
                        for hh in range(2):
                            A("vector", [(sPE, sc_done)] if hh == 0 else [],
                              lambda e, hh=hh, cb=cb, sb=sb: e.tensor_add(
                                  sb[hh][:, cb + 128: cb + 256],
                                  sb[hh][:, cb + 128: cb + 256], maskB),
                              inc=sDVE)
                        nmask = 2
                    msk_done = sDVE.n
                    ewts = [(sPE, sc_done), (sPE, ebuf_ready[kt % 3])]
                    if nmask:
                        ewts.append((sDVE, msk_done))
                    for hh in range(2):
                        A("scalar", ewts if hh == 0 else [],
                          lambda e, hh=hh, kt=kt, cb=cb, qlo=qlo, sb=sb: e.activation(
                              Ebuf2[kt % 3][:, hh * 256 + qlo:(hh + 1) * 256],
                              sb[hh][:, cb + qlo:cb + 256], AF.Exp),
                          inc=(sACT if hh == 1 else None))
                    return sACT.n

                def emit_avb(kt, qc=qc, exp_n=0, ntk=ntk, avb=avb):
                    half = (kt == 2 * qc + 1)
                    qlo = 128 if half else 0
                    for hh in range(2):
                        wts = []
                        if hh == 0:
                            wts = [(sACT, exp_n)]
                            if kt == 0:
                                wts.append((sDVE, av_free[qc % 2]))
                        A("tensor", wts,
                          lambda e, hh=hh, kt=kt, qc=qc, p=p, ntk=ntk, qlo=qlo, avb=avb: e.matmul(
                              avb[0:65, hh * 256 + qlo:(hh + 1) * 256],
                              Vplus[:, kt * G65 + (2 * p + hh) * 65: kt * G65 + (2 * p + hh) * 65 + 65],
                              Ebuf2[kt % 3][:, hh * 256 + qlo:(hh + 1) * 256],
                              start=(kt == 0 and hh == 0), stop=(kt == ntk - 1 and hh == 1)),
                          inc=(sPE if hh == 1 else None))
                    ebuf_ready[kt % 3] = sPE.n

                exps = []
                for kt in range(ntk):
                    if kt >= 2:
                        stb_free[kt % 2] = exps[kt - 2]
                    exps.append(emit_sc(kt))
                    if kt >= 1:
                        emit_avb(kt - 1, exp_n=exps[kt - 1])
                emit_avb(ntk - 1, exp_n=exps[ntk - 1])
                av_fin = sPE.n
                stb_free[(ntk - 2) % 2] = exps[ntk - 2]
                stb_free[(ntk - 1) % 2] = exps[ntk - 1]

                A("scalar", [(sPE, av_fin)],
                  lambda e, avb=avb: e.activation(
                      rsrow[0:1, 0:512], avb[64:65, 0:512],
                      AF.Ln, bias=eps1[0:1, 0:1]))
                A("scalar", [],
                  lambda e: e.activation(
                      rsrow[0:1, 0:512], rsrow[0:1, 0:512],
                      AF.Exp, bias=0.0, scale=-1.0),
                  inc=sACT)
                rs_done = sACT.n
                A("tensor", [(sACT, rs_done), (sDVE, rsb_free)],
                  lambda e: e.matmul(pC[0:64, 0:512], onescol[0:1, 0:64], rsrow,
                                     start=True, stop=True),
                  inc=sPE)
                bc_done = sPE.n
                A("vector", [(sPE, bc_done)],
                  lambda e: e.tensor_copy(rsb, pC[0:64, 0:512]), inc=sDVE)
                rsb_free = sDVE.n
                A("vector", [],
                  lambda e, p=p, qc=qc, avb=avb: e.tensor_mul(
                      attnOT[p][0:64, qc * 256:(qc + 1) * 256],
                      avb[0:64, 0:256], rsb[0:64, 0:256]),
                  inc=sDVE)
                A("vector", [(dTD[qc % 2], todd_free[qc % 2])],
                  lambda e, qc=qc, avb=avb: e.tensor_mul(
                      todd[0:64, (qc % 2) * 256:(qc % 2) * 256 + 256],
                      avb[0:64, 256:512], rsb[0:64, 256:512]),
                  inc=sDVE)
                av_free[qc % 2] = sDVE.n
                A("sync", [(sDVE, sDVE.n)],
                  lambda e, p=p, qc=qc: e.dma_start(
                      attnOT[p][64:128, qc * 256:(qc + 1) * 256],
                      todd[0:64, (qc % 2) * 256:(qc % 2) * 256 + 256]),
                  inc=dTD[qc % 2], k=16)
                todd_free[qc % 2] = dTD[qc % 2].n
        pl.flush(f"B23_s{s}")

        if upto == "B23d" and s == 0:
            for i in range(2):
                A("vector", [(sO, sO.n)],
                  lambda e, i=i: e.tensor_copy(dbgscr, attnOT[0][:, i * 1024:(i + 1) * 1024]),
                  inc=sDVE)
                A("sync", [(sDVE, sDVE.n)],
                  lambda e, i=i: e.dma_start(dbg_d[:, i * 1024:(i + 1) * 1024], dbgscr), inc=sO, k=16)
            A("vector", [(sO, sO.n)], lambda e: e.tensor_copy(dbgscr[:, 0:32], mnegbf), inc=sDVE)
            A("sync", [(sDVE, sDVE.n)], lambda e: e.dma_start(dbg_d[:, 2048:2112], dbgscr[:, 0:64]),
              inc=sO, k=16)
            A("vector", [(sO, sO.n)], lambda e: e.tensor_copy(dbgscr[:, 0:256], h1T[0][:, 0:256]),
              inc=sDVE)
            A("sync", [(sDVE, sDVE.n)], lambda e: e.dma_start(dbg_d[:, 2304:2560], dbgscr[:, 0:256]),
              inc=sO, k=16)
            A("vector", [(sO, sO.n)], lambda e: e.tensor_copy(dbgscr[:, 0:1024], Vplus[:, 0:1024]),
              inc=sDVE)
            A("sync", [(sDVE, sDVE.n)], lambda e: e.dma_start(dbg_d[:, 2560:3584], dbgscr[:, 0:1024]),
              inc=sO, k=16)
            A("vector", [(sO, sO.n)], lambda e: e.tensor_copy(dbgscr[:, 0:1024], qkpT[:, 0:1024]),
              inc=sDVE)
            A("sync", [(sDVE, sDVE.n)], lambda e: e.dma_start(dbg_d[:, 3072:4096], dbgscr[:, 0:1024]),
              inc=sO, k=16)
            pl.flush("dbgB23")
            break

        A("gpsimd", [],
          lambda e: e.dma_start(owT.rearrange("p (c j) -> p c j", c=8),
                                owT_d.rearrange("(c p) j -> p c j", p=128)),
          inc=sG, k=16)
        ow_rdy = sG.n
        attn_fin = [(sS, sS.n), (sDVE, sDVE.n), (dTD[0], dTD[0].n), (dTD[1], dTD[1].n)]
        b4_free = {0: 0, 1: 0}
        for st in range(NT):
            bank = pB if st % 2 == 0 else pA[:, 0:1024]
            for c in range(8):
                wts = []
                if c == 0:
                    wts = [(sG, ow_rdy)] + attn_fin + [(sDVE, b4_free[st % 2])]
                for half in range(2):
                    A("tensor", wts if half == 0 else [],
                      lambda e, c=c, half=half, st=st, bank=bank: e.matmul(
                          bank[:, half * 512:(half + 1) * 512],
                          attnOT[c][:, st * 128:(st + 1) * 128],
                          owT[:, c * D + half * 512: c * D + (half + 1) * 512],
                          start=(c == 0), stop=(c == 7)),
                      inc=(sPE if (c == 7 and half == 1) else None))
            A("vector", [(sPE, sPE.n)],
              lambda e, st=st, bank=bank: e.tensor_add(
                  X2[:, st * D:(st + 1) * D], bank, X1R[:, st * D:(st + 1) * D]),
              inc=sDVE)
            A("vector", [],
              lambda e, st=st: e.tensor_add(
                  X1R[:, st * D:(st + 1) * D], X2[:, st * D:(st + 1) * D],
                  X1R[:, st * D:(st + 1) * D]),
              inc=sDVE)
            b4_free[st % 2] = sDVE.n
        pl.flush(f"B4_s{s}")

        if upto == "X2":
            for st in range(NT):
                A("sync", [(sDVE, sDVE.n)],
                  lambda e, s=s, st=st: e.dma_start(
                      y_d[s, st * 128:(st + 1) * 128, :], X2[:, st * D:(st + 1) * D]),
                  inc=sO, k=16)
            A("vector", [(sO, sO.n)], lambda e: e.memset(epsA[:, 4:5], 0.0), inc=sDVE)
            pl.flush(f"dbgX2_s{s}")
            continue

        A("sync", [], lambda e: e.dma_start(n2row, n2_d), inc=sS, k=16)
        n2_rdy = sS.n
        A("tensor", [(sS, n2_rdy)],
          lambda e: e.matmul(pC, onescol[0:1, 0:128], n2row[0:1, 0:512], start=True, stop=True))
        A("tensor", [],
          lambda e: e.matmul(a(0), onescol[0:1, 0:128], n2row[0:1, 512:1024], start=True, stop=True),
          inc=sPE)
        A("vector", [(sPE, sPE.n)], lambda e: e.tensor_copy(n2rep[:, 0:512], pC), inc=sDVE)
        A("vector", [], lambda e: e.tensor_copy(n2rep[:, 512:1024], a(0)), inc=sDVE)
        A("gpsimd", [],
          lambda e: e.dma_start(gateT_sb.rearrange("p (c j) -> p c j", c=8),
                                gateT_d.rearrange("(c p) j -> p c j", p=128)),
          inc=sG, k=16)
        gate_rdy = sG.n
        for st in range(NT):
            A("scalar", [(sDVE, sDVE.n)],
              lambda e, st=st: e.activation(
                  gbf, X2[:, st * D:(st + 1) * D], AF.Square, accum_out=stat[:, 20:21]),
              inc=sACT)
            A("scalar", [],
              lambda e: e.activation(stat[:, 21:22], stat[:, 20:21], AF.Ln,
                                     bias=epsA[:, 1:2], scale=1.0 / D))
            A("scalar", [],
              lambda e: e.activation(stat[:, 22:23], stat[:, 21:22], AF.Exp, bias=0.0, scale=-0.5),
              inc=sACT)
            A("vector", [(sACT, sACT.n)],
              lambda e, st=st: e.tensor_scalar(
                  bscr2, X2[:, st * D:(st + 1) * D], stat[:, 22:23], None, op0=ALU.mult),
              inc=sDVE)
            A("vector", [], lambda e: e.tensor_mul(bscr2, bscr2, n2rep), inc=sDVE)
            gdone = sDVE.n
            trslot = [pA[:, 1024:1152], pA[:, 1536:1664]]
            for c in range(8):
                wts = [(sCP, max(0, sCP.n - 1))]
                if c == 0:
                    wts.append((sDVE, gdone))
                A("tensor", wts,
                  lambda e, c=c, trslot=trslot: e.transpose(
                      trslot[c % 2], bscr2[:, c * 128:(c + 1) * 128], identf),
                  inc=sPE)
                A("vector", [(sPE, sPE.n)],
                  lambda e, st=st, c=c, trslot=trslot: e.tensor_copy(
                      gT[c][:, st * 128:(st + 1) * 128], trslot[c % 2]),
                  inc=sCP)
            gt_done = sCP.n
            for c in range(8):
                wts = [(sCP, gt_done), (sG, gate_rdy), (sDVE, sDVE.n)] if c == 0 else []
                A("tensor", wts,
                  lambda e, c=c, st=st: e.matmul(
                      pC[:, 0:NE], gT[c][:, st * 128:(st + 1) * 128],
                      gateT_sb[:, c * NE:(c + 1) * NE], start=(c == 0), stop=(c == 7)),
                  inc=(sPE if c == 7 else None))
            A("vector", [(sPE, sPE.n)],
              lambda e: e.reduce_max(stat[:, 24:25], pC[:, 0:NE], axis=AX.X, negate=True),
              inc=sDVE)
            A("scalar", [(sDVE, sDVE.n)],
              lambda e: e.activation(stat[:, 32:40], pC[:, 0:NE], AF.Exp,
                                     bias=stat[:, 24:25], accum_out=stat[:, 25:26]),
              inc=sACT)
            A("vector", [(sACT, sACT.n)],
              lambda e: e.reciprocal(stat[:, 26:27], stat[:, 25:26]), inc=sDVE)
            A("vector", [(sDVE, sDVE.n)],
              lambda e: e.tensor_scalar(stat[:, 32:40], stat[:, 32:40], stat[:, 26:27], None,
                                        op0=ALU.mult),
              inc=sDVE)
            A("vector", [(sDVE, sDVE.n)], lambda e: e.reduce_max(stat[:, 27:28], stat[:, 32:40], axis=AX.X),
              inc=sDVE)
            A("vector", [(sDVE, sDVE.n)],
              lambda e: e.tensor_scalar(stat[:, 40:48], stat[:, 32:40], stat[:, 27:28], None,
                                        op0=ALU.subtract),
              inc=sDVE)
            A("vector", [(sDVE, sDVE.n)],
              lambda e: e.tensor_scalar(stat[:, 40:48], stat[:, 40:48], -1e30, 1.0,
                                        op0=ALU.mult, op1=ALU.min),
              inc=sDVE)
            A("vector", [(sDVE, sDVE.n)],
              lambda e: e.tensor_scalar(stat[:, 40:48], stat[:, 40:48], 1.0, 1e30,
                                        op0=ALU.subtract, op1=ALU.mult),
              inc=sDVE)
            A("vector", [(sDVE, sDVE.n)], lambda e: e.tensor_add(stat[:, 40:48], stat[:, 32:40], stat[:, 40:48]),
              inc=sDVE)
            A("vector", [(sDVE, sDVE.n)], lambda e: e.reduce_max(stat[:, 28:29], stat[:, 40:48], axis=AX.X),
              inc=sDVE)
            A("vector", [(sDVE, sDVE.n)], lambda e: e.tensor_add(stat[:, 29:30], stat[:, 27:28], stat[:, 28:29]),
              inc=sDVE)
            A("vector", [(sDVE, sDVE.n)],
              lambda e: e.tensor_scalar(stat[:, 30:31], stat[:, 29:30], 1e-9, None, op0=ALU.add),
              inc=sDVE)
            A("vector", [(sDVE, sDVE.n)], lambda e: e.reciprocal(stat[:, 31:32], stat[:, 30:31]), inc=sDVE)
            A("vector", [(sDVE, sDVE.n)],
              lambda e, st=st: e.tensor_mul(factor[:, st:st + 1], stat[:, 29:30], stat[:, 31:32]),
              inc=sDVE)
        pl.flush(f"B5_s{s}")

        up_free = {0: 0, 1: 0}
        dn_free = {0: 0, 1: 0}
        for ep in range(4):
            for c in range(8):
                wts = [(sACT, sACT.n), (sPE, sPE.n)] if (c == 0 and ep >= 2) else []
                A("gpsimd", wts,
                  lambda e, c=c, ep=ep: e.dma_start(
                      upw[ep % 2][:, c * KB:(c + 1) * KB],
                      upT_d[c * 128:(c + 1) * 128, ep * KB:(ep + 1) * KB]),
                  inc=dUW[ep % 2], k=16)
            for cc in range(8):
                A("gpsimd", [],
                  lambda e, cc=cc, ep=ep: e.dma_start(
                      dww[ep % 2][:, cc * KB:(cc + 1) * KB],
                      downT_d[ep * KB + cc * 128: ep * KB + (cc + 1) * 128, :]),
                  inc=dUW[ep % 2], k=16)
            w_rdy = dUW[ep % 2].n
            for sc in range(4):
                for et in range(8):
                    bank = a(0) if et % 2 == 0 else a(1)
                    for c in range(8):
                        wts = [(dUW[ep % 2], w_rdy), (sACT, up_free[et % 2])] if c == 0 else []
                        A("tensor", wts,
                          lambda e, c=c, et=et, sc=sc, ep=ep, bank=bank: e.matmul(
                              bank, upw[ep % 2][:, c * KB + et * 128: c * KB + (et + 1) * 128],
                              gT[c][:, sc * 512:(sc + 1) * 512],
                              start=(c == 0), stop=(c == 7)),
                          inc=(sPE if c == 7 else None))
                    A("scalar", [(sPE, sPE.n)],
                      lambda e, et=et, sc=sc, bank=bank: e.activation(
                          siluT[sc % 2][:, et * 512:(et + 1) * 512], bank, AF.Silu),
                      inc=sACT)
                    up_free[et % 2] = sACT.n
                silu_done = sACT.n
                for st2 in range(4):
                    st = sc * 4 + st2
                    bank = pB if st2 % 2 == 0 else pA[:, 1024:2048]
                    for et in range(8):
                        wts = []
                        if et == 0:
                            wts = [(sACT, silu_done), (sDVE, dn_free[st2 % 2]),
                                   (dUW[ep % 2], w_rdy)]
                        for half in range(2):
                            A("tensor", wts if half == 0 else [],
                              lambda e, et=et, half=half, st2=st2, sc=sc, ep=ep, bank=bank: e.matmul(
                                  bank[:, half * 512:(half + 1) * 512],
                                  siluT[sc % 2][:, et * 512 + st2 * 128: et * 512 + st2 * 128 + 128],
                                  dww[ep % 2][:, et * KB + half * 512: et * KB + (half + 1) * 512],
                                  start=(et == 0), stop=(et == 7)),
                              inc=(sPE if (et == 7 and half == 1) else None))
                    A("vector", [(sPE, sPE.n)],
                      lambda e, st=st, bank=bank: e.scalar_tensor_tensor(
                          X1R[:, st * D:(st + 1) * D], bank, factor[:, st:st + 1],
                          X1R[:, st * D:(st + 1) * D], op0=ALU.mult, op1=ALU.add),
                      inc=sDVE)
                    dn_free[st2 % 2] = sDVE.n
        pl.flush(f"B6_s{s}")

        ostore_free = {0: dX[0].n, 1: dX[1].n}
        for st in range(NT):
            A("scalar", [(sDVE, sDVE.n)],
              lambda e, st=st: e.activation(
                  lnscr, X1R[:, st * D:(st + 1) * D], AF.Square, accum_out=stat[:, 50:51]),
              inc=sACT)
            A("vector", [(sACT, sACT.n)],
              lambda e, st=st: e.reduce_sum(stat[:, 49:50], X1R[:, st * D:(st + 1) * D], axis=AX.X),
              inc=sDVE)
            A("vector", [(sDVE, sDVE.n)],
              lambda e: e.tensor_scalar(stat[:, 51:52], stat[:, 49:50], 1.0 / D, None, op0=ALU.mult),
              inc=sDVE)
            A("vector", [(sDVE, sDVE.n)],
              lambda e: e.tensor_scalar(stat[:, 52:53], stat[:, 50:51], 1.0 / D, None, op0=ALU.mult),
              inc=sDVE)
            A("vector", [(sDVE, sDVE.n)], lambda e: e.tensor_mul(stat[:, 53:54], stat[:, 51:52], stat[:, 51:52]),
              inc=sDVE)
            A("vector", [(sDVE, sDVE.n)], lambda e: e.tensor_sub(stat[:, 54:55], stat[:, 52:53], stat[:, 53:54]),
              inc=sDVE)
            A("scalar", [(sDVE, sDVE.n)],
              lambda e: e.activation(stat[:, 55:56], stat[:, 54:55], AF.Ln, bias=epsA[:, 0:1]))
            A("scalar", [],
              lambda e: e.activation(stat[:, 56:57], stat[:, 55:56], AF.Exp, bias=0.0, scale=-0.5),
              inc=sACT)
            A("vector", [(sACT, sACT.n), (sDVE, sDVE.n), (dX[st % 2], ostore_free[st % 2])],
              lambda e, st=st: e.tensor_scalar(
                  outst[:, (st % 2) * D:(st % 2 + 1) * D], X1R[:, st * D:(st + 1) * D],
                  stat[:, 51:52], stat[:, 56:57], op0=ALU.subtract, op1=ALU.mult),
              inc=sDVE)
            A("sync", [(sDVE, sDVE.n)],
              lambda e, s=s, st=st: e.dma_start(
                  y_d[s, st * 128:(st + 1) * 128, :], outst[:, (st % 2) * D:(st % 2 + 1) * D]),
              inc=dX[st % 2], k=16)
            ostore_free[st % 2] = dX[st % 2].n
        pl.flush(f"B7_s{s}")

    with nc.Block("fin") as blk:
        @blk.sync
        def _(e):
            e.wait_ge(sO.h, sO.n)
            e.wait_ge(dX[0].h, dX[0].n)
            e.wait_ge(dX[1].h, dX[1].n)

    nc.finalize()
    return nc


_NC_CACHE = {}


def _get_nc(upto):
    if upto not in _NC_CACHE:
        _NC_CACHE[upto] = build_program(upto)
    return _NC_CACHE[upto]


def make_consts():
    identbf = np.eye(128, dtype=ml_dtypes.bfloat16)
    identf = np.eye(128, dtype=np.float32)
    onescol = np.ones((1, 128), dtype=np.float32)
    onescolb = np.ones((128, 128), dtype=ml_dtypes.bfloat16)
    ones32 = np.ones((128, 32), dtype=ml_dtypes.bfloat16)
    i = np.arange(128).reshape(128, 1)
    c = np.arange(2048).reshape(1, 2048)
    rampA = (c - C0 - i).astype(np.float32)
    j = np.arange(128).reshape(1, 128)
    maskA = np.where(j > i, -1e30, 0.0).astype(np.float32)
    maskB = np.where(i > j, -1e30, 0.0).astype(np.float32)
    return identbf, identf, onescol, onescolb, ones32, rampA, maskA, maskB


def kernel(L0_attn_x, L0_attn_W_qk, L0_attn_W_v, L0_attn_alibi_slope,
           L0_ffn_n1_w, L0_ffn_qkv_w, L0_ffn_o_w, L0_ffn_n2_w,
           L0_ffn_gate_w, L0_ffn_up_w, L0_ffn_down_w, _upto="full", _ncores=8):
    X = np.ascontiguousarray(np.asarray(L0_attn_x, dtype=np.float32))
    wqk = np.ascontiguousarray(np.asarray(L0_attn_W_qk, dtype=np.float32))
    wvT = np.ascontiguousarray(np.asarray(L0_attn_W_v, dtype=np.float32).T)
    slope = np.asarray(L0_attn_alibi_slope, dtype=np.float32).reshape(1, 1)
    n1 = np.asarray(L0_ffn_n1_w, dtype=np.float32).reshape(1, D)
    n2 = np.asarray(L0_ffn_n2_w, dtype=np.float32).reshape(1, D)
    qkvT = np.ascontiguousarray(np.asarray(L0_ffn_qkv_w, dtype=np.float32).T)
    owT = np.ascontiguousarray(np.asarray(L0_ffn_o_w, dtype=np.float32).T)
    gateT = np.ascontiguousarray(np.asarray(L0_ffn_gate_w, dtype=np.float32).T)
    upT = np.ascontiguousarray(np.asarray(L0_ffn_up_w, dtype=np.float32).T)
    downT = np.ascontiguousarray(np.asarray(L0_ffn_down_w, dtype=np.float32).T)
    identbf, identf, onescol, onescolb, ones32, rampA, maskA, maskB = make_consts()

    nc = _get_nc(_upto)
    in_maps = []
    for i in range(_ncores):
        in_maps.append({
            "x": np.ascontiguousarray(X[2 * i:2 * i + 2]),
            "wqk": wqk, "wvT": wvT, "slope": slope, "n1": n1, "n2": n2,
            "qkvT": qkvT, "owT": owT, "gateT": gateT, "upT": upT, "downT": downT,
            "identbf": identbf, "identf": identf, "onescol": onescol,
            "onescolb": onescolb, "ones32": ones32,
            "rampA": rampA, "maskA": maskA, "maskB": maskB,
        })
    res = run_bass_kernel_spmd(nc, in_maps, core_ids=list(range(_ncores)))
    if _upto in ("A01", "setup", "A2d", "B23d"):
        return res.results[0]["dbg"]
    out = np.concatenate([r["y"] for r in res.results], axis=0)
    return out.reshape(2 * _ncores, S, D).astype(np.float32)


# version 13
# speedup vs baseline: 1.1374x; 1.1374x over previous
import os
import numpy as np
import ml_dtypes
import concourse.bass as bass
import concourse.mybir as mybir
from concourse.bass_utils import run_bass_kernel_spmd

F32 = mybir.dt.float32
BF16 = mybir.dt.bfloat16
AF = mybir.ActivationFunctionType
ALU = mybir.AluOpType
AX = mybir.AxisListType

S = 2048
D = 1024
NT = 16
H = 16
HD = 64
NP = 8
ED = 4096
NE = 8
NSEQ = 2
C0 = 1920
KB = 1024
G65 = H * 65


class Sem:
    def __init__(self, nc, name):
        self.cm = nc.semaphore(name)
        self.h = self.cm.__enter__()
        self.n = 0


class Plan:
    ENGS = ("sync", "gpsimd", "tensor", "scalar", "vector")

    def __init__(self, nc):
        self.nc = nc
        self.q = {e: [] for e in self.ENGS}

    def add(self, eng, waits, fn, inc=None, k=1):
        wcap = [(sm.h, v) for (sm, v) in waits]
        if inc is not None:
            inc.n += k
        ih = None if inc is None else inc.h

        def run(e, wcap=wcap, fn=fn, ih=ih, k=k):
            for (h, v) in wcap:
                if v > 0:
                    e.wait_ge(h, v)
            ins = fn(e)
            if ih is not None:
                ins.then_inc(ih, k)
        self.q[eng].append(run)

    def flush(self, name=None):
        q = self.q
        self.q = {e: [] for e in self.ENGS}
        with self.nc.Block(name) as blk:
            for ename in self.ENGS:
                fns = q[ename]
                if not fns:
                    continue

                def body(eng, fns=fns):
                    for f in fns:
                        f(eng)
                getattr(blk, ename)(body)


def build_program(upto="full"):
    nc = bass.Bass("TRN2", target_bir_lowering=False)

    x_d = nc.dram_tensor("x", [NSEQ, S, D], F32, kind="ExternalInput").ap()
    wqk_d = nc.dram_tensor("wqk", [D, D], F32, kind="ExternalInput").ap()
    wvT_d = nc.dram_tensor("wvT", [D, D], F32, kind="ExternalInput").ap()
    slope_d = nc.dram_tensor("slope", [1, 1], F32, kind="ExternalInput").ap()
    n1_d = nc.dram_tensor("n1", [1, D], F32, kind="ExternalInput").ap()
    n2_d = nc.dram_tensor("n2", [1, D], F32, kind="ExternalInput").ap()
    qkvT_d = nc.dram_tensor("qkvT", [D, 3 * D], F32, kind="ExternalInput").ap()
    owT_d = nc.dram_tensor("owT", [D, D], F32, kind="ExternalInput").ap()
    gateT_d = nc.dram_tensor("gateT", [D, NE], F32, kind="ExternalInput").ap()
    upT_d = nc.dram_tensor("upT", [D, ED], F32, kind="ExternalInput").ap()
    downT_d = nc.dram_tensor("downT", [ED, D], F32, kind="ExternalInput").ap()
    identbf_d = nc.dram_tensor("identbf", [128, 128], BF16, kind="ExternalInput").ap()
    identf_d = nc.dram_tensor("identf", [128, 128], F32, kind="ExternalInput").ap()
    onescol_d = nc.dram_tensor("onescol", [1, 128], F32, kind="ExternalInput").ap()
    onescolb_d = nc.dram_tensor("onescolb", [128, 128], BF16, kind="ExternalInput").ap()
    ones32_d = nc.dram_tensor("ones32", [128, 32], BF16, kind="ExternalInput").ap()
    rampA_d = nc.dram_tensor("rampA", [128, 2048], F32, kind="ExternalInput").ap()
    maskA_d = nc.dram_tensor("maskA", [128, 128], F32, kind="ExternalInput").ap()
    maskB_d = nc.dram_tensor("maskB", [128, 128], F32, kind="ExternalInput").ap()
    y_d = nc.dram_tensor("y", [NSEQ, S, D], F32, kind="ExternalOutput").ap()
    dbg_d = nc.dram_tensor("dbg", [128, 4096], F32, kind="ExternalOutput").ap()

    X1R = nc.alloc_sbuf_tensor("X1R", [128, NT * D], F32).ap()
    factor = nc.alloc_sbuf_tensor("factor", [128, NT], F32).ap()
    maskA = nc.alloc_sbuf_tensor("maskA_s", [128, 128], F32).ap()
    maskB = nc.alloc_sbuf_tensor("maskB_s", [128, 128], F32).ap()
    identbf = nc.alloc_sbuf_tensor("identbf_s", [128, 128], BF16).ap()
    identf = nc.alloc_sbuf_tensor("identf_s", [128, 128], F32).ap()
    onescol = nc.alloc_sbuf_tensor("onescol_s", [1, 128], F32).ap()
    onescolb = nc.alloc_sbuf_tensor("onescolb_s", [128, 128], BF16).ap()
    ones32 = nc.alloc_sbuf_tensor("ones32_s", [128, 32], BF16).ap()
    slope_sb = nc.alloc_sbuf_tensor("slope_sb", [1, 32], F32).ap()
    slope_bc = nc.alloc_sbuf_tensor("slope_bc", [128, 1], F32).ap()
    epsA = nc.alloc_sbuf_tensor("epsA", [128, 8], F32).ap()
    eps1 = nc.alloc_sbuf_tensor("eps1", [1, 8], F32).ap()
    stat = nc.alloc_sbuf_tensor("stat", [128, 64], F32).ap()
    mu1 = nc.alloc_sbuf_tensor("mu1", [1, 16], F32).ap()
    mu1b = nc.alloc_sbuf_tensor("mu1b", [8, 8], BF16).ap()
    nbr = nc.alloc_sbuf_tensor("nbr", [128, 8], F32).ap()
    nbrb = nc.alloc_sbuf_tensor("nbrb", [128, 8], BF16).ap()
    biasH = nc.alloc_sbuf_tensor("biasH", [128, 2], F32).ap()

    TOP = 229344
    ARENA = 138 * KB
    AB = TOP - ARENA

    def at(name, off, shape, dtype):
        esz = 4 if dtype == F32 else 2
        assert off + shape[1] * esz <= ARENA, (name, off, shape)
        return nc.alloc_sbuf_tensor_at(name, shape, dtype, offset=AB + off).ap()

    XT = [at(f"XT{c}", c * 4 * KB, [128, S], BF16) for c in range(8)]
    Vsb = at("Vsb", 32 * KB, [128, NT * D], BF16)
    wslot = at("wslot", 64 * KB, [128, 8 * D], BF16)
    Sstrip = [at(f"Sstrip{i}", 80 * KB + i * 8 * KB, [128, S], F32) for i in range(2)]
    PTring = at("PTring", 104 * KB, [128, 4 * 128], BF16)
    qpt = [at(f"qpt{m}", 105 * KB + m * 512, [128, 256], BF16) for m in range(8)]
    xres = [at(f"xres{i}", 109 * KB + i * 4 * KB, [128, D], F32) for i in range(2)]
    xbf = [at(f"xbf{i}", 117 * KB + i * 2 * KB, [128, D], BF16) for i in range(2)]
    Gfull = at("Gfull", 121 * KB, [128, 2048], F32)
    sqscrA = at("sqscrA", 129 * KB, [128, D], BF16)

    h1T = [at(f"h1T{c}", c * 4 * KB, [128, S], BF16) for c in range(8)]
    Vplus = at("Vplus", 32 * KB, [128, NT * G65], BF16)
    qkpT = [at(f"qkpT{i}", 65 * KB + i * 8 * KB, [128, 2 * S], BF16) for i in range(2)]
    negMrowA = at("negMrowA", 81 * KB, [65, S], BF16)
    qkvw = [at(f"qkvw{i}", 89 * KB + i * 4 * KB, [128, 2048], BF16) for i in range(2)]
    Ebuf2 = [at(f"Ebuf2_{i}", 97 * KB + i * KB, [128, 512], BF16) for i in range(3)]
    n1rep = at("n1rep", 100 * KB, [128, D], F32)
    bscr = at("bscr", 104 * KB, [128, D], F32)
    n1row = at("n1row", 108 * KB, [1, D], F32)
    vwslot = at("vwslot", 65 * KB, [128, 8 * D], BF16)
    rsrow = at("rsrow", 100 * KB, [1, 512], F32)
    rsb = at("rsb", 102 * KB, [64, 512], F32)
    todd = at("todd", 104 * KB, [64, 2 * 256], BF16)
    negrow = at("negrow", 105 * KB, [32, 128], BF16)
    mnegbf = at("mnegbf", 105 * KB + 512, [128, 32], F32)
    attnOT = [at(f"attnOT{c}", 106 * KB + c * 4 * KB, [128, S], BF16) for c in range(8)]
    dbgscr = at("dbgscr", 100 * KB, [128, 1024], F32)

    X2 = at("X2", 0, [128, NT * D], F32)
    owT = at("owT", 65 * KB, [128, 8 * D], BF16)
    n2rep = at("n2rep", 81 * KB, [128, D], F32)
    bscr2 = at("bscr2", 85 * KB, [128, D], F32)
    gbf = at("gbf", 89 * KB, [128, D], BF16)
    gateT_sb = at("gateT_sb", 91 * KB, [128, NE * 8], BF16)
    n2row = at("n2row", 92 * KB, [1, D], F32)
    gT = [at(f"gT{c}", 106 * KB + c * 4 * KB, [128, S], BF16) for c in range(8)]

    upw = [at(f"upw{i}", i * 16 * KB, [128, 8 * KB], BF16) for i in range(2)]
    dww = [at(f"dww{i}", 32 * KB + i * 16 * KB, [128, 8 * KB], BF16) for i in range(2)]
    siluT = [at(f"siluT{i}", 65 * KB + i * 8 * KB, [128, 8 * 512], BF16) for i in range(2)]
    lnscr = at("lnscr", 81 * KB, [128, D], BF16)
    outst = at("outst", 85 * KB, [128, 2 * D], F32)

    assert nc.sbuf_base <= AB, (nc.sbuf_base, AB)

    pA = nc.alloc_psum_tensor("pA", [128, 2048], F32).ap()
    pB = nc.alloc_psum_tensor("pB", [128, 1024], F32).ap()
    pC = nc.alloc_psum_tensor("pC", [128, 512], F32).ap()
    pT = nc.alloc_psum_tensor("pT", [128, 512], F32).ap()

    def a(i):
        return pA[:, i * 512:(i + 1) * 512]

    sG = Sem(nc, "sG")
    sS = Sem(nc, "sS")
    sO = Sem(nc, "sO")
    sPE = Sem(nc, "sPE")
    sACT = Sem(nc, "sACT")
    sDVE = Sem(nc, "sDVE")
    sCP = Sem(nc, "sCP")
    dX = [Sem(nc, "dX0"), Sem(nc, "dX1")]
    dQW = [Sem(nc, "dQW0"), Sem(nc, "dQW1")]
    dUW = [Sem(nc, "dUW0"), Sem(nc, "dUW1")]
    dTD = [Sem(nc, "dTD0"), Sem(nc, "dTD1")]
    dWS = Sem(nc, "dWS")

    pl = Plan(nc)
    A = pl.add

    A("sync", [], lambda e: e.dma_start(maskA, maskA_d), inc=sS, k=16)
    A("sync", [], lambda e: e.dma_start(maskB, maskB_d), inc=sS, k=16)
    A("sync", [], lambda e: e.dma_start(identbf, identbf_d), inc=sS, k=16)
    A("sync", [], lambda e: e.dma_start(identf, identf_d), inc=sS, k=16)
    A("sync", [], lambda e: e.dma_start(onescol, onescol_d), inc=sS, k=16)
    A("sync", [], lambda e: e.dma_start(onescolb, onescolb_d), inc=sS, k=16)
    A("sync", [], lambda e: e.dma_start(ones32, ones32_d), inc=sS, k=16)
    A("sync", [], lambda e: e.dma_start(slope_sb[0:1, 0:1], slope_d), inc=sS, k=16)
    A("vector", [], lambda e: e.memset(epsA[:, 0:1], 1e-5))
    A("vector", [], lambda e: e.memset(epsA[:, 1:2], 1e-6))
    A("vector", [], lambda e: e.memset(eps1[0:1, 0:1], 1e-38), inc=sDVE)
    A("tensor", [(sS, sS.n)],
      lambda e: e.matmul(pC[:, 0:1], onescol[0:1, 0:128], slope_sb[0:1, 0:1],
                         start=True, stop=True), inc=sPE)
    A("vector", [(sPE, sPE.n)], lambda e: e.tensor_copy(slope_bc, pC[:, 0:1]), inc=sDVE)
    A("vector", [(sS, sS.n)], lambda e: e.memset(epsA[:, 3:4], 0.0), inc=sDVE)
    pl.flush("setup")

    if upto == "setup":
        A("vector", [], lambda e: e.tensor_copy(outst[:, 0:1], slope_bc), inc=sDVE)
        A("vector", [], lambda e: e.tensor_copy(outst[:, 1:9], epsA), inc=sDVE)
        A("vector", [], lambda e: e.tensor_copy(outst[:, 9:137], maskA), inc=sDVE)
        A("sync", [(sDVE, sDVE.n)], lambda e: e.dma_start(dbg_d[:, 0:137], outst[:, 0:137]),
          inc=sO, k=16)
        pl.flush("dbgsetup")

    for s in range(NSEQ if upto != "setup" else 0):
        A("sync", [], lambda e: e.dma_start(Gfull, rampA_d), inc=sS, k=16)
        A("scalar", [(sS, sS.n), (sDVE, sDVE.n)],
          lambda e: e.activation(Gfull, Gfull, AF.Copy, bias=0.0, scale=slope_bc[:, 0:1]),
          inc=sACT)
        A("gpsimd", [],
          lambda e: e.dma_start(wslot.rearrange("p (c j) -> p c j", c=8),
                                wvT_d.rearrange("(c p) j -> p c j", p=128)),
          inc=dWS, k=16)
        wv_ready = dWS.n

        _tiles = int(os.environ.get("KDBG_TILES", NT))
        _nov = int(os.environ.get("KDBG_NOV", "0"))
        _notr = int(os.environ.get("KDBG_NOTR", "0"))
        trslot = [pA[:, 1024:1152], pA[:, 1536:1664]]
        tr_done = {0: 0, 1: 0}
        vv_free = {0: 0, 1: 0}
        for st in range(_tiles):
            w = [(sPE, tr_done[st % 2])] if st >= 2 else []
            A("gpsimd", w,
              lambda e, s=s, st=st: e.dma_start(xres[st % 2], x_d[s, st * 128:(st + 1) * 128, :]),
              inc=dX[st % 2], k=16)
            xw = dX[st % 2].n
            for c in range(0 if _notr else 8):
                wts = [(sCP, max(0, sCP.n - 1))]
                if c == 0:
                    wts.append((dX[st % 2], xw))
                A("tensor", wts,
                  lambda e, st=st, c=c: e.transpose(
                      trslot[c % 2], xres[st % 2][:, c * 128:(c + 1) * 128], identf),
                  inc=sPE)
                A("vector", [(sPE, sPE.n)],
                  lambda e, st=st, c=c: e.tensor_copy(
                      XT[c][:, st * 128:(st + 1) * 128], trslot[c % 2]),
                  inc=sCP)
            tr_done[st % 2] = sPE.n
            vbank = pB if st % 2 == 0 else pA[:, 0:1024]
            for c in range(0 if _nov else 8):
                wts = []
                if c == 0:
                    wts = [(sCP, sCP.n), (dWS, wv_ready)]
                    if st >= 2:
                        wts.append((sACT, vv_free[st % 2]))
                for half in range(2):
                    A("tensor", wts if half == 0 else [],
                      lambda e, c=c, half=half, st=st, vbank=vbank: e.matmul(
                          vbank[:, half * 512:(half + 1) * 512],
                          XT[c][:, st * 128:(st + 1) * 128],
                          wslot[:, c * D + half * 512: c * D + (half + 1) * 512],
                          start=(c == 0), stop=(c == 7)),
                      inc=(sPE if (c == 7 and half == 1) else None))
            if not _nov:
                A("scalar", [(sPE, sPE.n)],
                  lambda e, st=st, vbank=vbank: e.activation(Vsb[:, st * D:(st + 1) * D], vbank, AF.Copy),
                  inc=sACT)
                vv_free[st % 2] = sACT.n
        pl.flush(f"A01_s{s}")

        if upto == "A01" and s == 0:
            A("vector", [], lambda e: e.tensor_copy(outst[:, 0:1024], Vsb[:, 0:1024]), inc=sDVE)
            A("sync", [(sDVE, sDVE.n)], lambda e: e.dma_start(dbg_d[:, 0:1024], outst[:, 0:1024]),
              inc=sO, k=16)
            A("vector", [], lambda e: e.tensor_copy(outst[:, 1024:2048], XT[0][:, 0:1024]), inc=sDVE)
            A("sync", [(sDVE, sDVE.n)], lambda e: e.dma_start(dbg_d[:, 1024:2048], outst[:, 1024:2048]),
              inc=sO, k=16)
            A("sync", [(sACT, sACT.n)], lambda e: e.dma_start(dbg_d[:, 2048:4096], Gfull), inc=sO, k=16)
            pl.flush("dbgA01")
            break

        A("gpsimd", [],
          lambda e: e.dma_start(wslot.rearrange("p (c j) -> p c j", c=8),
                                wqk_d.rearrange("(c p) j -> p c j", p=128)),
          inc=dWS, k=16)
        wq_ready = dWS.n

        qpt_done = 0
        qp_free = {0: 0, 1: 0}
        sc_free = {0: 0, 1: 0}
        xr_free = {0: 0, 1: 0}
        yv_free = 0
        tr_fin = {0: 0, 1: 0}
        _qis = int(os.environ.get("KDBG_QIS", NT))
        for qi in range(_qis):
            kend = (qi + 1) * 128
            nkc = (kend + 511) // 512
            buf = qi % 2
            A("sync", [(sDVE, xr_free[qi % 2])] if qi >= 2 else [],
              lambda e, s=s, qi=qi: e.dma_start(xres[qi % 2], x_d[s, qi * 128:(qi + 1) * 128, :]),
              inc=dX[qi % 2], k=16)
            xres_rdy = dX[qi % 2].n

            if qi % 2 == 0:
                for m in range(8):
                    bank = a(0) if m % 2 == 0 else a(1)
                    for c in range(8):
                        wts = []
                        if c == 0:
                            wts = [(dWS, wq_ready), (sACT, qp_free[m % 2])]
                        A("tensor", wts,
                          lambda e, m=m, c=c, qi=qi, bank=bank: e.matmul(
                              bank[:, 0:256],
                              wslot[:, c * D + m * 128: c * D + (m + 1) * 128],
                              XT[c][:, qi * 128: qi * 128 + 256],
                              start=(c == 0), stop=(c == 7)),
                          inc=(sPE if c == 7 else None))
                    A("scalar", [(sPE, sPE.n)],
                      lambda e, m=m, bank=bank: e.activation(
                          qpt[m], bank[:, 0:256], AF.Copy, bias=0.0, scale=1.0 / 32.0),
                      inc=sACT)
                    qp_free[m % 2] = sACT.n
                qpt_done = sACT.n

            for kc in range(nkc):
                w = min(512, kend - kc * 512)
                bank = a(2) if kc % 2 == 0 else a(3)
                for c in range(8):
                    wts = []
                    if c == 0:
                        wts = [(sACT, qpt_done), (sDVE, sc_free[kc % 2])]
                    A("tensor", wts,
                      lambda e, c=c, kc=kc, qi=qi, w=w, bank=bank: e.matmul(
                          bank[:, 0:w],
                          qpt[c][:, (qi % 2) * 128:(qi % 2) * 128 + 128],
                          XT[c][:, kc * 512: kc * 512 + w],
                          start=(c == 0), stop=(c == 7)),
                      inc=(sPE if c == 7 else None))
                off = kc * 512 - qi * 128 + C0
                sw = [(sPE, sPE.n)]
                if kc == 0:
                    sw.append((sCP, tr_fin[qi % 2]))
                A("vector", sw,
                  lambda e, kc=kc, w=w, off=off, buf=buf, bank=bank: e.tensor_add(
                      Sstrip[buf][:, kc * 512: kc * 512 + w], bank[:, 0:w],
                      Gfull[:, off: off + w]),
                  inc=sDVE)
                sc_free[kc % 2] = sDVE.n
                if kc == nkc - 1:
                    A("vector", [],
                      lambda e, qi=qi, buf=buf: e.tensor_add(
                          Sstrip[buf][:, qi * 128: qi * 128 + 128],
                          Sstrip[buf][:, qi * 128: qi * 128 + 128], maskA),
                      inc=sDVE)
            A("vector", [(sDVE, sDVE.n)],
              lambda e, buf=buf, kend=kend: e.reduce_max(
                  stat[:, 4:5], Sstrip[buf][:, 0:kend], axis=AX.X, negate=True),
              inc=sDVE)
            A("scalar", [(sDVE, sDVE.n)],
              lambda e, buf=buf, kend=kend: e.activation(
                  Sstrip[buf][:, 0:kend], Sstrip[buf][:, 0:kend], AF.Exp,
                  bias=stat[:, 4:5], accum_out=stat[:, 5:6]),
              inc=sACT)
            exp_done = sACT.n
            A("vector", [(sACT, exp_done)],
              lambda e: e.reciprocal(stat[:, 6:7], stat[:, 5:6]), inc=sDVE)

            ntk = qi + 1
            cps = []

            def emit_tr(kt, buf=buf):
                A("tensor", [(sACT, exp_done), (sCP, sCP.n)],
                  lambda e, kt=kt, buf=buf: e.transpose(
                      pT[:, (kt % 4) * 128:(kt % 4 + 1) * 128],
                      Sstrip[buf][:, kt * 128:(kt + 1) * 128], identf),
                  inc=sPE)
                A("vector", [(sPE, sPE.n)],
                  lambda e, kt=kt: e.tensor_copy(
                      PTring[:, (kt % 4) * 128:(kt % 4 + 1) * 128],
                      pT[:, (kt % 4) * 128:(kt % 4 + 1) * 128]),
                  inc=sCP)
                cps.append(sCP.n)

            def emit_av(kt, qi=qi):
                for half in range(2):
                    wts = []
                    if half == 0:
                        wts = [(sCP, cps[kt])]
                        if kt == 0:
                            wts.append((sDVE, yv_free))
                    A("tensor", wts,
                      lambda e, kt=kt, half=half, qi=qi: e.matmul(
                          pB[:, half * 512:(half + 1) * 512],
                          PTring[:, (kt % 4) * 128:(kt % 4 + 1) * 128],
                          Vsb[:, kt * D + half * 512: kt * D + (half + 1) * 512],
                          start=(kt == 0), stop=(kt == qi)),
                      inc=(sPE if (kt == qi and half == 1) else None))

            for kt in range(ntk):
                emit_tr(kt)
                if kt >= 1:
                    emit_av(kt - 1)
            emit_av(ntk - 1)
            tr_fin[qi % 2] = cps[-1]
            av_done = sPE.n

            A("vector", [(sPE, av_done), (dX[qi % 2], xres_rdy)],
              lambda e, qi=qi: e.scalar_tensor_tensor(
                  xres[qi % 2], pB, stat[:, 6:7], xres[qi % 2], op0=ALU.mult, op1=ALU.add),
              inc=sDVE)
            yv_free = sDVE.n
            ydone = sDVE.n
            A("scalar", [(sDVE, ydone)],
              lambda e, qi=qi: e.activation(sqscrA, xres[qi % 2], AF.Square, accum_out=stat[:, 8:9]),
              inc=sACT)
            A("vector", [],
              lambda e, qi=qi: e.reduce_sum(stat[:, 7:8], xres[qi % 2], axis=AX.X), inc=sDVE)
            A("vector", [(sDVE, sDVE.n)],
              lambda e: e.tensor_scalar(stat[:, 9:10], stat[:, 7:8], 1.0 / D, None, op0=ALU.mult),
              inc=sDVE)
            A("vector", [(sACT, sACT.n), (sDVE, sDVE.n)],
              lambda e: e.tensor_scalar(stat[:, 10:11], stat[:, 8:9], 1.0 / D, None, op0=ALU.mult),
              inc=sDVE)
            A("vector", [(sDVE, sDVE.n)],
              lambda e: e.tensor_mul(stat[:, 11:12], stat[:, 9:10], stat[:, 9:10]),
              inc=sDVE)
            A("vector", [(sDVE, sDVE.n)],
              lambda e: e.tensor_sub(stat[:, 12:13], stat[:, 10:11], stat[:, 11:12]),
              inc=sDVE)
            A("scalar", [(sDVE, sDVE.n)],
              lambda e: e.activation(stat[:, 13:14], stat[:, 12:13], AF.Ln, bias=epsA[:, 0:1]))
            A("scalar", [],
              lambda e: e.activation(stat[:, 14:15], stat[:, 13:14], AF.Exp, bias=0.0, scale=-0.5),
              inc=sACT)
            A("vector", [(sACT, sACT.n), (sDVE, sDVE.n)],
              lambda e, qi=qi: e.tensor_scalar(
                  X1R[:, qi * D:(qi + 1) * D], xres[qi % 2],
                  stat[:, 9:10], stat[:, 14:15], op0=ALU.subtract, op1=ALU.mult),
              inc=sDVE)
            xr_free[qi % 2] = sDVE.n
        pl.flush(f"A2_s{s}")

        if upto == "A2d" and s == 0:
            A("sync", [(sDVE, sDVE.n), (sACT, sACT.n)],
              lambda e: e.dma_start(dbg_d[:, 0:2048], Sstrip[0]), inc=sO, k=16)
            A("vector", [(sO, sO.n)], lambda e: e.tensor_copy(Sstrip[1], Sstrip[0]), inc=sDVE)
            A("vector", [], lambda e: e.tensor_copy(Sstrip[1][:, 0:64], stat), inc=sDVE)
            A("sync", [(sDVE, sDVE.n)], lambda e: e.dma_start(dbg_d[:, 2048:4096], Sstrip[1]), inc=sO, k=16)
            pl.flush("dbgA2")
            break

        if upto == "X1":
            for st in range(NT):
                A("sync", [(sDVE, sDVE.n)],
                  lambda e, s=s, st=st: e.dma_start(
                      y_d[s, st * 128:(st + 1) * 128, :], X1R[:, st * D:(st + 1) * D]),
                  inc=sO, k=16)
            A("vector", [(sO, sO.n)], lambda e: e.memset(epsA[:, 4:5], 0.0), inc=sDVE)
            pl.flush(f"dbgX1_s{s}")
            continue

        A("sync", [], lambda e: e.dma_start(n1row, n1_d), inc=sS, k=16)
        n1_rdy = sS.n
        A("tensor", [(sS, n1_rdy)],
          lambda e: e.matmul(pC, onescol[0:1, 0:128], n1row[0:1, 0:512], start=True, stop=True))
        A("tensor", [],
          lambda e: e.matmul(a(0), onescol[0:1, 0:128], n1row[0:1, 512:1024], start=True, stop=True),
          inc=sPE)
        A("vector", [(sPE, sPE.n)], lambda e: e.tensor_copy(n1rep[:, 0:512], pC), inc=sDVE)
        A("vector", [], lambda e: e.tensor_copy(n1rep[:, 512:1024], a(0)), inc=sDVE)

        for st in range(NT):
            A("scalar", [(sDVE, sDVE.n)],
              lambda e, st=st: e.activation(
                  gbf, X1R[:, st * D:(st + 1) * D], AF.Square, accum_out=stat[:, 16:17]),
              inc=sACT)
            A("scalar", [],
              lambda e: e.activation(stat[:, 17:18], stat[:, 16:17], AF.Ln,
                                     bias=epsA[:, 1:2], scale=1.0 / D))
            A("scalar", [],
              lambda e: e.activation(stat[:, 18:19], stat[:, 17:18], AF.Exp, bias=0.0, scale=-0.5),
              inc=sACT)
            A("vector", [(sACT, sACT.n)],
              lambda e, st=st: e.tensor_scalar(
                  bscr, X1R[:, st * D:(st + 1) * D], stat[:, 18:19], None, op0=ALU.mult),
              inc=sDVE)
            A("vector", [], lambda e: e.tensor_mul(bscr, bscr, n1rep), inc=sDVE)
            h1done = sDVE.n
            trslot = [pA[:, 1024:1152], pA[:, 1536:1664]]
            for c in range(8):
                wts = [(sCP, max(0, sCP.n - 1))]
                if c == 0:
                    wts.append((sDVE, h1done))
                A("tensor", wts,
                  lambda e, c=c, trslot=trslot: e.transpose(
                      trslot[c % 2], bscr[:, c * 128:(c + 1) * 128], identf),
                  inc=sPE)
                A("vector", [(sPE, sPE.n)],
                  lambda e, st=st, c=c, trslot=trslot: e.tensor_copy(
                      h1T[c][:, st * 128:(st + 1) * 128], trslot[c % 2]),
                  inc=sCP)
        pl.flush(f"B0_s{s}")

        A("gpsimd", [],
          lambda e: e.dma_start(vwslot.rearrange("p (c j) -> p c j", c=8),
                                qkvT_d[:, 2 * D:3 * D].rearrange("(c p) j -> p c j", p=128)),
          inc=dWS, k=16)
        vw_rdy = dWS.n
        b1_free = {0: 0, 1: 0}
        for st in range(NT):
            vbank = pB if st % 2 == 0 else pA[:, 0:1024]
            for c in range(8):
                wts = []
                if c == 0:
                    wts = [(dWS, vw_rdy), (sCP, sCP.n)]
                    if st >= 2:
                        wts.append((sDVE, b1_free[st % 2]))
                for half in range(2):
                    A("tensor", wts if half == 0 else [],
                      lambda e, c=c, half=half, st=st, vbank=vbank: e.matmul(
                          vbank[:, half * 512:(half + 1) * 512],
                          h1T[c][:, st * 128:(st + 1) * 128],
                          vwslot[:, c * D + half * 512: c * D + (half + 1) * 512],
                          start=(c == 0), stop=(c == 7)),
                      inc=(sPE if (c == 7 and half == 1) else None))
            A("vector", [(sPE, sPE.n)],
              lambda e, st=st, vbank=vbank: e.tensor_copy(
                  Vplus[:, st * G65:(st + 1) * G65].rearrange("p (g w) -> p g w", w=65)[:, :, 0:64],
                  vbank.rearrange("p (g w) -> p g w", w=64)),
              inc=sDVE)
            A("vector", [],
              lambda e, st=st: e.memset(
                  Vplus[:, st * G65:(st + 1) * G65].rearrange("p (g w) -> p g w", w=65)[:, :, 64:65],
                  1.0),
              inc=sDVE)
            b1_free[st % 2] = sDVE.n
        pl.flush(f"B1_s{s}")

        ebuf_ready = [sPE.n, sPE.n, sPE.n]
        stb_free = {0: 0, 1: 0}
        av_free = {0: 0, 1: 0}
        todd_free = {0: 0, 1: 0}
        ps = {"qkw_free": {0: 0, 1: 0}, "qkpT_free": {0: 0, 1: 0},
              "pc_w": [], "pt_w": [], "evac": {}, "wrdy": {}, "ops": {}}

        def emit_qkdma(r):
            pb = r % 2
            for c in range(8):
                wts = [(sPE, ps["qkw_free"][pb])] if c == 0 else []
                A("gpsimd", wts,
                  lambda e, c=c, r=r, pb=pb: e.dma_start(
                      qkvw[pb][:, c * 256: c * 256 + 128],
                      qkvT_d[c * 128:(c + 1) * 128, r * 128:(r + 1) * 128]),
                  inc=dQW[pb], k=16)
                A("gpsimd", [],
                  lambda e, c=c, r=r, pb=pb: e.dma_start(
                      qkvw[pb][:, c * 256 + 128: c * 256 + 256],
                      qkvT_d[c * 128:(c + 1) * 128, D + r * 128: D + (r + 1) * 128]),
                  inc=dQW[pb], k=16)
            ps["wrdy"][r] = dQW[pb].n

        def emit_qkgrp(r, tgt, qq):
            pb = r % 2
            for c in range(8):
                wts = ([(dQW[pb], ps["wrdy"][r])] + ps["pc_w"]) if c == 0 else []
                A("tensor", wts,
                  lambda e, c=c, tgt=tgt, qq=qq, pb=pb: e.matmul(
                      pC[:, 0:512],
                      qkvw[pb][:, c * 256 + tgt * 128: c * 256 + tgt * 128 + 128],
                      h1T[c][:, qq * 512:(qq + 1) * 512],
                      start=(c == 0), stop=(c == 7)),
                  inc=(sPE if c == 7 else None))
            ps["qkw_free"][pb] = sPE.n
            A("scalar", [(sPE, sPE.n), (sPE, ps["qkpT_free"][pb])],
              lambda e, tgt=tgt, qq=qq, pb=pb: e.activation(
                  qkpT[pb][:, tgt * S + qq * 512: tgt * S + (qq + 1) * 512],
                  pC[:, 0:512], AF.Copy, bias=0.0, scale=(0.125 if tgt == 0 else 1.0)),
              inc=sACT)
            ps["evac"][(pb, tgt, qq)] = sACT.n
            ps["pc_w"] = [(sACT, sACT.n)]

        def emit_tmchunk(r, qt, kc):
            pb = r % 2
            kend2 = (qt + 1) * 128
            nkc2 = (kend2 + 511) // 512
            w2 = min(512, kend2 - kc * 512)
            wts = [(sACT, ps["evac"][(pb, 0, qt // 4)]),
                   (sACT, ps["evac"][(pb, 1, kc)])] + ps["pc_w"]
            A("tensor", wts,
              lambda e, qt=qt, kc=kc, w2=w2, pb=pb: e.matmul(
                  pC[:, 0:w2],
                  qkpT[pb][0:64, qt * 128:(qt + 1) * 128],
                  qkpT[pb][0:64, S + kc * 512: S + kc * 512 + w2],
                  start=True, stop=True, tile_position=(0, 0)))
            A("tensor", list(ps["pt_w"]),
              lambda e, qt=qt, kc=kc, w2=w2, pb=pb: e.matmul(
                  pT[:, 0:w2],
                  qkpT[pb][64:128, qt * 128:(qt + 1) * 128],
                  qkpT[pb][64:128, S + kc * 512: S + kc * 512 + w2],
                  start=True, stop=True, tile_position=(64, 0)),
              inc=sPE)
            mm2 = sPE.n
            if kc == nkc2 - 1:
                doff = qt * 128 - kc * 512
                A("vector", [(sPE, mm2)],
                  lambda e, doff=doff: e.tensor_add(
                      pC[:, doff:doff + 128], pC[:, doff:doff + 128], maskA),
                  inc=sDVE)
                A("vector", [],
                  lambda e, doff=doff: e.tensor_add(
                      pT[:, doff:doff + 128], pT[:, doff:doff + 128], maskA),
                  inc=sDVE)
            A("vector", [(sPE, mm2), (sDVE, sDVE.n)],
              lambda e, w2=w2: e.reduce_max(stat[:, 32:33], pC[:, 0:w2], axis=AX.X),
              inc=sDVE)
            ps["pc_w"] = [(sDVE, sDVE.n)]
            A("vector", [],
              lambda e, w2=w2: e.reduce_max(stat[:, 33:34], pT[:, 0:w2], axis=AX.X),
              inc=sDVE)
            ps["pt_w"] = [(sDVE, sDVE.n)]
            if kc == 0:
                A("vector", [(sDVE, sDVE.n)],
                  lambda e: e.tensor_copy(stat[:, 34:36], stat[:, 32:34]), inc=sDVE)
            else:
                A("vector", [(sDVE, sDVE.n)],
                  lambda e: e.tensor_max(stat[:, 34:36], stat[:, 34:36], stat[:, 32:34]),
                  inc=sDVE)
            if kc == nkc2 - 1:
                for hh in range(2):
                    mc = hh * 16 + qt
                    A("vector", [(sDVE, sDVE.n)],
                      lambda e, mc=mc, hh=hh: e.tensor_scalar(
                          mnegbf[:, mc: mc + 1],
                          stat[:, 34 + hh:35 + hh], -1.0, None, op0=ALU.mult),
                      inc=sDVE)

        def emit_dance(r):
            A("tensor", [(sDVE, sDVE.n)] + list(ps["pt_w"]),
              lambda e: e.transpose(pT[0:32, 0:128], mnegbf[:, 0:32], identf),
              inc=sPE)
            tpn = sPE.n
            cw = [(sPE, tpn)]
            if r >= 1:
                cw.append((sS, ps["ops"][r - 1]))
            A("vector", cw,
              lambda e: e.tensor_copy(negrow, pT[0:32, 0:128]), inc=sDVE)
            ps["pt_w"] = [(sDVE, sDVE.n)]
            A("sync", [(sDVE, sDVE.n)],
              lambda e: e.dma_start(negMrowA[0:1, 0:S], negrow[0:16, :]),
              inc=sS, k=16)
            A("sync", [],
              lambda e: e.dma_start(negMrowA[64:65, 0:S], negrow[16:32, :]),
              inc=sS, k=16)
            ps["ops"][r] = sS.n

        def emit_stav(pp, units):
            pb = pp % 2
            ops_rdy = ps["ops"][pp]
            uq = list(units)
            tot = len(uq)
            nslots = 72
            sl = [0]

            def pump():
                sl[0] += 1
                while uq and (tot - len(uq)) * nslots < sl[0] * tot:
                    uq.pop(0)()

            for qc in range(8):
                ntk = 2 * qc + 2
                avb = pB[:, 0:512] if qc % 2 == 0 else pB[:, 512:1024]

                def emit_sc(kt, qc=qc):
                    par = kt % 2
                    half = (kt == 2 * qc + 1)
                    qlo = 128 if half else 0
                    sb = [a(0), a(1)] if par == 0 else [a(2), a(3)]
                    for hh in range(2):
                        wts = [(sS, ops_rdy), (sACT, stb_free[par])] if hh == 0 else []
                        A("tensor", wts,
                          lambda e, hh=hh, kt=kt, qc=qc, qlo=qlo, sb=sb: e.matmul(
                              sb[hh][:, qlo:256],
                              qkpT[pb][hh * 64:(hh + 1) * 64, S + kt * 128:S + (kt + 1) * 128],
                              qkpT[pb][hh * 64:(hh + 1) * 64, qc * 256 + qlo:(qc + 1) * 256],
                              start=True, stop=False,
                              tile_position=(hh * 64, 0)))
                    for hh in range(2):
                        A("tensor", [],
                          lambda e, hh=hh, qc=qc, qlo=qlo, sb=sb: e.matmul(
                              sb[hh][:, qlo:256],
                              onescolb[hh * 64:hh * 64 + 1, 0:128],
                              negMrowA[hh * 64:hh * 64 + 1, qc * 256 + qlo:(qc + 1) * 256],
                              start=False, stop=True,
                              tile_position=(hh * 64, 0)),
                          inc=(sPE if hh == 1 else None))
                    sc_done = sPE.n
                    nmask = 0
                    if kt == 2 * qc:
                        for hh in range(2):
                            A("vector", [(sPE, sc_done)] if hh == 0 else [],
                              lambda e, hh=hh, sb=sb: e.tensor_add(
                                  sb[hh][:, 0:128], sb[hh][:, 0:128], maskB),
                              inc=sDVE)
                        nmask = 2
                    elif half:
                        for hh in range(2):
                            A("vector", [(sPE, sc_done)] if hh == 0 else [],
                              lambda e, hh=hh, sb=sb: e.tensor_add(
                                  sb[hh][:, 128:256], sb[hh][:, 128:256], maskB),
                              inc=sDVE)
                        nmask = 2
                    msk_done = sDVE.n
                    ewts = [(sPE, sc_done), (sPE, ebuf_ready[kt % 3])]
                    if nmask:
                        ewts.append((sDVE, msk_done))
                    for hh in range(2):
                        A("scalar", ewts if hh == 0 else [],
                          lambda e, hh=hh, kt=kt, qlo=qlo, sb=sb: e.activation(
                              Ebuf2[kt % 3][:, hh * 256 + qlo:(hh + 1) * 256],
                              sb[hh][:, qlo:256], AF.Exp),
                          inc=(sACT if hh == 1 else None))
                    return sACT.n

                def emit_avb(kt, qc=qc, exp_n=0, ntk=ntk, avb=avb):
                    half = (kt == 2 * qc + 1)
                    qlo = 128 if half else 0
                    for hh in range(2):
                        wts = []
                        if hh == 0:
                            wts = [(sACT, exp_n)]
                            if kt == 0:
                                wts.append((sDVE, av_free[qc % 2]))
                        A("tensor", wts,
                          lambda e, hh=hh, kt=kt, qc=qc, pp=pp, ntk=ntk, qlo=qlo, avb=avb: e.matmul(
                              avb[0:65, hh * 256 + qlo:(hh + 1) * 256],
                              Vplus[:, kt * G65 + (2 * pp + hh) * 65: kt * G65 + (2 * pp + hh) * 65 + 65],
                              Ebuf2[kt % 3][:, hh * 256 + qlo:(hh + 1) * 256],
                              start=(kt == 0 and hh == 0), stop=(kt == ntk - 1 and hh == 1)),
                          inc=(sPE if hh == 1 else None))
                    ebuf_ready[kt % 3] = sPE.n

                exps = []
                for kt in range(ntk):
                    if kt >= 2:
                        stb_free[kt % 2] = exps[kt - 2]
                    exps.append(emit_sc(kt))
                    pump()
                    if kt >= 1:
                        emit_avb(kt - 1, exp_n=exps[kt - 1])
                emit_avb(ntk - 1, exp_n=exps[ntk - 1])
                av_fin = sPE.n
                stb_free[(ntk - 2) % 2] = exps[ntk - 2]
                stb_free[(ntk - 1) % 2] = exps[ntk - 1]

                A("scalar", [(sPE, av_fin)],
                  lambda e, avb=avb: e.activation(
                      rsrow[0:1, 0:512], avb[64:65, 0:512],
                      AF.Ln, bias=eps1[0:1, 0:1]))
                A("scalar", [],
                  lambda e: e.activation(
                      rsrow[0:1, 0:512], rsrow[0:1, 0:512],
                      AF.Exp, bias=0.0, scale=-1.0),
                  inc=sACT)
                rs_done = sACT.n
                A("tensor", [(sACT, rs_done)],
                  lambda e, avb=avb: e.matmul(avb[64:128, 0:512], onescol[0:1, 0:64],
                                              rsrow, start=True, stop=True,
                                              tile_position=(0, 64)),
                  inc=sPE)
                bc_done = sPE.n
                A("vector", [(sPE, bc_done)],
                  lambda e, avb=avb: e.tensor_copy(rsb, avb[64:128, 0:512]),
                  inc=sDVE)
                A("vector", [],
                  lambda e, pp=pp, qc=qc, avb=avb: e.tensor_mul(
                      attnOT[pp][0:64, qc * 256:(qc + 1) * 256],
                      avb[0:64, 0:256], rsb[0:64, 0:256]),
                  inc=sDVE)
                A("vector", [(dTD[qc % 2], todd_free[qc % 2])],
                  lambda e, qc=qc, avb=avb: e.tensor_mul(
                      todd[0:64, (qc % 2) * 256:(qc % 2) * 256 + 256],
                      avb[0:64, 256:512], rsb[0:64, 256:512]),
                  inc=sDVE)
                av_free[qc % 2] = sDVE.n
                A("sync", [(sDVE, sDVE.n)],
                  lambda e, pp=pp, qc=qc: e.dma_start(
                      attnOT[pp][64:128, qc * 256:(qc + 1) * 256],
                      todd[0:64, (qc % 2) * 256:(qc % 2) * 256 + 256]),
                  inc=dTD[qc % 2], k=16)
                todd_free[qc % 2] = dTD[qc % 2].n
            while uq:
                uq.pop(0)()
            ps["qkpT_free"][pb] = sPE.n

        for r in range(NP + 1):
            units = []
            if r < NP:
                emit_qkdma(r)
                for qq in range(4):
                    units.append(lambda r=r, qq=qq: emit_qkgrp(r, 0, qq))
                    units.append(lambda r=r, qq=qq: emit_qkgrp(r, 1, qq))
                for qt in range(NT):
                    for kc in range((qt // 4) + 1):
                        units.append(lambda r=r, qt=qt, kc=kc: emit_tmchunk(r, qt, kc))
            if r == 0:
                for u in units:
                    u()
            else:
                emit_stav(r - 1, units)
            if r < NP:
                emit_dance(r)
        pl.flush(f"B23_s{s}")

        if upto == "B23d" and s == 0:
            for i in range(2):
                A("vector", [(sO, sO.n)],
                  lambda e, i=i: e.tensor_copy(dbgscr, attnOT[0][:, i * 1024:(i + 1) * 1024]),
                  inc=sDVE)
                A("sync", [(sDVE, sDVE.n)],
                  lambda e, i=i: e.dma_start(dbg_d[:, i * 1024:(i + 1) * 1024], dbgscr), inc=sO, k=16)
            A("vector", [(sO, sO.n)], lambda e: e.tensor_copy(dbgscr[:, 0:32], mnegbf), inc=sDVE)
            A("sync", [(sDVE, sDVE.n)], lambda e: e.dma_start(dbg_d[:, 2048:2112], dbgscr[:, 0:64]),
              inc=sO, k=16)
            A("vector", [(sO, sO.n)], lambda e: e.tensor_copy(dbgscr[:, 0:256], h1T[0][:, 0:256]),
              inc=sDVE)
            A("sync", [(sDVE, sDVE.n)], lambda e: e.dma_start(dbg_d[:, 2304:2560], dbgscr[:, 0:256]),
              inc=sO, k=16)
            A("vector", [(sO, sO.n)], lambda e: e.tensor_copy(dbgscr[:, 0:1024], Vplus[:, 0:1024]),
              inc=sDVE)
            A("sync", [(sDVE, sDVE.n)], lambda e: e.dma_start(dbg_d[:, 2560:3584], dbgscr[:, 0:1024]),
              inc=sO, k=16)
            A("vector", [(sO, sO.n)], lambda e: e.tensor_copy(dbgscr[:, 0:1024], qkpT[0][:, 0:1024]),
              inc=sDVE)
            A("sync", [(sDVE, sDVE.n)], lambda e: e.dma_start(dbg_d[:, 3072:4096], dbgscr[:, 0:1024]),
              inc=sO, k=16)
            pl.flush("dbgB23")
            break

        A("gpsimd", [],
          lambda e: e.dma_start(owT.rearrange("p (c j) -> p c j", c=8),
                                owT_d.rearrange("(c p) j -> p c j", p=128)),
          inc=sG, k=16)
        ow_rdy = sG.n
        attn_fin = [(sS, sS.n), (sDVE, sDVE.n), (dTD[0], dTD[0].n), (dTD[1], dTD[1].n)]
        b4_free = {0: 0, 1: 0}
        for st in range(NT):
            bank = pB if st % 2 == 0 else pA[:, 0:1024]
            for c in range(8):
                wts = []
                if c == 0:
                    wts = [(sG, ow_rdy)] + attn_fin + [(sDVE, b4_free[st % 2])]
                for half in range(2):
                    A("tensor", wts if half == 0 else [],
                      lambda e, c=c, half=half, st=st, bank=bank: e.matmul(
                          bank[:, half * 512:(half + 1) * 512],
                          attnOT[c][:, st * 128:(st + 1) * 128],
                          owT[:, c * D + half * 512: c * D + (half + 1) * 512],
                          start=(c == 0), stop=(c == 7)),
                      inc=(sPE if (c == 7 and half == 1) else None))
            A("vector", [(sPE, sPE.n)],
              lambda e, st=st, bank=bank: e.tensor_add(
                  X2[:, st * D:(st + 1) * D], bank, X1R[:, st * D:(st + 1) * D]),
              inc=sDVE)
            A("vector", [],
              lambda e, st=st: e.tensor_add(
                  X1R[:, st * D:(st + 1) * D], X2[:, st * D:(st + 1) * D],
                  X1R[:, st * D:(st + 1) * D]),
              inc=sDVE)
            b4_free[st % 2] = sDVE.n
        pl.flush(f"B4_s{s}")

        if upto == "X2":
            for st in range(NT):
                A("sync", [(sDVE, sDVE.n)],
                  lambda e, s=s, st=st: e.dma_start(
                      y_d[s, st * 128:(st + 1) * 128, :], X2[:, st * D:(st + 1) * D]),
                  inc=sO, k=16)
            A("vector", [(sO, sO.n)], lambda e: e.memset(epsA[:, 4:5], 0.0), inc=sDVE)
            pl.flush(f"dbgX2_s{s}")
            continue

        A("sync", [], lambda e: e.dma_start(n2row, n2_d), inc=sS, k=16)
        n2_rdy = sS.n
        A("tensor", [(sS, n2_rdy)],
          lambda e: e.matmul(pC, onescol[0:1, 0:128], n2row[0:1, 0:512], start=True, stop=True))
        A("tensor", [],
          lambda e: e.matmul(a(0), onescol[0:1, 0:128], n2row[0:1, 512:1024], start=True, stop=True),
          inc=sPE)
        A("vector", [(sPE, sPE.n)], lambda e: e.tensor_copy(n2rep[:, 0:512], pC), inc=sDVE)
        A("vector", [], lambda e: e.tensor_copy(n2rep[:, 512:1024], a(0)), inc=sDVE)
        A("gpsimd", [],
          lambda e: e.dma_start(gateT_sb.rearrange("p (c j) -> p c j", c=8),
                                gateT_d.rearrange("(c p) j -> p c j", p=128)),
          inc=sG, k=16)
        gate_rdy = sG.n
        for st in range(NT):
            A("scalar", [(sDVE, sDVE.n)],
              lambda e, st=st: e.activation(
                  gbf, X2[:, st * D:(st + 1) * D], AF.Square, accum_out=stat[:, 20:21]),
              inc=sACT)
            A("scalar", [],
              lambda e: e.activation(stat[:, 21:22], stat[:, 20:21], AF.Ln,
                                     bias=epsA[:, 1:2], scale=1.0 / D))
            A("scalar", [],
              lambda e: e.activation(stat[:, 22:23], stat[:, 21:22], AF.Exp, bias=0.0, scale=-0.5),
              inc=sACT)
            A("vector", [(sACT, sACT.n)],
              lambda e, st=st: e.tensor_scalar(
                  bscr2, X2[:, st * D:(st + 1) * D], stat[:, 22:23], None, op0=ALU.mult),
              inc=sDVE)
            A("vector", [], lambda e: e.tensor_mul(bscr2, bscr2, n2rep), inc=sDVE)
            gdone = sDVE.n
            trslot = [pA[:, 1024:1152], pA[:, 1536:1664]]
            for c in range(8):
                wts = [(sCP, max(0, sCP.n - 1))]
                if c == 0:
                    wts.append((sDVE, gdone))
                A("tensor", wts,
                  lambda e, c=c, trslot=trslot: e.transpose(
                      trslot[c % 2], bscr2[:, c * 128:(c + 1) * 128], identf),
                  inc=sPE)
                A("vector", [(sPE, sPE.n)],
                  lambda e, st=st, c=c, trslot=trslot: e.tensor_copy(
                      gT[c][:, st * 128:(st + 1) * 128], trslot[c % 2]),
                  inc=sCP)
            gt_done = sCP.n
            for c in range(8):
                wts = [(sCP, gt_done), (sG, gate_rdy), (sDVE, sDVE.n)] if c == 0 else []
                A("tensor", wts,
                  lambda e, c=c, st=st: e.matmul(
                      pC[:, 0:NE], gT[c][:, st * 128:(st + 1) * 128],
                      gateT_sb[:, c * NE:(c + 1) * NE], start=(c == 0), stop=(c == 7)),
                  inc=(sPE if c == 7 else None))
            A("vector", [(sPE, sPE.n)],
              lambda e: e.reduce_max(stat[:, 24:25], pC[:, 0:NE], axis=AX.X, negate=True),
              inc=sDVE)
            A("scalar", [(sDVE, sDVE.n)],
              lambda e: e.activation(stat[:, 32:40], pC[:, 0:NE], AF.Exp,
                                     bias=stat[:, 24:25], accum_out=stat[:, 25:26]),
              inc=sACT)
            A("vector", [(sACT, sACT.n)],
              lambda e: e.reciprocal(stat[:, 26:27], stat[:, 25:26]), inc=sDVE)
            A("vector", [(sDVE, sDVE.n)],
              lambda e: e.tensor_scalar(stat[:, 32:40], stat[:, 32:40], stat[:, 26:27], None,
                                        op0=ALU.mult),
              inc=sDVE)
            A("vector", [(sDVE, sDVE.n)], lambda e: e.reduce_max(stat[:, 27:28], stat[:, 32:40], axis=AX.X),
              inc=sDVE)
            A("vector", [(sDVE, sDVE.n)],
              lambda e: e.tensor_scalar(stat[:, 40:48], stat[:, 32:40], stat[:, 27:28], None,
                                        op0=ALU.subtract),
              inc=sDVE)
            A("vector", [(sDVE, sDVE.n)],
              lambda e: e.tensor_scalar(stat[:, 40:48], stat[:, 40:48], -1e30, 1.0,
                                        op0=ALU.mult, op1=ALU.min),
              inc=sDVE)
            A("vector", [(sDVE, sDVE.n)],
              lambda e: e.tensor_scalar(stat[:, 40:48], stat[:, 40:48], 1.0, 1e30,
                                        op0=ALU.subtract, op1=ALU.mult),
              inc=sDVE)
            A("vector", [(sDVE, sDVE.n)], lambda e: e.tensor_add(stat[:, 40:48], stat[:, 32:40], stat[:, 40:48]),
              inc=sDVE)
            A("vector", [(sDVE, sDVE.n)], lambda e: e.reduce_max(stat[:, 28:29], stat[:, 40:48], axis=AX.X),
              inc=sDVE)
            A("vector", [(sDVE, sDVE.n)], lambda e: e.tensor_add(stat[:, 29:30], stat[:, 27:28], stat[:, 28:29]),
              inc=sDVE)
            A("vector", [(sDVE, sDVE.n)],
              lambda e: e.tensor_scalar(stat[:, 30:31], stat[:, 29:30], 1e-9, None, op0=ALU.add),
              inc=sDVE)
            A("vector", [(sDVE, sDVE.n)], lambda e: e.reciprocal(stat[:, 31:32], stat[:, 30:31]), inc=sDVE)
            A("vector", [(sDVE, sDVE.n)],
              lambda e, st=st: e.tensor_mul(factor[:, st:st + 1], stat[:, 29:30], stat[:, 31:32]),
              inc=sDVE)
        pl.flush(f"B5_s{s}")

        up_free = {0: 0, 1: 0}
        dn_free = {0: 0, 1: 0}
        for ep in range(4):
            for c in range(8):
                wts = [(sACT, sACT.n), (sPE, sPE.n)] if (c == 0 and ep >= 2) else []
                A("gpsimd", wts,
                  lambda e, c=c, ep=ep: e.dma_start(
                      upw[ep % 2][:, c * KB:(c + 1) * KB],
                      upT_d[c * 128:(c + 1) * 128, ep * KB:(ep + 1) * KB]),
                  inc=dUW[ep % 2], k=16)
            for cc in range(8):
                A("gpsimd", [],
                  lambda e, cc=cc, ep=ep: e.dma_start(
                      dww[ep % 2][:, cc * KB:(cc + 1) * KB],
                      downT_d[ep * KB + cc * 128: ep * KB + (cc + 1) * 128, :]),
                  inc=dUW[ep % 2], k=16)
            w_rdy = dUW[ep % 2].n
            for sc in range(4):
                for et in range(8):
                    bank = a(0) if et % 2 == 0 else a(1)
                    for c in range(8):
                        wts = [(dUW[ep % 2], w_rdy), (sACT, up_free[et % 2])] if c == 0 else []
                        A("tensor", wts,
                          lambda e, c=c, et=et, sc=sc, ep=ep, bank=bank: e.matmul(
                              bank, upw[ep % 2][:, c * KB + et * 128: c * KB + (et + 1) * 128],
                              gT[c][:, sc * 512:(sc + 1) * 512],
                              start=(c == 0), stop=(c == 7)),
                          inc=(sPE if c == 7 else None))
                    A("scalar", [(sPE, sPE.n)],
                      lambda e, et=et, sc=sc, bank=bank: e.activation(
                          siluT[sc % 2][:, et * 512:(et + 1) * 512], bank, AF.Silu),
                      inc=sACT)
                    up_free[et % 2] = sACT.n
                silu_done = sACT.n
                for st2 in range(4):
                    st = sc * 4 + st2
                    bank = pB if st2 % 2 == 0 else pA[:, 1024:2048]
                    for et in range(8):
                        wts = []
                        if et == 0:
                            wts = [(sACT, silu_done), (sDVE, dn_free[st2 % 2]),
                                   (dUW[ep % 2], w_rdy)]
                        for half in range(2):
                            A("tensor", wts if half == 0 else [],
                              lambda e, et=et, half=half, st2=st2, sc=sc, ep=ep, bank=bank: e.matmul(
                                  bank[:, half * 512:(half + 1) * 512],
                                  siluT[sc % 2][:, et * 512 + st2 * 128: et * 512 + st2 * 128 + 128],
                                  dww[ep % 2][:, et * KB + half * 512: et * KB + (half + 1) * 512],
                                  start=(et == 0), stop=(et == 7)),
                              inc=(sPE if (et == 7 and half == 1) else None))
                    A("vector", [(sPE, sPE.n)],
                      lambda e, st=st, bank=bank: e.scalar_tensor_tensor(
                          X1R[:, st * D:(st + 1) * D], bank, factor[:, st:st + 1],
                          X1R[:, st * D:(st + 1) * D], op0=ALU.mult, op1=ALU.add),
                      inc=sDVE)
                    dn_free[st2 % 2] = sDVE.n
        pl.flush(f"B6_s{s}")

        ostore_free = {0: dX[0].n, 1: dX[1].n}
        for st in range(NT):
            A("scalar", [(sDVE, sDVE.n)],
              lambda e, st=st: e.activation(
                  lnscr, X1R[:, st * D:(st + 1) * D], AF.Square, accum_out=stat[:, 50:51]),
              inc=sACT)
            A("vector", [(sACT, sACT.n)],
              lambda e, st=st: e.reduce_sum(stat[:, 49:50], X1R[:, st * D:(st + 1) * D], axis=AX.X),
              inc=sDVE)
            A("vector", [(sDVE, sDVE.n)],
              lambda e: e.tensor_scalar(stat[:, 51:52], stat[:, 49:50], 1.0 / D, None, op0=ALU.mult),
              inc=sDVE)
            A("vector", [(sDVE, sDVE.n)],
              lambda e: e.tensor_scalar(stat[:, 52:53], stat[:, 50:51], 1.0 / D, None, op0=ALU.mult),
              inc=sDVE)
            A("vector", [(sDVE, sDVE.n)], lambda e: e.tensor_mul(stat[:, 53:54], stat[:, 51:52], stat[:, 51:52]),
              inc=sDVE)
            A("vector", [(sDVE, sDVE.n)], lambda e: e.tensor_sub(stat[:, 54:55], stat[:, 52:53], stat[:, 53:54]),
              inc=sDVE)
            A("scalar", [(sDVE, sDVE.n)],
              lambda e: e.activation(stat[:, 55:56], stat[:, 54:55], AF.Ln, bias=epsA[:, 0:1]))
            A("scalar", [],
              lambda e: e.activation(stat[:, 56:57], stat[:, 55:56], AF.Exp, bias=0.0, scale=-0.5),
              inc=sACT)
            A("vector", [(sACT, sACT.n), (sDVE, sDVE.n), (dX[st % 2], ostore_free[st % 2])],
              lambda e, st=st: e.tensor_scalar(
                  outst[:, (st % 2) * D:(st % 2 + 1) * D], X1R[:, st * D:(st + 1) * D],
                  stat[:, 51:52], stat[:, 56:57], op0=ALU.subtract, op1=ALU.mult),
              inc=sDVE)
            A("sync", [(sDVE, sDVE.n)],
              lambda e, s=s, st=st: e.dma_start(
                  y_d[s, st * 128:(st + 1) * 128, :], outst[:, (st % 2) * D:(st % 2 + 1) * D]),
              inc=dX[st % 2], k=16)
            ostore_free[st % 2] = dX[st % 2].n
        pl.flush(f"B7_s{s}")

    with nc.Block("fin") as blk:
        @blk.sync
        def _(e):
            e.wait_ge(sO.h, sO.n)
            e.wait_ge(dX[0].h, dX[0].n)
            e.wait_ge(dX[1].h, dX[1].n)

    nc.finalize()
    return nc


_NC_CACHE = {}


def _get_nc(upto):
    if upto not in _NC_CACHE:
        _NC_CACHE[upto] = build_program(upto)
    return _NC_CACHE[upto]


def make_consts():
    identbf = np.eye(128, dtype=ml_dtypes.bfloat16)
    identf = np.eye(128, dtype=np.float32)
    onescol = np.ones((1, 128), dtype=np.float32)
    onescolb = np.ones((128, 128), dtype=ml_dtypes.bfloat16)
    ones32 = np.ones((128, 32), dtype=ml_dtypes.bfloat16)
    i = np.arange(128).reshape(128, 1)
    c = np.arange(2048).reshape(1, 2048)
    rampA = (c - C0 - i).astype(np.float32)
    j = np.arange(128).reshape(1, 128)
    maskA = np.where(j > i, -1e30, 0.0).astype(np.float32)
    maskB = np.where(i > j, -1e30, 0.0).astype(np.float32)
    return identbf, identf, onescol, onescolb, ones32, rampA, maskA, maskB


def kernel(L0_attn_x, L0_attn_W_qk, L0_attn_W_v, L0_attn_alibi_slope,
           L0_ffn_n1_w, L0_ffn_qkv_w, L0_ffn_o_w, L0_ffn_n2_w,
           L0_ffn_gate_w, L0_ffn_up_w, L0_ffn_down_w, _upto="full", _ncores=8):
    X = np.ascontiguousarray(np.asarray(L0_attn_x, dtype=np.float32))
    wqk = np.ascontiguousarray(np.asarray(L0_attn_W_qk, dtype=np.float32))
    wvT = np.ascontiguousarray(np.asarray(L0_attn_W_v, dtype=np.float32).T)
    slope = np.asarray(L0_attn_alibi_slope, dtype=np.float32).reshape(1, 1)
    n1 = np.asarray(L0_ffn_n1_w, dtype=np.float32).reshape(1, D)
    n2 = np.asarray(L0_ffn_n2_w, dtype=np.float32).reshape(1, D)
    qkvT = np.ascontiguousarray(np.asarray(L0_ffn_qkv_w, dtype=np.float32).T)
    owT = np.ascontiguousarray(np.asarray(L0_ffn_o_w, dtype=np.float32).T)
    gateT = np.ascontiguousarray(np.asarray(L0_ffn_gate_w, dtype=np.float32).T)
    upT = np.ascontiguousarray(np.asarray(L0_ffn_up_w, dtype=np.float32).T)
    downT = np.ascontiguousarray(np.asarray(L0_ffn_down_w, dtype=np.float32).T)
    identbf, identf, onescol, onescolb, ones32, rampA, maskA, maskB = make_consts()

    nc = _get_nc(_upto)
    in_maps = []
    for i in range(_ncores):
        in_maps.append({
            "x": np.ascontiguousarray(X[2 * i:2 * i + 2]),
            "wqk": wqk, "wvT": wvT, "slope": slope, "n1": n1, "n2": n2,
            "qkvT": qkvT, "owT": owT, "gateT": gateT, "upT": upT, "downT": downT,
            "identbf": identbf, "identf": identf, "onescol": onescol,
            "onescolb": onescolb, "ones32": ones32,
            "rampA": rampA, "maskA": maskA, "maskB": maskB,
        })
    res = run_bass_kernel_spmd(nc, in_maps, core_ids=list(range(_ncores)))
    if _upto in ("A01", "setup", "A2d", "B23d"):
        return res.results[0]["dbg"]
    out = np.concatenate([r["y"] for r in res.results], axis=0)
    return out.reshape(2 * _ncores, S, D).astype(np.float32)


# version 16
# speedup vs baseline: 1.1590x; 1.0190x over previous
import os
import numpy as np
import ml_dtypes
import concourse.bass as bass
import concourse.mybir as mybir
from concourse.bass_utils import run_bass_kernel_spmd

F32 = mybir.dt.float32
BF16 = mybir.dt.bfloat16
AF = mybir.ActivationFunctionType
ALU = mybir.AluOpType
AX = mybir.AxisListType

S = 2048
D = 1024
NT = 16
H = 16
HD = 64
NP = 8
ED = 4096
NE = 8
NSEQ = 2
C0 = 1920
KB = 1024
G65 = H * 65


class Sem:
    def __init__(self, nc, name):
        self.cm = nc.semaphore(name)
        self.h = self.cm.__enter__()
        self.n = 0


class Plan:
    ENGS = ("sync", "gpsimd", "tensor", "scalar", "vector")

    def __init__(self, nc):
        self.nc = nc
        self.q = {e: [] for e in self.ENGS}

    def add(self, eng, waits, fn, inc=None, k=1):
        wcap = [(sm.h, v) for (sm, v) in waits]
        if inc is not None:
            inc.n += k
        ih = None if inc is None else inc.h

        def run(e, wcap=wcap, fn=fn, ih=ih, k=k):
            for (h, v) in wcap:
                if v > 0:
                    e.wait_ge(h, v)
            ins = fn(e)
            if ih is not None:
                ins.then_inc(ih, k)
        self.q[eng].append(run)

    def flush(self, name=None):
        q = self.q
        self.q = {e: [] for e in self.ENGS}
        with self.nc.Block(name) as blk:
            for ename in self.ENGS:
                fns = q[ename]
                if not fns:
                    continue

                def body(eng, fns=fns):
                    for f in fns:
                        f(eng)
                getattr(blk, ename)(body)


def build_program(upto="full"):
    nc = bass.Bass("TRN2", target_bir_lowering=False)

    x_d = nc.dram_tensor("x", [NSEQ, S, D], F32, kind="ExternalInput").ap()
    wqk_d = nc.dram_tensor("wqk", [D, D], F32, kind="ExternalInput").ap()
    wvT_d = nc.dram_tensor("wvT", [D, D], F32, kind="ExternalInput").ap()
    slope_d = nc.dram_tensor("slope", [1, 1], F32, kind="ExternalInput").ap()
    n1_d = nc.dram_tensor("n1", [1, D], F32, kind="ExternalInput").ap()
    n2_d = nc.dram_tensor("n2", [1, D], F32, kind="ExternalInput").ap()
    qkvT_d = nc.dram_tensor("qkvT", [D, 3 * D], F32, kind="ExternalInput").ap()
    owT_d = nc.dram_tensor("owT", [D, D], F32, kind="ExternalInput").ap()
    gateT_d = nc.dram_tensor("gateT", [D, NE], F32, kind="ExternalInput").ap()
    upT_d = nc.dram_tensor("upT", [D, ED], F32, kind="ExternalInput").ap()
    downT_d = nc.dram_tensor("downT", [ED, D], F32, kind="ExternalInput").ap()
    identbf_d = nc.dram_tensor("identbf", [128, 128], BF16, kind="ExternalInput").ap()
    identf_d = nc.dram_tensor("identf", [128, 128], F32, kind="ExternalInput").ap()
    onescol_d = nc.dram_tensor("onescol", [1, 128], F32, kind="ExternalInput").ap()
    onescolb_d = nc.dram_tensor("onescolb", [128, 128], BF16, kind="ExternalInput").ap()
    ones32_d = nc.dram_tensor("ones32", [128, 32], BF16, kind="ExternalInput").ap()
    rampA_d = nc.dram_tensor("rampA", [128, 2048], F32, kind="ExternalInput").ap()
    maskA_d = nc.dram_tensor("maskA", [128, 128], F32, kind="ExternalInput").ap()
    maskB_d = nc.dram_tensor("maskB", [128, 128], F32, kind="ExternalInput").ap()
    y_d = nc.dram_tensor("y", [NSEQ, S, D], F32, kind="ExternalOutput").ap()
    dbg_d = nc.dram_tensor("dbg", [128, 4096], F32, kind="ExternalOutput").ap()

    X1R = nc.alloc_sbuf_tensor("X1R", [128, NT * D], F32).ap()
    factor = nc.alloc_sbuf_tensor("factor", [128, NT], F32).ap()
    maskA = nc.alloc_sbuf_tensor("maskA_s", [128, 128], F32).ap()
    maskB = nc.alloc_sbuf_tensor("maskB_s", [128, 128], F32).ap()
    identbf = nc.alloc_sbuf_tensor("identbf_s", [128, 128], BF16).ap()
    identf = nc.alloc_sbuf_tensor("identf_s", [128, 128], F32).ap()
    onescol = nc.alloc_sbuf_tensor("onescol_s", [1, 128], F32).ap()
    onescolb = nc.alloc_sbuf_tensor("onescolb_s", [128, 128], BF16).ap()
    ones32 = nc.alloc_sbuf_tensor("ones32_s", [128, 32], BF16).ap()
    slope_sb = nc.alloc_sbuf_tensor("slope_sb", [1, 32], F32).ap()
    slope_bc = nc.alloc_sbuf_tensor("slope_bc", [128, 1], F32).ap()
    epsA = nc.alloc_sbuf_tensor("epsA", [128, 8], F32).ap()
    eps1 = nc.alloc_sbuf_tensor("eps1", [1, 8], F32).ap()
    stat = nc.alloc_sbuf_tensor("stat", [128, 64], F32).ap()
    mu1 = nc.alloc_sbuf_tensor("mu1", [1, 16], F32).ap()
    mu1b = nc.alloc_sbuf_tensor("mu1b", [8, 8], BF16).ap()
    nbr = nc.alloc_sbuf_tensor("nbr", [128, 8], F32).ap()
    nbrb = nc.alloc_sbuf_tensor("nbrb", [128, 8], BF16).ap()
    biasH = nc.alloc_sbuf_tensor("biasH", [128, 2], F32).ap()

    TOP = 229344
    ARENA = 138 * KB
    AB = TOP - ARENA

    def at(name, off, shape, dtype):
        esz = 4 if dtype == F32 else 2
        assert off + shape[1] * esz <= ARENA, (name, off, shape)
        return nc.alloc_sbuf_tensor_at(name, shape, dtype, offset=AB + off).ap()

    XT = [at(f"XT{c}", c * 4 * KB, [128, S], BF16) for c in range(8)]
    Vsb = at("Vsb", 32 * KB, [128, NT * D], BF16)
    wslot = at("wslot", 64 * KB, [128, 8 * D], BF16)
    Sstrip = [at(f"Sstrip{i}", 80 * KB + i * 8 * KB, [128, S], F32) for i in range(2)]
    PTring = at("PTring", 104 * KB, [128, 4 * 128], BF16)
    qpt = [at(f"qpt{m}", 105 * KB + m * 512, [128, 256], BF16) for m in range(8)]
    xres = [at(f"xres{i}", 109 * KB + i * 4 * KB, [128, D], F32) for i in range(2)]
    xbf = [at(f"xbf{i}", 117 * KB + i * 2 * KB, [128, D], BF16) for i in range(2)]
    Gfull = at("Gfull", 121 * KB, [128, 2048], F32)
    sqscrA = at("sqscrA", 129 * KB, [128, D], BF16)

    h1T = [at(f"h1T{c}", c * 4 * KB, [128, S], BF16) for c in range(8)]
    Vplus = at("Vplus", 32 * KB, [128, NT * G65], BF16)
    qkpT = [at(f"qkpT{i}", 65 * KB + i * 8 * KB, [128, 2 * S], BF16) for i in range(2)]
    negMrowA = at("negMrowA", 81 * KB, [65, S], BF16)
    qkvw = [at(f"qkvw{i}", 89 * KB + i * 4 * KB, [128, 2048], BF16) for i in range(2)]
    Ebuf2 = [at(f"Ebuf2_{i}", 97 * KB + i * KB, [128, 512], BF16) for i in range(3)]
    n1rep = at("n1rep", 100 * KB, [128, D], F32)
    bscr = at("bscr", 104 * KB, [128, D], F32)
    n1row = at("n1row", 108 * KB, [1, D], F32)
    vwslot = at("vwslot", 65 * KB, [128, 8 * D], BF16)
    rsrow = at("rsrow", 100 * KB, [1, 512], F32)
    rsb = at("rsb", 102 * KB, [64, 512], F32)
    todd = at("todd", 104 * KB, [64, 2 * 256], BF16)
    negrow = at("negrow", 105 * KB, [32, 128], BF16)
    mnegbf = at("mnegbf", 105 * KB + 512, [128, 32], F32)
    attnOT = [at(f"attnOT{c}", 106 * KB + c * 4 * KB, [128, S], BF16) for c in range(8)]
    dbgscr = at("dbgscr", 100 * KB, [128, 1024], F32)

    X2 = at("X2", 0, [128, NT * D], F32)
    owT = at("owT", 65 * KB, [128, 8 * D], BF16)
    n2rep = at("n2rep", 81 * KB, [128, D], F32)
    bscr2 = at("bscr2", 85 * KB, [128, D], F32)
    gbf = at("gbf", 89 * KB, [128, D], BF16)
    gateT_sb = at("gateT_sb", 91 * KB, [128, NE * 8], BF16)
    n2row = at("n2row", 92 * KB, [1, D], F32)
    gT = [at(f"gT{c}", 106 * KB + c * 4 * KB, [128, S], BF16) for c in range(8)]

    upw = [at(f"upw{i}", i * 16 * KB, [128, 8 * KB], BF16) for i in range(2)]
    dww = [at(f"dww{i}", 32 * KB + i * 16 * KB, [128, 8 * KB], BF16) for i in range(2)]
    siluT = [at(f"siluT{i}", 65 * KB + i * 8 * KB, [128, 8 * 512], BF16) for i in range(2)]
    lnscr = at("lnscr", 81 * KB, [128, D], BF16)
    outst = at("outst", 85 * KB, [128, 2 * D], F32)

    assert nc.sbuf_base <= AB, (nc.sbuf_base, AB)

    pA = nc.alloc_psum_tensor("pA", [128, 2048], F32).ap()
    pB = nc.alloc_psum_tensor("pB", [128, 1024], F32).ap()
    pC = nc.alloc_psum_tensor("pC", [128, 512], F32).ap()
    pT = nc.alloc_psum_tensor("pT", [128, 512], F32).ap()

    def a(i):
        return pA[:, i * 512:(i + 1) * 512]

    sG = Sem(nc, "sG")
    sS = Sem(nc, "sS")
    sO = Sem(nc, "sO")
    sPE = Sem(nc, "sPE")
    sACT = Sem(nc, "sACT")
    sDVE = Sem(nc, "sDVE")
    sCP = Sem(nc, "sCP")
    dX = [Sem(nc, "dX0"), Sem(nc, "dX1")]
    dQW = [Sem(nc, "dQW0"), Sem(nc, "dQW1")]
    dUW = [Sem(nc, "dUW0"), Sem(nc, "dUW1")]
    dTD = [Sem(nc, "dTD0"), Sem(nc, "dTD1")]
    dWS = Sem(nc, "dWS")

    pl = Plan(nc)
    A = pl.add

    A("sync", [], lambda e: e.dma_start(maskA, maskA_d), inc=sS, k=16)
    A("sync", [], lambda e: e.dma_start(maskB, maskB_d), inc=sS, k=16)
    A("sync", [], lambda e: e.dma_start(identbf, identbf_d), inc=sS, k=16)
    A("sync", [], lambda e: e.dma_start(identf, identf_d), inc=sS, k=16)
    A("sync", [], lambda e: e.dma_start(onescol, onescol_d), inc=sS, k=16)
    A("sync", [], lambda e: e.dma_start(onescolb, onescolb_d), inc=sS, k=16)
    A("sync", [], lambda e: e.dma_start(ones32, ones32_d), inc=sS, k=16)
    A("sync", [], lambda e: e.dma_start(slope_sb[0:1, 0:1], slope_d), inc=sS, k=16)
    A("vector", [], lambda e: e.memset(epsA[:, 0:1], 1e-5))
    A("vector", [], lambda e: e.memset(epsA[:, 1:2], 1e-6))
    A("vector", [], lambda e: e.memset(eps1[0:1, 0:1], 1e-38), inc=sDVE)
    A("tensor", [(sS, sS.n)],
      lambda e: e.matmul(pC[:, 0:1], onescol[0:1, 0:128], slope_sb[0:1, 0:1],
                         start=True, stop=True), inc=sPE)
    A("vector", [(sPE, sPE.n)], lambda e: e.tensor_copy(slope_bc, pC[:, 0:1]), inc=sDVE)
    A("vector", [(sS, sS.n)], lambda e: e.memset(epsA[:, 3:4], 0.0), inc=sDVE)
    pl.flush("setup")

    if upto == "setup":
        A("vector", [], lambda e: e.tensor_copy(outst[:, 0:1], slope_bc), inc=sDVE)
        A("vector", [], lambda e: e.tensor_copy(outst[:, 1:9], epsA), inc=sDVE)
        A("vector", [], lambda e: e.tensor_copy(outst[:, 9:137], maskA), inc=sDVE)
        A("sync", [(sDVE, sDVE.n)], lambda e: e.dma_start(dbg_d[:, 0:137], outst[:, 0:137]),
          inc=sO, k=16)
        pl.flush("dbgsetup")

    for s in range(NSEQ if upto != "setup" else 0):
        A("sync", [], lambda e: e.dma_start(Gfull, rampA_d), inc=sS, k=16)
        A("scalar", [(sS, sS.n), (sDVE, sDVE.n)],
          lambda e: e.activation(Gfull, Gfull, AF.Copy, bias=0.0, scale=slope_bc[:, 0:1]),
          inc=sACT)
        A("gpsimd", [],
          lambda e: e.dma_start(wslot.rearrange("p (c j) -> p c j", c=8),
                                wvT_d.rearrange("(c p) j -> p c j", p=128)),
          inc=dWS, k=16)
        wv_ready = dWS.n

        _tiles = int(os.environ.get("KDBG_TILES", NT))
        _nov = int(os.environ.get("KDBG_NOV", "0"))
        _notr = int(os.environ.get("KDBG_NOTR", "0"))
        trslot = [pA[:, 1024:1152], pA[:, 1536:1664]]
        tr_done = {0: 0, 1: 0}
        vv_free = {0: 0, 1: 0}
        for st in range(_tiles):
            w = [(sPE, tr_done[st % 2])] if st >= 2 else []
            A("gpsimd", w,
              lambda e, s=s, st=st: e.dma_start(xres[st % 2], x_d[s, st * 128:(st + 1) * 128, :]),
              inc=dX[st % 2], k=16)
            xw = dX[st % 2].n
            for c in range(0 if _notr else 8):
                wts = [(sCP, max(0, sCP.n - 1))]
                if c == 0:
                    wts.append((dX[st % 2], xw))
                A("tensor", wts,
                  lambda e, st=st, c=c: e.transpose(
                      trslot[c % 2], xres[st % 2][:, c * 128:(c + 1) * 128], identf),
                  inc=sPE)
                A("vector", [(sPE, sPE.n)],
                  lambda e, st=st, c=c: e.tensor_copy(
                      XT[c][:, st * 128:(st + 1) * 128], trslot[c % 2]),
                  inc=sCP)
            tr_done[st % 2] = sPE.n
            vbank = pB if st % 2 == 0 else pA[:, 0:1024]
            for c in range(0 if _nov else 8):
                wts = []
                if c == 0:
                    wts = [(sCP, sCP.n), (dWS, wv_ready)]
                    if st >= 2:
                        wts.append((sACT, vv_free[st % 2]))
                for half in range(2):
                    A("tensor", wts if half == 0 else [],
                      lambda e, c=c, half=half, st=st, vbank=vbank: e.matmul(
                          vbank[:, half * 512:(half + 1) * 512],
                          XT[c][:, st * 128:(st + 1) * 128],
                          wslot[:, c * D + half * 512: c * D + (half + 1) * 512],
                          start=(c == 0), stop=(c == 7)),
                      inc=(sPE if (c == 7 and half == 1) else None))
            if not _nov:
                A("scalar", [(sPE, sPE.n)],
                  lambda e, st=st, vbank=vbank: e.activation(Vsb[:, st * D:(st + 1) * D], vbank, AF.Copy),
                  inc=sACT)
                vv_free[st % 2] = sACT.n
        pl.flush(f"A01_s{s}")

        if upto == "A01" and s == 0:
            A("vector", [], lambda e: e.tensor_copy(outst[:, 0:1024], Vsb[:, 0:1024]), inc=sDVE)
            A("sync", [(sDVE, sDVE.n)], lambda e: e.dma_start(dbg_d[:, 0:1024], outst[:, 0:1024]),
              inc=sO, k=16)
            A("vector", [], lambda e: e.tensor_copy(outst[:, 1024:2048], XT[0][:, 0:1024]), inc=sDVE)
            A("sync", [(sDVE, sDVE.n)], lambda e: e.dma_start(dbg_d[:, 1024:2048], outst[:, 1024:2048]),
              inc=sO, k=16)
            A("sync", [(sACT, sACT.n)], lambda e: e.dma_start(dbg_d[:, 2048:4096], Gfull), inc=sO, k=16)
            pl.flush("dbgA01")
            break

        A("gpsimd", [],
          lambda e: e.dma_start(wslot.rearrange("p (c j) -> p c j", c=8),
                                wqk_d.rearrange("(c p) j -> p c j", p=128)),
          inc=dWS, k=16)
        wq_ready = dWS.n

        qpt_done = 0
        qp_free = {0: 0, 1: 0}
        sc_free = {0: 0, 1: 0}
        xr_free = {0: 0, 1: 0}
        yv_free = 0
        tr_fin = {0: 0, 1: 0}
        _qis = int(os.environ.get("KDBG_QIS", NT))
        for qi in range(_qis):
            kend = (qi + 1) * 128
            nkc = (kend + 511) // 512
            buf = qi % 2
            A("sync", [(sDVE, xr_free[qi % 2])] if qi >= 2 else [],
              lambda e, s=s, qi=qi: e.dma_start(xres[qi % 2], x_d[s, qi * 128:(qi + 1) * 128, :]),
              inc=dX[qi % 2], k=16)
            xres_rdy = dX[qi % 2].n

            if qi % 2 == 0:
                for m in range(8):
                    bank = a(0) if m % 2 == 0 else a(1)
                    for c in range(8):
                        wts = []
                        if c == 0:
                            wts = [(dWS, wq_ready), (sACT, qp_free[m % 2])]
                        A("tensor", wts,
                          lambda e, m=m, c=c, qi=qi, bank=bank: e.matmul(
                              bank[:, 0:256],
                              wslot[:, c * D + m * 128: c * D + (m + 1) * 128],
                              XT[c][:, qi * 128: qi * 128 + 256],
                              start=(c == 0), stop=(c == 7)),
                          inc=(sPE if c == 7 else None))
                    A("scalar", [(sPE, sPE.n)],
                      lambda e, m=m, bank=bank: e.activation(
                          qpt[m], bank[:, 0:256], AF.Copy, bias=0.0, scale=1.0 / 32.0),
                      inc=sACT)
                    qp_free[m % 2] = sACT.n
                qpt_done = sACT.n

            for kc in range(nkc):
                w = min(512, kend - kc * 512)
                bank = a(2) if kc % 2 == 0 else a(3)
                for c in range(8):
                    wts = []
                    if c == 0:
                        wts = [(sACT, qpt_done), (sDVE, sc_free[kc % 2])]
                    A("tensor", wts,
                      lambda e, c=c, kc=kc, qi=qi, w=w, bank=bank: e.matmul(
                          bank[:, 0:w],
                          qpt[c][:, (qi % 2) * 128:(qi % 2) * 128 + 128],
                          XT[c][:, kc * 512: kc * 512 + w],
                          start=(c == 0), stop=(c == 7)),
                      inc=(sPE if c == 7 else None))
                off = kc * 512 - qi * 128 + C0
                sw = [(sPE, sPE.n)]
                if kc == 0:
                    sw.append((sCP, tr_fin[qi % 2]))
                A("vector", sw,
                  lambda e, kc=kc, w=w, off=off, buf=buf, bank=bank: e.tensor_add(
                      Sstrip[buf][:, kc * 512: kc * 512 + w], bank[:, 0:w],
                      Gfull[:, off: off + w]),
                  inc=sDVE)
                sc_free[kc % 2] = sDVE.n
                if kc == nkc - 1:
                    A("vector", [],
                      lambda e, qi=qi, buf=buf: e.tensor_add(
                          Sstrip[buf][:, qi * 128: qi * 128 + 128],
                          Sstrip[buf][:, qi * 128: qi * 128 + 128], maskA),
                      inc=sDVE)
            A("vector", [(sDVE, sDVE.n)],
              lambda e, buf=buf, kend=kend: e.reduce_max(
                  stat[:, 4:5], Sstrip[buf][:, 0:kend], axis=AX.X, negate=True),
              inc=sDVE)
            A("scalar", [(sDVE, sDVE.n)],
              lambda e, buf=buf, kend=kend: e.activation(
                  Sstrip[buf][:, 0:kend], Sstrip[buf][:, 0:kend], AF.Exp,
                  bias=stat[:, 4:5], accum_out=stat[:, 5:6]),
              inc=sACT)
            exp_done = sACT.n
            A("vector", [(sACT, exp_done)],
              lambda e: e.reciprocal(stat[:, 6:7], stat[:, 5:6]), inc=sDVE)

            ntk = qi + 1
            cps = []

            def emit_tr(kt, buf=buf):
                A("tensor", [(sACT, exp_done), (sCP, sCP.n)],
                  lambda e, kt=kt, buf=buf: e.transpose(
                      pT[:, (kt % 4) * 128:(kt % 4 + 1) * 128],
                      Sstrip[buf][:, kt * 128:(kt + 1) * 128], identf),
                  inc=sPE)
                A("vector", [(sPE, sPE.n)],
                  lambda e, kt=kt: e.tensor_copy(
                      PTring[:, (kt % 4) * 128:(kt % 4 + 1) * 128],
                      pT[:, (kt % 4) * 128:(kt % 4 + 1) * 128]),
                  inc=sCP)
                cps.append(sCP.n)

            def emit_av(kt, qi=qi):
                for half in range(2):
                    wts = []
                    if half == 0:
                        wts = [(sCP, cps[kt])]
                        if kt == 0:
                            wts.append((sDVE, yv_free))
                    A("tensor", wts,
                      lambda e, kt=kt, half=half, qi=qi: e.matmul(
                          pB[:, half * 512:(half + 1) * 512],
                          PTring[:, (kt % 4) * 128:(kt % 4 + 1) * 128],
                          Vsb[:, kt * D + half * 512: kt * D + (half + 1) * 512],
                          start=(kt == 0), stop=(kt == qi)),
                      inc=(sPE if (kt == qi and half == 1) else None))

            for kt in range(ntk):
                emit_tr(kt)
                if kt >= 1:
                    emit_av(kt - 1)
            emit_av(ntk - 1)
            tr_fin[qi % 2] = cps[-1]
            av_done = sPE.n

            A("vector", [(sPE, av_done), (dX[qi % 2], xres_rdy)],
              lambda e, qi=qi: e.scalar_tensor_tensor(
                  xres[qi % 2], pB, stat[:, 6:7], xres[qi % 2], op0=ALU.mult, op1=ALU.add),
              inc=sDVE)
            yv_free = sDVE.n
            ydone = sDVE.n
            A("scalar", [(sDVE, ydone)],
              lambda e, qi=qi: e.activation(sqscrA, xres[qi % 2], AF.Square, accum_out=stat[:, 8:9]),
              inc=sACT)
            A("vector", [],
              lambda e, qi=qi: e.reduce_sum(stat[:, 7:8], xres[qi % 2], axis=AX.X), inc=sDVE)
            A("vector", [(sDVE, sDVE.n)],
              lambda e: e.tensor_scalar(stat[:, 9:10], stat[:, 7:8], 1.0 / D, None, op0=ALU.mult),
              inc=sDVE)
            A("vector", [(sACT, sACT.n), (sDVE, sDVE.n)],
              lambda e: e.tensor_scalar(stat[:, 10:11], stat[:, 8:9], 1.0 / D, None, op0=ALU.mult),
              inc=sDVE)
            A("vector", [(sDVE, sDVE.n)],
              lambda e: e.tensor_mul(stat[:, 11:12], stat[:, 9:10], stat[:, 9:10]),
              inc=sDVE)
            A("vector", [(sDVE, sDVE.n)],
              lambda e: e.tensor_sub(stat[:, 12:13], stat[:, 10:11], stat[:, 11:12]),
              inc=sDVE)
            A("scalar", [(sDVE, sDVE.n)],
              lambda e: e.activation(stat[:, 13:14], stat[:, 12:13], AF.Ln, bias=epsA[:, 0:1]))
            A("scalar", [],
              lambda e: e.activation(stat[:, 14:15], stat[:, 13:14], AF.Exp, bias=0.0, scale=-0.5),
              inc=sACT)
            A("vector", [(sACT, sACT.n), (sDVE, sDVE.n)],
              lambda e, qi=qi: e.tensor_scalar(
                  X1R[:, qi * D:(qi + 1) * D], xres[qi % 2],
                  stat[:, 9:10], stat[:, 14:15], op0=ALU.subtract, op1=ALU.mult),
              inc=sDVE)
            xr_free[qi % 2] = sDVE.n
        pl.flush(f"A2_s{s}")

        if upto == "A2d" and s == 0:
            A("sync", [(sDVE, sDVE.n), (sACT, sACT.n)],
              lambda e: e.dma_start(dbg_d[:, 0:2048], Sstrip[0]), inc=sO, k=16)
            A("vector", [(sO, sO.n)], lambda e: e.tensor_copy(Sstrip[1], Sstrip[0]), inc=sDVE)
            A("vector", [], lambda e: e.tensor_copy(Sstrip[1][:, 0:64], stat), inc=sDVE)
            A("sync", [(sDVE, sDVE.n)], lambda e: e.dma_start(dbg_d[:, 2048:4096], Sstrip[1]), inc=sO, k=16)
            pl.flush("dbgA2")
            break

        if upto == "X1":
            for st in range(NT):
                A("sync", [(sDVE, sDVE.n)],
                  lambda e, s=s, st=st: e.dma_start(
                      y_d[s, st * 128:(st + 1) * 128, :], X1R[:, st * D:(st + 1) * D]),
                  inc=sO, k=16)
            A("vector", [(sO, sO.n)], lambda e: e.memset(epsA[:, 4:5], 0.0), inc=sDVE)
            pl.flush(f"dbgX1_s{s}")
            continue

        A("sync", [], lambda e: e.dma_start(n1row, n1_d), inc=sS, k=16)
        n1_rdy = sS.n
        A("tensor", [(sS, n1_rdy)],
          lambda e: e.matmul(pC, onescol[0:1, 0:128], n1row[0:1, 0:512], start=True, stop=True))
        A("tensor", [],
          lambda e: e.matmul(a(0), onescol[0:1, 0:128], n1row[0:1, 512:1024], start=True, stop=True),
          inc=sPE)
        A("vector", [(sPE, sPE.n)], lambda e: e.tensor_copy(n1rep[:, 0:512], pC), inc=sDVE)
        A("vector", [], lambda e: e.tensor_copy(n1rep[:, 512:1024], a(0)), inc=sDVE)
        n1cp = sDVE.n
        h1ts = []

        for st in range(NT):
            A("scalar", [(sDVE, sDVE.n)],
              lambda e, st=st: e.activation(
                  gbf, X1R[:, st * D:(st + 1) * D], AF.Square, accum_out=stat[:, 16:17]),
              inc=sACT)
            A("scalar", [],
              lambda e: e.activation(stat[:, 17:18], stat[:, 16:17], AF.Ln,
                                     bias=epsA[:, 1:2], scale=1.0 / D))
            A("scalar", [],
              lambda e: e.activation(stat[:, 18:19], stat[:, 17:18], AF.Exp, bias=0.0, scale=-0.5),
              inc=sACT)
            A("vector", [(sACT, sACT.n)],
              lambda e, st=st: e.tensor_scalar(
                  bscr, X1R[:, st * D:(st + 1) * D], stat[:, 18:19], None, op0=ALU.mult),
              inc=sDVE)
            A("vector", [], lambda e: e.tensor_mul(bscr, bscr, n1rep), inc=sDVE)
            h1done = sDVE.n
            trslot = [pA[:, 1024:1152], pA[:, 1536:1664]]
            for c in range(8):
                wts = [(sCP, max(0, sCP.n - 1))]
                if c == 0:
                    wts.append((sDVE, h1done))
                A("tensor", wts,
                  lambda e, c=c, trslot=trslot: e.transpose(
                      trslot[c % 2], bscr[:, c * 128:(c + 1) * 128], identf),
                  inc=sPE)
                A("vector", [(sPE, sPE.n)],
                  lambda e, st=st, c=c, trslot=trslot: e.tensor_copy(
                      h1T[c][:, st * 128:(st + 1) * 128], trslot[c % 2]),
                  inc=sCP)
            h1ts.append(sCP.n)

        A("gpsimd", [],
          lambda e: e.dma_start(vwslot.rearrange("p (c j) -> p c j", c=8),
                                qkvT_d[:, 2 * D:3 * D].rearrange("(c p) j -> p c j", p=128)),
          inc=dWS, k=16)
        vw_rdy = dWS.n
        b1_free = {0: 0, 1: 0}
        for st in range(NT):
            vbank = pB if st % 2 == 0 else pA[:, 0:1024]
            for c in range(8):
                wts = []
                if c == 0:
                    wts = [(dWS, vw_rdy), (sCP, h1ts[st])]
                    if st == 1:
                        wts.append((sDVE, n1cp))
                    if st >= 2:
                        wts.append((sDVE, b1_free[st % 2]))
                for half in range(2):
                    A("tensor", wts if half == 0 else [],
                      lambda e, c=c, half=half, st=st, vbank=vbank: e.matmul(
                          vbank[:, half * 512:(half + 1) * 512],
                          h1T[c][:, st * 128:(st + 1) * 128],
                          vwslot[:, c * D + half * 512: c * D + (half + 1) * 512],
                          start=(c == 0), stop=(c == 7)),
                      inc=(sPE if (c == 7 and half == 1) else None))
            A("vector", [(sPE, sPE.n)],
              lambda e, st=st, vbank=vbank: e.tensor_copy(
                  Vplus[:, st * G65:(st + 1) * G65].rearrange("p (g w) -> p g w", w=65)[:, :, 0:64],
                  vbank.rearrange("p (g w) -> p g w", w=64)),
              inc=sDVE)
            A("vector", [],
              lambda e, st=st: e.memset(
                  Vplus[:, st * G65:(st + 1) * G65].rearrange("p (g w) -> p g w", w=65)[:, :, 64:65],
                  1.0),
              inc=sDVE)
            b1_free[st % 2] = sDVE.n
        pl.flush(f"B1_s{s}")

        ebuf_ready = [sPE.n, sPE.n, sPE.n]
        stb_free = {0: 0, 1: 0}
        av_free = {0: 0, 1: 0}
        todd_free = {0: 0, 1: 0}
        ps = {"qkw_free": {0: 0, 1: 0}, "qkpT_free": {0: 0, 1: 0},
              "pc_w": [], "pt_w": [], "evac": {}, "wrdy": {}, "ops": {}}

        def emit_qkdma(r):
            pb = r % 2
            for c in range(8):
                wts = [(sPE, ps["qkw_free"][pb])] if c == 0 else []
                A("gpsimd", wts,
                  lambda e, c=c, r=r, pb=pb: e.dma_start(
                      qkvw[pb][:, c * 256: c * 256 + 128],
                      qkvT_d[c * 128:(c + 1) * 128, r * 128:(r + 1) * 128]),
                  inc=dQW[pb], k=16)
                A("gpsimd", [],
                  lambda e, c=c, r=r, pb=pb: e.dma_start(
                      qkvw[pb][:, c * 256 + 128: c * 256 + 256],
                      qkvT_d[c * 128:(c + 1) * 128, D + r * 128: D + (r + 1) * 128]),
                  inc=dQW[pb], k=16)
            ps["wrdy"][r] = dQW[pb].n

        def emit_qkgrp(r, tgt, qq):
            pb = r % 2
            for c in range(8):
                wts = ([(dQW[pb], ps["wrdy"][r])] + ps["pc_w"]) if c == 0 else []
                A("tensor", wts,
                  lambda e, c=c, tgt=tgt, qq=qq, pb=pb: e.matmul(
                      pC[:, 0:512],
                      qkvw[pb][:, c * 256 + tgt * 128: c * 256 + tgt * 128 + 128],
                      h1T[c][:, qq * 512:(qq + 1) * 512],
                      start=(c == 0), stop=(c == 7)),
                  inc=(sPE if c == 7 else None))
            ps["qkw_free"][pb] = sPE.n
            A("scalar", [(sPE, sPE.n), (sPE, ps["qkpT_free"][pb])],
              lambda e, tgt=tgt, qq=qq, pb=pb: e.activation(
                  qkpT[pb][:, tgt * S + qq * 512: tgt * S + (qq + 1) * 512],
                  pC[:, 0:512], AF.Copy, bias=0.0, scale=(0.125 if tgt == 0 else 1.0)),
              inc=sACT)
            ps["evac"][(pb, tgt, qq)] = sACT.n
            ps["pc_w"] = [(sACT, sACT.n)]

        def emit_tmchunk(r, qt, kc):
            pb = r % 2
            kend2 = (qt + 1) * 128
            nkc2 = (kend2 + 511) // 512
            w2 = min(512, kend2 - kc * 512)
            wts = [(sACT, ps["evac"][(pb, 0, qt // 4)]),
                   (sACT, ps["evac"][(pb, 1, kc)])] + ps["pc_w"]
            A("tensor", wts,
              lambda e, qt=qt, kc=kc, w2=w2, pb=pb: e.matmul(
                  pC[:, 0:w2],
                  qkpT[pb][0:64, qt * 128:(qt + 1) * 128],
                  qkpT[pb][0:64, S + kc * 512: S + kc * 512 + w2],
                  start=True, stop=True, tile_position=(0, 0)))
            A("tensor", list(ps["pt_w"]),
              lambda e, qt=qt, kc=kc, w2=w2, pb=pb: e.matmul(
                  pT[:, 0:w2],
                  qkpT[pb][64:128, qt * 128:(qt + 1) * 128],
                  qkpT[pb][64:128, S + kc * 512: S + kc * 512 + w2],
                  start=True, stop=True, tile_position=(64, 0)),
              inc=sPE)
            mm2 = sPE.n
            if kc == nkc2 - 1:
                doff = qt * 128 - kc * 512
                A("vector", [(sPE, mm2)],
                  lambda e, doff=doff: e.tensor_add(
                      pC[:, doff:doff + 128], pC[:, doff:doff + 128], maskA),
                  inc=sDVE)
                A("vector", [],
                  lambda e, doff=doff: e.tensor_add(
                      pT[:, doff:doff + 128], pT[:, doff:doff + 128], maskA),
                  inc=sDVE)
            A("vector", [(sPE, mm2), (sDVE, sDVE.n)],
              lambda e, w2=w2: e.reduce_max(stat[:, 32:33], pC[:, 0:w2], axis=AX.X),
              inc=sDVE)
            ps["pc_w"] = [(sDVE, sDVE.n)]
            A("vector", [],
              lambda e, w2=w2: e.reduce_max(stat[:, 33:34], pT[:, 0:w2], axis=AX.X),
              inc=sDVE)
            ps["pt_w"] = [(sDVE, sDVE.n)]
            if kc == 0:
                A("vector", [(sDVE, sDVE.n)],
                  lambda e: e.tensor_copy(stat[:, 34:36], stat[:, 32:34]), inc=sDVE)
            else:
                A("vector", [(sDVE, sDVE.n)],
                  lambda e: e.tensor_max(stat[:, 34:36], stat[:, 34:36], stat[:, 32:34]),
                  inc=sDVE)
            if kc == nkc2 - 1:
                for hh in range(2):
                    mc = hh * 16 + qt
                    A("vector", [(sDVE, sDVE.n)],
                      lambda e, mc=mc, hh=hh: e.tensor_scalar(
                          mnegbf[:, mc: mc + 1],
                          stat[:, 34 + hh:35 + hh], -1.0, None, op0=ALU.mult),
                      inc=sDVE)

        def emit_dance(r):
            A("tensor", [(sDVE, sDVE.n)] + list(ps["pt_w"]),
              lambda e: e.transpose(pT[0:32, 0:128], mnegbf[:, 0:32], identf),
              inc=sPE)
            tpn = sPE.n
            cw = [(sPE, tpn)]
            if r >= 1:
                cw.append((sS, ps["ops"][r - 1]))
            A("vector", cw,
              lambda e: e.tensor_copy(negrow, pT[0:32, 0:128]), inc=sDVE)
            ps["pt_w"] = [(sDVE, sDVE.n)]
            A("sync", [(sDVE, sDVE.n)],
              lambda e: e.dma_start(negMrowA[0:1, 0:S], negrow[0:16, :]),
              inc=sS, k=16)
            A("sync", [],
              lambda e: e.dma_start(negMrowA[64:65, 0:S], negrow[16:32, :]),
              inc=sS, k=16)
            ps["ops"][r] = sS.n

        def emit_stav(pp, units):
            pb = pp % 2
            ops_rdy = ps["ops"][pp]
            uq = list(units)
            tot = len(uq)
            nslots = 72
            sl = [0]

            def pump():
                sl[0] += 1
                while uq and (tot - len(uq)) * nslots < sl[0] * tot:
                    uq.pop(0)()

            for qc in range(8):
                ntk = 2 * qc + 2
                avb = pB[:, 0:512] if qc % 2 == 0 else pB[:, 512:1024]

                def emit_sc(kt, qc=qc):
                    par = kt % 2
                    half = (kt == 2 * qc + 1)
                    qlo = 128 if half else 0
                    sb = [a(0), a(1)] if par == 0 else [a(2), a(3)]
                    for hh in range(2):
                        wts = [(sS, ops_rdy), (sACT, stb_free[par])] if hh == 0 else []
                        A("tensor", wts,
                          lambda e, hh=hh, kt=kt, qc=qc, qlo=qlo, sb=sb: e.matmul(
                              sb[hh][:, qlo:256],
                              qkpT[pb][hh * 64:(hh + 1) * 64, S + kt * 128:S + (kt + 1) * 128],
                              qkpT[pb][hh * 64:(hh + 1) * 64, qc * 256 + qlo:(qc + 1) * 256],
                              start=True, stop=False,
                              tile_position=(hh * 64, 0)))
                    for hh in range(2):
                        A("tensor", [],
                          lambda e, hh=hh, qc=qc, qlo=qlo, sb=sb: e.matmul(
                              sb[hh][:, qlo:256],
                              onescolb[hh * 64:hh * 64 + 1, 0:128],
                              negMrowA[hh * 64:hh * 64 + 1, qc * 256 + qlo:(qc + 1) * 256],
                              start=False, stop=True,
                              tile_position=(hh * 64, 0)),
                          inc=(sPE if hh == 1 else None))
                    sc_done = sPE.n
                    nmask = 0
                    if kt == 2 * qc:
                        for hh in range(2):
                            A("vector", [(sPE, sc_done)] if hh == 0 else [],
                              lambda e, hh=hh, sb=sb: e.tensor_add(
                                  sb[hh][:, 0:128], sb[hh][:, 0:128], maskB),
                              inc=sDVE)
                        nmask = 2
                    elif half:
                        for hh in range(2):
                            A("vector", [(sPE, sc_done)] if hh == 0 else [],
                              lambda e, hh=hh, sb=sb: e.tensor_add(
                                  sb[hh][:, 128:256], sb[hh][:, 128:256], maskB),
                              inc=sDVE)
                        nmask = 2
                    msk_done = sDVE.n
                    ewts = [(sPE, sc_done), (sPE, ebuf_ready[kt % 3])]
                    if nmask:
                        ewts.append((sDVE, msk_done))
                    for hh in range(2):
                        A("scalar", ewts if hh == 0 else [],
                          lambda e, hh=hh, kt=kt, qlo=qlo, sb=sb: e.activation(
                              Ebuf2[kt % 3][:, hh * 256 + qlo:(hh + 1) * 256],
                              sb[hh][:, qlo:256], AF.Exp),
                          inc=(sACT if hh == 1 else None))
                    return sACT.n

                def emit_avb(kt, qc=qc, exp_n=0, ntk=ntk, avb=avb):
                    half = (kt == 2 * qc + 1)
                    qlo = 128 if half else 0
                    for hh in range(2):
                        wts = []
                        if hh == 0:
                            wts = [(sACT, exp_n)]
                            if kt == 0:
                                wts.append((sDVE, av_free[qc % 2]))
                        A("tensor", wts,
                          lambda e, hh=hh, kt=kt, qc=qc, pp=pp, ntk=ntk, qlo=qlo, avb=avb: e.matmul(
                              avb[0:65, hh * 256 + qlo:(hh + 1) * 256],
                              Vplus[:, kt * G65 + (2 * pp + hh) * 65: kt * G65 + (2 * pp + hh) * 65 + 65],
                              Ebuf2[kt % 3][:, hh * 256 + qlo:(hh + 1) * 256],
                              start=(kt == 0 and hh == 0), stop=(kt == ntk - 1 and hh == 1)),
                          inc=(sPE if hh == 1 else None))
                    ebuf_ready[kt % 3] = sPE.n

                exps = []
                for kt in range(ntk):
                    if kt >= 2:
                        stb_free[kt % 2] = exps[kt - 2]
                    exps.append(emit_sc(kt))
                    pump()
                    if kt >= 1:
                        emit_avb(kt - 1, exp_n=exps[kt - 1])
                emit_avb(ntk - 1, exp_n=exps[ntk - 1])
                av_fin = sPE.n
                stb_free[(ntk - 2) % 2] = exps[ntk - 2]
                stb_free[(ntk - 1) % 2] = exps[ntk - 1]

                A("scalar", [(sPE, av_fin)],
                  lambda e, avb=avb: e.activation(
                      rsrow[0:1, 0:512], avb[64:65, 0:512],
                      AF.Ln, bias=eps1[0:1, 0:1]))
                A("scalar", [],
                  lambda e: e.activation(
                      rsrow[0:1, 0:512], rsrow[0:1, 0:512],
                      AF.Exp, bias=0.0, scale=-1.0),
                  inc=sACT)
                rs_done = sACT.n
                A("tensor", [(sACT, rs_done)],
                  lambda e, avb=avb: e.matmul(avb[64:128, 0:512], onescol[0:1, 0:64],
                                              rsrow, start=True, stop=True,
                                              tile_position=(0, 64)),
                  inc=sPE)
                bc_done = sPE.n
                A("vector", [(sPE, bc_done)],
                  lambda e, avb=avb: e.tensor_copy(rsb, avb[64:128, 0:512]),
                  inc=sDVE)
                A("vector", [],
                  lambda e, pp=pp, qc=qc, avb=avb: e.tensor_mul(
                      attnOT[pp][0:64, qc * 256:(qc + 1) * 256],
                      avb[0:64, 0:256], rsb[0:64, 0:256]),
                  inc=sDVE)
                A("vector", [(dTD[qc % 2], todd_free[qc % 2])],
                  lambda e, qc=qc, avb=avb: e.tensor_mul(
                      todd[0:64, (qc % 2) * 256:(qc % 2) * 256 + 256],
                      avb[0:64, 256:512], rsb[0:64, 256:512]),
                  inc=sDVE)
                av_free[qc % 2] = sDVE.n
                A("sync", [(sDVE, sDVE.n)],
                  lambda e, pp=pp, qc=qc: e.dma_start(
                      attnOT[pp][64:128, qc * 256:(qc + 1) * 256],
                      todd[0:64, (qc % 2) * 256:(qc % 2) * 256 + 256]),
                  inc=dTD[qc % 2], k=16)
                todd_free[qc % 2] = dTD[qc % 2].n
            while uq:
                uq.pop(0)()
            ps["qkpT_free"][pb] = sPE.n

        for r in range(NP + 1):
            units = []
            if r < NP:
                emit_qkdma(r)
                for qq in range(4):
                    units.append(lambda r=r, qq=qq: emit_qkgrp(r, 0, qq))
                    units.append(lambda r=r, qq=qq: emit_qkgrp(r, 1, qq))
                for qt in range(NT):
                    for kc in range((qt // 4) + 1):
                        units.append(lambda r=r, qt=qt, kc=kc: emit_tmchunk(r, qt, kc))
            if r == 0:
                for u in units:
                    u()
            else:
                emit_stav(r - 1, units)
            if r < NP:
                emit_dance(r)
        pl.flush(f"B23_s{s}")

        if upto == "B23d" and s == 0:
            for i in range(2):
                A("vector", [(sO, sO.n)],
                  lambda e, i=i: e.tensor_copy(dbgscr, attnOT[0][:, i * 1024:(i + 1) * 1024]),
                  inc=sDVE)
                A("sync", [(sDVE, sDVE.n)],
                  lambda e, i=i: e.dma_start(dbg_d[:, i * 1024:(i + 1) * 1024], dbgscr), inc=sO, k=16)
            A("vector", [(sO, sO.n)], lambda e: e.tensor_copy(dbgscr[:, 0:32], mnegbf), inc=sDVE)
            A("sync", [(sDVE, sDVE.n)], lambda e: e.dma_start(dbg_d[:, 2048:2112], dbgscr[:, 0:64]),
              inc=sO, k=16)
            A("vector", [(sO, sO.n)], lambda e: e.tensor_copy(dbgscr[:, 0:256], h1T[0][:, 0:256]),
              inc=sDVE)
            A("sync", [(sDVE, sDVE.n)], lambda e: e.dma_start(dbg_d[:, 2304:2560], dbgscr[:, 0:256]),
              inc=sO, k=16)
            A("vector", [(sO, sO.n)], lambda e: e.tensor_copy(dbgscr[:, 0:1024], Vplus[:, 0:1024]),
              inc=sDVE)
            A("sync", [(sDVE, sDVE.n)], lambda e: e.dma_start(dbg_d[:, 2560:3584], dbgscr[:, 0:1024]),
              inc=sO, k=16)
            A("vector", [(sO, sO.n)], lambda e: e.tensor_copy(dbgscr[:, 0:1024], qkpT[0][:, 0:1024]),
              inc=sDVE)
            A("sync", [(sDVE, sDVE.n)], lambda e: e.dma_start(dbg_d[:, 3072:4096], dbgscr[:, 0:1024]),
              inc=sO, k=16)
            pl.flush("dbgB23")
            break

        A("gpsimd", [],
          lambda e: e.dma_start(owT.rearrange("p (c j) -> p c j", c=8),
                                owT_d.rearrange("(c p) j -> p c j", p=128)),
          inc=sG, k=16)
        ow_rdy = sG.n
        attn_fin = [(sS, sS.n), (sDVE, sDVE.n), (dTD[0], dTD[0].n), (dTD[1], dTD[1].n)]
        b4_free = {0: 0, 1: 0}
        for st in range(NT):
            bank = pB if st % 2 == 0 else pA[:, 0:1024]
            for c in range(8):
                wts = []
                if c == 0:
                    wts = [(sG, ow_rdy)] + attn_fin + [(sDVE, b4_free[st % 2])]
                for half in range(2):
                    A("tensor", wts if half == 0 else [],
                      lambda e, c=c, half=half, st=st, bank=bank: e.matmul(
                          bank[:, half * 512:(half + 1) * 512],
                          attnOT[c][:, st * 128:(st + 1) * 128],
                          owT[:, c * D + half * 512: c * D + (half + 1) * 512],
                          start=(c == 0), stop=(c == 7)),
                      inc=(sPE if (c == 7 and half == 1) else None))
            A("vector", [(sPE, sPE.n)],
              lambda e, st=st, bank=bank: e.tensor_add(
                  X2[:, st * D:(st + 1) * D], bank, X1R[:, st * D:(st + 1) * D]),
              inc=sDVE)
            A("vector", [],
              lambda e, st=st: e.tensor_add(
                  X1R[:, st * D:(st + 1) * D], X2[:, st * D:(st + 1) * D],
                  X1R[:, st * D:(st + 1) * D]),
              inc=sDVE)
            b4_free[st % 2] = sDVE.n
        pl.flush(f"B4_s{s}")

        if upto == "X2":
            for st in range(NT):
                A("sync", [(sDVE, sDVE.n)],
                  lambda e, s=s, st=st: e.dma_start(
                      y_d[s, st * 128:(st + 1) * 128, :], X2[:, st * D:(st + 1) * D]),
                  inc=sO, k=16)
            A("vector", [(sO, sO.n)], lambda e: e.memset(epsA[:, 4:5], 0.0), inc=sDVE)
            pl.flush(f"dbgX2_s{s}")
            continue

        A("sync", [], lambda e: e.dma_start(n2row, n2_d), inc=sS, k=16)
        n2_rdy = sS.n
        A("tensor", [(sS, n2_rdy)],
          lambda e: e.matmul(pC, onescol[0:1, 0:128], n2row[0:1, 0:512], start=True, stop=True))
        A("tensor", [],
          lambda e: e.matmul(a(0), onescol[0:1, 0:128], n2row[0:1, 512:1024], start=True, stop=True),
          inc=sPE)
        A("vector", [(sPE, sPE.n)], lambda e: e.tensor_copy(n2rep[:, 0:512], pC), inc=sDVE)
        A("vector", [], lambda e: e.tensor_copy(n2rep[:, 512:1024], a(0)), inc=sDVE)
        n2cp = sDVE.n
        gts = []
        fdone = []
        gd = []
        sqa = []
        A("gpsimd", [],
          lambda e: e.dma_start(gateT_sb.rearrange("p (c j) -> p c j", c=8),
                                gateT_d.rearrange("(c p) j -> p c j", p=128)),
          inc=sG, k=16)
        gate_rdy = sG.n
        for st in range(NT):
            A("scalar", [(sDVE, sDVE.n)],
              lambda e, st=st: e.activation(
                  gbf, X2[:, st * D:(st + 1) * D], AF.Square, accum_out=stat[:, 20:21]),
              inc=sACT)
            A("scalar", [],
              lambda e: e.activation(stat[:, 21:22], stat[:, 20:21], AF.Ln,
                                     bias=epsA[:, 1:2], scale=1.0 / D))
            A("scalar", [],
              lambda e: e.activation(stat[:, 22:23], stat[:, 21:22], AF.Exp, bias=0.0, scale=-0.5),
              inc=sACT)
            sqa.append(sACT.n)
            A("vector", [(sACT, sACT.n)],
              lambda e, st=st: e.tensor_scalar(
                  bscr2, X2[:, st * D:(st + 1) * D], stat[:, 22:23], None, op0=ALU.mult),
              inc=sDVE)
            A("vector", [], lambda e: e.tensor_mul(bscr2, bscr2, n2rep), inc=sDVE)
            gdone = sDVE.n
            gd.append(gdone)
            trslot = [pA[:, 1024:1152], pA[:, 1536:1664]]
            for c in range(8):
                wts = [(sCP, max(0, sCP.n - 1))]
                if c == 0:
                    wts.append((sDVE, gdone))
                A("tensor", wts,
                  lambda e, c=c, trslot=trslot: e.transpose(
                      trslot[c % 2], bscr2[:, c * 128:(c + 1) * 128], identf),
                  inc=sPE)
                A("vector", [(sPE, sPE.n)],
                  lambda e, st=st, c=c, trslot=trslot: e.tensor_copy(
                      gT[c][:, st * 128:(st + 1) * 128], trslot[c % 2]),
                  inc=sCP)
            gt_done = sCP.n
            gts.append(gt_done)
            for c in range(8):
                wts = [(sCP, gt_done), (sG, gate_rdy), (sDVE, sDVE.n)] if c == 0 else []
                A("tensor", wts,
                  lambda e, c=c, st=st: e.matmul(
                      pC[:, 0:NE], gT[c][:, st * 128:(st + 1) * 128],
                      gateT_sb[:, c * NE:(c + 1) * NE], start=(c == 0), stop=(c == 7)),
                  inc=(sPE if c == 7 else None))
            A("vector", [(sPE, sPE.n)],
              lambda e: e.reduce_max(stat[:, 24:25], pC[:, 0:NE], axis=AX.X, negate=True),
              inc=sDVE)
            A("scalar", [(sDVE, sDVE.n)],
              lambda e: e.activation(stat[:, 32:40], pC[:, 0:NE], AF.Exp,
                                     bias=stat[:, 24:25], accum_out=stat[:, 25:26]),
              inc=sACT)
            A("vector", [(sACT, sACT.n)],
              lambda e: e.reciprocal(stat[:, 26:27], stat[:, 25:26]), inc=sDVE)
            A("vector", [(sDVE, sDVE.n)],
              lambda e: e.tensor_scalar(stat[:, 32:40], stat[:, 32:40], stat[:, 26:27], None,
                                        op0=ALU.mult),
              inc=sDVE)
            A("vector", [(sDVE, sDVE.n)], lambda e: e.reduce_max(stat[:, 27:28], stat[:, 32:40], axis=AX.X),
              inc=sDVE)
            A("vector", [(sDVE, sDVE.n)],
              lambda e: e.tensor_scalar(stat[:, 40:48], stat[:, 32:40], stat[:, 27:28], None,
                                        op0=ALU.subtract),
              inc=sDVE)
            A("vector", [(sDVE, sDVE.n)],
              lambda e: e.tensor_scalar(stat[:, 40:48], stat[:, 40:48], -1e30, 1.0,
                                        op0=ALU.mult, op1=ALU.min),
              inc=sDVE)
            A("vector", [(sDVE, sDVE.n)],
              lambda e: e.tensor_scalar(stat[:, 40:48], stat[:, 40:48], 1.0, 1e30,
                                        op0=ALU.subtract, op1=ALU.mult),
              inc=sDVE)
            A("vector", [(sDVE, sDVE.n)], lambda e: e.tensor_add(stat[:, 40:48], stat[:, 32:40], stat[:, 40:48]),
              inc=sDVE)
            A("vector", [(sDVE, sDVE.n)], lambda e: e.reduce_max(stat[:, 28:29], stat[:, 40:48], axis=AX.X),
              inc=sDVE)
            A("vector", [(sDVE, sDVE.n)], lambda e: e.tensor_add(stat[:, 29:30], stat[:, 27:28], stat[:, 28:29]),
              inc=sDVE)
            A("vector", [(sDVE, sDVE.n)],
              lambda e: e.tensor_scalar(stat[:, 30:31], stat[:, 29:30], 1e-9, None, op0=ALU.add),
              inc=sDVE)
            A("vector", [(sDVE, sDVE.n)], lambda e: e.reciprocal(stat[:, 31:32], stat[:, 30:31]), inc=sDVE)
            A("vector", [(sDVE, sDVE.n)],
              lambda e, st=st: e.tensor_mul(factor[:, st:st + 1], stat[:, 29:30], stat[:, 31:32]),
              inc=sDVE)
            fdone.append(sDVE.n)
        b5cp = sCP.n

        ostore_free = {0: dX[0].n, 1: dX[1].n}

        def emit_b7(st):
            A("scalar", [(sDVE, sDVE.n)],
              lambda e, st=st: e.activation(
                  lnscr, X1R[:, st * D:(st + 1) * D], AF.Square, accum_out=stat[:, 50:51]),
              inc=sACT)
            A("vector", [(sACT, sACT.n)],
              lambda e, st=st: e.reduce_sum(stat[:, 49:50], X1R[:, st * D:(st + 1) * D], axis=AX.X),
              inc=sDVE)
            A("vector", [(sDVE, sDVE.n)],
              lambda e: e.tensor_scalar(stat[:, 51:52], stat[:, 49:50], 1.0 / D, None, op0=ALU.mult),
              inc=sDVE)
            A("vector", [(sDVE, sDVE.n)],
              lambda e: e.tensor_scalar(stat[:, 52:53], stat[:, 50:51], 1.0 / D, None, op0=ALU.mult),
              inc=sDVE)
            A("vector", [(sDVE, sDVE.n)], lambda e: e.tensor_mul(stat[:, 53:54], stat[:, 51:52], stat[:, 51:52]),
              inc=sDVE)
            A("vector", [(sDVE, sDVE.n)], lambda e: e.tensor_sub(stat[:, 54:55], stat[:, 52:53], stat[:, 53:54]),
              inc=sDVE)
            A("scalar", [(sDVE, sDVE.n)],
              lambda e: e.activation(stat[:, 55:56], stat[:, 54:55], AF.Ln, bias=epsA[:, 0:1]))
            A("scalar", [],
              lambda e: e.activation(stat[:, 56:57], stat[:, 55:56], AF.Exp, bias=0.0, scale=-0.5),
              inc=sACT)
            A("vector", [(sACT, sACT.n), (sDVE, sDVE.n), (dX[st % 2], ostore_free[st % 2])],
              lambda e, st=st: e.tensor_scalar(
                  outst[:, (st % 2) * D:(st % 2 + 1) * D], X1R[:, st * D:(st + 1) * D],
                  stat[:, 51:52], stat[:, 56:57], op0=ALU.subtract, op1=ALU.mult),
              inc=sDVE)
            A("sync", [(sDVE, sDVE.n)],
              lambda e, s=s, st=st: e.dma_start(
                  y_d[s, st * 128:(st + 1) * 128, :], outst[:, (st % 2) * D:(st % 2 + 1) * D]),
              inc=dX[st % 2], k=16)
            ostore_free[st % 2] = dX[st % 2].n

        up_free = {0: 0, 1: 0}
        dn_free = {0: 0, 1: 0}
        for ep in range(4):
            for c in range(8):
                if ep >= 2:
                    wts = [(sACT, sACT.n), (sPE, sPE.n)] if c == 0 else []
                else:
                    t = ep * 4 + c // 2
                    wts = [(sDVE, gd[t]), (sACT, sqa[t])]
                A("gpsimd", wts,
                  lambda e, c=c, ep=ep: e.dma_start(
                      upw[ep % 2][:, c * KB:(c + 1) * KB],
                      upT_d[c * 128:(c + 1) * 128, ep * KB:(ep + 1) * KB]),
                  inc=dUW[ep % 2], k=16)
            for cc in range(8):
                if ep >= 2:
                    wts = []
                else:
                    t = 8 + ep * 4 + cc // 2
                    wts = [(sDVE, gd[t]), (sACT, sqa[t])]
                A("gpsimd", wts,
                  lambda e, cc=cc, ep=ep: e.dma_start(
                      dww[ep % 2][:, cc * KB:(cc + 1) * KB],
                      downT_d[ep * KB + cc * 128: ep * KB + (cc + 1) * 128, :]),
                  inc=dUW[ep % 2], k=16)
            w_rdy = dUW[ep % 2].n
            for sc in range(4):
                for et in range(8):
                    bank = a(0) if et % 2 == 0 else a(1)
                    for c in range(8):
                        wts = []
                        if c == 0:
                            wts = [(dUW[ep % 2], w_rdy), (sACT, up_free[et % 2]),
                                   (sCP, gts[4 * sc + 3])]
                            if ep == 0 and sc == 0 and et == 0:
                                wts.append((sDVE, n2cp))
                        A("tensor", wts,
                          lambda e, c=c, et=et, sc=sc, ep=ep, bank=bank: e.matmul(
                              bank, upw[ep % 2][:, c * KB + et * 128: c * KB + (et + 1) * 128],
                              gT[c][:, sc * 512:(sc + 1) * 512],
                              start=(c == 0), stop=(c == 7)),
                          inc=(sPE if c == 7 else None))
                    A("scalar", [(sPE, sPE.n)],
                      lambda e, et=et, sc=sc, bank=bank: e.activation(
                          siluT[sc % 2][:, et * 512:(et + 1) * 512], bank, AF.Silu),
                      inc=sACT)
                    up_free[et % 2] = sACT.n
                silu_done = sACT.n
                for st2 in range(4):
                    st = sc * 4 + st2
                    bank = pB if st2 % 2 == 0 else pA[:, 1024:2048]
                    for et in range(8):
                        wts = []
                        if et == 0:
                            wts = [(sACT, silu_done), (sDVE, dn_free[st2 % 2]),
                                   (dUW[ep % 2], w_rdy)]
                            if ep == 0 and sc == 0 and st2 == 1:
                                wts.append((sCP, b5cp))
                        for half in range(2):
                            A("tensor", wts if half == 0 else [],
                              lambda e, et=et, half=half, st2=st2, sc=sc, ep=ep, bank=bank: e.matmul(
                                  bank[:, half * 512:(half + 1) * 512],
                                  siluT[sc % 2][:, et * 512 + st2 * 128: et * 512 + st2 * 128 + 128],
                                  dww[ep % 2][:, et * KB + half * 512: et * KB + (half + 1) * 512],
                                  start=(et == 0), stop=(et == 7)),
                              inc=(sPE if (et == 7 and half == 1) else None))
                    A("vector", [(sPE, sPE.n), (sDVE, fdone[st])],
                      lambda e, st=st, bank=bank: e.scalar_tensor_tensor(
                          X1R[:, st * D:(st + 1) * D], bank, factor[:, st:st + 1],
                          X1R[:, st * D:(st + 1) * D], op0=ALU.mult, op1=ALU.add),
                      inc=sDVE)
                    dn_free[st2 % 2] = sDVE.n
                    if ep == 3:
                        emit_b7(st)
        pl.flush(f"B6_s{s}")


    with nc.Block("fin") as blk:
        @blk.sync
        def _(e):
            e.wait_ge(sO.h, sO.n)
            e.wait_ge(dX[0].h, dX[0].n)
            e.wait_ge(dX[1].h, dX[1].n)

    nc.finalize()
    return nc


_NC_CACHE = {}


def _get_nc(upto):
    if upto not in _NC_CACHE:
        _NC_CACHE[upto] = build_program(upto)
    return _NC_CACHE[upto]


def make_consts():
    identbf = np.eye(128, dtype=ml_dtypes.bfloat16)
    identf = np.eye(128, dtype=np.float32)
    onescol = np.ones((1, 128), dtype=np.float32)
    onescolb = np.ones((128, 128), dtype=ml_dtypes.bfloat16)
    ones32 = np.ones((128, 32), dtype=ml_dtypes.bfloat16)
    i = np.arange(128).reshape(128, 1)
    c = np.arange(2048).reshape(1, 2048)
    rampA = (c - C0 - i).astype(np.float32)
    j = np.arange(128).reshape(1, 128)
    maskA = np.where(j > i, -1e30, 0.0).astype(np.float32)
    maskB = np.where(i > j, -1e30, 0.0).astype(np.float32)
    return identbf, identf, onescol, onescolb, ones32, rampA, maskA, maskB


def kernel(L0_attn_x, L0_attn_W_qk, L0_attn_W_v, L0_attn_alibi_slope,
           L0_ffn_n1_w, L0_ffn_qkv_w, L0_ffn_o_w, L0_ffn_n2_w,
           L0_ffn_gate_w, L0_ffn_up_w, L0_ffn_down_w, _upto="full", _ncores=8):
    X = np.ascontiguousarray(np.asarray(L0_attn_x, dtype=np.float32))
    wqk = np.ascontiguousarray(np.asarray(L0_attn_W_qk, dtype=np.float32))
    wvT = np.ascontiguousarray(np.asarray(L0_attn_W_v, dtype=np.float32).T)
    slope = np.asarray(L0_attn_alibi_slope, dtype=np.float32).reshape(1, 1)
    n1 = np.asarray(L0_ffn_n1_w, dtype=np.float32).reshape(1, D)
    n2 = np.asarray(L0_ffn_n2_w, dtype=np.float32).reshape(1, D)
    qkvT = np.ascontiguousarray(np.asarray(L0_ffn_qkv_w, dtype=np.float32).T)
    owT = np.ascontiguousarray(np.asarray(L0_ffn_o_w, dtype=np.float32).T)
    gateT = np.ascontiguousarray(np.asarray(L0_ffn_gate_w, dtype=np.float32).T)
    upT = np.ascontiguousarray(np.asarray(L0_ffn_up_w, dtype=np.float32).T)
    downT = np.ascontiguousarray(np.asarray(L0_ffn_down_w, dtype=np.float32).T)
    identbf, identf, onescol, onescolb, ones32, rampA, maskA, maskB = make_consts()

    nc = _get_nc(_upto)
    in_maps = []
    for i in range(_ncores):
        in_maps.append({
            "x": np.ascontiguousarray(X[2 * i:2 * i + 2]),
            "wqk": wqk, "wvT": wvT, "slope": slope, "n1": n1, "n2": n2,
            "qkvT": qkvT, "owT": owT, "gateT": gateT, "upT": upT, "downT": downT,
            "identbf": identbf, "identf": identf, "onescol": onescol,
            "onescolb": onescolb, "ones32": ones32,
            "rampA": rampA, "maskA": maskA, "maskB": maskB,
        })
    res = run_bass_kernel_spmd(nc, in_maps, core_ids=list(range(_ncores)))
    if _upto in ("A01", "setup", "A2d", "B23d"):
        return res.results[0]["dbg"]
    out = np.concatenate([r["y"] for r in res.results], axis=0)
    return out.reshape(2 * _ncores, S, D).astype(np.float32)


# version 17
# speedup vs baseline: 1.1660x; 1.0060x over previous
import os
import numpy as np
import ml_dtypes
import concourse.bass as bass
import concourse.mybir as mybir
from concourse.bass_utils import run_bass_kernel_spmd

F32 = mybir.dt.float32
BF16 = mybir.dt.bfloat16
AF = mybir.ActivationFunctionType
ALU = mybir.AluOpType
AX = mybir.AxisListType

S = 2048
D = 1024
NT = 16
H = 16
HD = 64
NP = 8
ED = 4096
NE = 8
NSEQ = 2
C0 = 1920
KB = 1024
G65 = H * 65


class Sem:
    def __init__(self, nc, name):
        self.cm = nc.semaphore(name)
        self.h = self.cm.__enter__()
        self.n = 0


class Plan:
    ENGS = ("sync", "gpsimd", "tensor", "scalar", "vector")

    def __init__(self, nc):
        self.nc = nc
        self.q = {e: [] for e in self.ENGS}

    def add(self, eng, waits, fn, inc=None, k=1):
        wcap = [(sm.h, v) for (sm, v) in waits]
        if inc is not None:
            inc.n += k
        ih = None if inc is None else inc.h

        def run(e, wcap=wcap, fn=fn, ih=ih, k=k):
            for (h, v) in wcap:
                if v > 0:
                    e.wait_ge(h, v)
            ins = fn(e)
            if ih is not None:
                ins.then_inc(ih, k)
        self.q[eng].append(run)

    def flush(self, name=None):
        q = self.q
        self.q = {e: [] for e in self.ENGS}
        with self.nc.Block(name) as blk:
            for ename in self.ENGS:
                fns = q[ename]
                if not fns:
                    continue

                def body(eng, fns=fns):
                    for f in fns:
                        f(eng)
                getattr(blk, ename)(body)


def build_program(upto="full"):
    nc = bass.Bass("TRN2", target_bir_lowering=False)

    x_d = nc.dram_tensor("x", [NSEQ, S, D], F32, kind="ExternalInput").ap()
    wqk_d = nc.dram_tensor("wqk", [D, D], F32, kind="ExternalInput").ap()
    wvT_d = nc.dram_tensor("wvT", [D, D], F32, kind="ExternalInput").ap()
    slope_d = nc.dram_tensor("slope", [1, 1], F32, kind="ExternalInput").ap()
    n1_d = nc.dram_tensor("n1", [1, D], F32, kind="ExternalInput").ap()
    n2_d = nc.dram_tensor("n2", [1, D], F32, kind="ExternalInput").ap()
    qkvT_d = nc.dram_tensor("qkvT", [D, 3 * D], F32, kind="ExternalInput").ap()
    owT_d = nc.dram_tensor("owT", [D, D], F32, kind="ExternalInput").ap()
    gateT_d = nc.dram_tensor("gateT", [D, NE], F32, kind="ExternalInput").ap()
    upT_d = nc.dram_tensor("upT", [D, ED], F32, kind="ExternalInput").ap()
    downT_d = nc.dram_tensor("downT", [ED, D], F32, kind="ExternalInput").ap()
    identbf_d = nc.dram_tensor("identbf", [128, 128], BF16, kind="ExternalInput").ap()
    identf_d = nc.dram_tensor("identf", [128, 128], F32, kind="ExternalInput").ap()
    onescol_d = nc.dram_tensor("onescol", [1, 128], F32, kind="ExternalInput").ap()
    onescolb_d = nc.dram_tensor("onescolb", [128, 128], BF16, kind="ExternalInput").ap()
    ones32_d = nc.dram_tensor("ones32", [128, 32], BF16, kind="ExternalInput").ap()
    rampA_d = nc.dram_tensor("rampA", [128, 2048], F32, kind="ExternalInput").ap()
    maskA_d = nc.dram_tensor("maskA", [128, 128], F32, kind="ExternalInput").ap()
    maskB_d = nc.dram_tensor("maskB", [128, 128], F32, kind="ExternalInput").ap()
    y_d = nc.dram_tensor("y", [NSEQ, S, D], F32, kind="ExternalOutput").ap()
    dbg_d = nc.dram_tensor("dbg", [128, 4096], F32, kind="ExternalOutput").ap()

    X1R = nc.alloc_sbuf_tensor("X1R", [128, NT * D], F32).ap()
    factor = nc.alloc_sbuf_tensor("factor", [128, NT], F32).ap()
    maskA = nc.alloc_sbuf_tensor("maskA_s", [128, 128], F32).ap()
    maskB = nc.alloc_sbuf_tensor("maskB_s", [128, 128], F32).ap()
    identbf = nc.alloc_sbuf_tensor("identbf_s", [128, 128], BF16).ap()
    identf = nc.alloc_sbuf_tensor("identf_s", [128, 128], F32).ap()
    onescol = nc.alloc_sbuf_tensor("onescol_s", [1, 128], F32).ap()
    onescolb = nc.alloc_sbuf_tensor("onescolb_s", [128, 128], BF16).ap()
    ones32 = nc.alloc_sbuf_tensor("ones32_s", [128, 32], BF16).ap()
    slope_sb = nc.alloc_sbuf_tensor("slope_sb", [1, 32], F32).ap()
    slope_bc = nc.alloc_sbuf_tensor("slope_bc", [128, 1], F32).ap()
    epsA = nc.alloc_sbuf_tensor("epsA", [128, 8], F32).ap()
    eps1 = nc.alloc_sbuf_tensor("eps1", [1, 8], F32).ap()
    stat = nc.alloc_sbuf_tensor("stat", [128, 64], F32).ap()
    mu1 = nc.alloc_sbuf_tensor("mu1", [1, 16], F32).ap()
    mu1b = nc.alloc_sbuf_tensor("mu1b", [8, 8], BF16).ap()
    nbr = nc.alloc_sbuf_tensor("nbr", [128, 8], F32).ap()
    nbrb = nc.alloc_sbuf_tensor("nbrb", [128, 8], BF16).ap()
    biasH = nc.alloc_sbuf_tensor("biasH", [128, 2], F32).ap()

    TOP = 229344
    ARENA = 138 * KB
    AB = TOP - ARENA

    def at(name, off, shape, dtype):
        esz = 4 if dtype == F32 else 2
        assert off + shape[1] * esz <= ARENA, (name, off, shape)
        return nc.alloc_sbuf_tensor_at(name, shape, dtype, offset=AB + off).ap()

    XT = [at(f"XT{c}", c * 4 * KB, [128, S], BF16) for c in range(8)]
    Vsb = at("Vsb", 32 * KB, [128, NT * D], BF16)
    wslot = at("wslot", 64 * KB, [128, 8 * D], BF16)
    Sstrip = [at(f"Sstrip{i}", 80 * KB + i * 8 * KB, [128, S], F32) for i in range(2)]
    PTring = at("PTring", 104 * KB, [128, 4 * 128], BF16)
    qpt = [at(f"qpt{m}", 105 * KB + m * 512, [128, 256], BF16) for m in range(8)]
    xres = [at(f"xres{i}", 109 * KB + i * 4 * KB, [128, D], F32) for i in range(2)]
    xbf = [at(f"xbf{i}", 117 * KB + i * 2 * KB, [128, D], BF16) for i in range(2)]
    Gfull = at("Gfull", 121 * KB, [128, 2048], F32)
    sqscrA = at("sqscrA", 129 * KB, [128, D], BF16)

    h1T = [at(f"h1T{c}", c * 4 * KB, [128, S], BF16) for c in range(8)]
    Vplus = at("Vplus", 32 * KB, [128, NT * G65], BF16)
    qkpT = [at(f"qkpT{i}", 65 * KB + i * 8 * KB, [128, 2 * S], BF16) for i in range(2)]
    negMrowA = at("negMrowA", 81 * KB, [65, S], BF16)
    qkvw = [at(f"qkvw{i}", 89 * KB + i * 4 * KB, [128, 2048], BF16) for i in range(2)]
    Ebuf2 = [at(f"Ebuf2_{i}", 97 * KB + i * KB, [128, 512], BF16) for i in range(3)]
    n1rep = at("n1rep", 100 * KB, [128, D], F32)
    bscr = at("bscr", 104 * KB, [128, D], F32)
    n1row = at("n1row", 108 * KB, [1, D], F32)
    vwslot = at("vwslot", 65 * KB, [128, 8 * D], BF16)
    rsrow = at("rsrow", 100 * KB, [1, 512], F32)
    rsb = at("rsb", 102 * KB, [64, 512], F32)
    todd = at("todd", 104 * KB, [64, 2 * 256], BF16)
    negrow = at("negrow", 105 * KB, [32, 128], BF16)
    mnegbf = at("mnegbf", 105 * KB + 512, [128, 32], F32)
    attnOT = [at(f"attnOT{c}", 106 * KB + c * 4 * KB, [128, S], BF16) for c in range(8)]
    dbgscr = at("dbgscr", 100 * KB, [128, 1024], F32)

    X2 = at("X2", 0, [128, NT * D], F32)
    owT = at("owT", 65 * KB, [128, 8 * D], BF16)
    n2rep = at("n2rep", 81 * KB, [128, D], F32)
    bscr2 = at("bscr2", 85 * KB, [128, D], F32)
    gbf = at("gbf", 89 * KB, [128, D], BF16)
    gateT_sb = at("gateT_sb", 91 * KB, [128, NE * 8], BF16)
    n2row = at("n2row", 92 * KB, [1, D], F32)
    gT = [at(f"gT{c}", 106 * KB + c * 4 * KB, [128, S], BF16) for c in range(8)]

    upw = [at(f"upw{i}", i * 16 * KB, [128, 8 * KB], BF16) for i in range(2)]
    dww = [at(f"dww{i}", 32 * KB + i * 16 * KB, [128, 8 * KB], BF16) for i in range(2)]
    siluT = [at(f"siluT{i}", 65 * KB + i * 8 * KB, [128, 8 * 512], BF16) for i in range(2)]
    lnscr = at("lnscr", 81 * KB, [128, D], BF16)
    outst = at("outst", 85 * KB, [128, 2 * D], F32)

    assert nc.sbuf_base <= AB, (nc.sbuf_base, AB)

    pA = nc.alloc_psum_tensor("pA", [128, 2048], F32).ap()
    pB = nc.alloc_psum_tensor("pB", [128, 1024], F32).ap()
    pC = nc.alloc_psum_tensor("pC", [128, 512], F32).ap()
    pT = nc.alloc_psum_tensor("pT", [128, 512], F32).ap()

    def a(i):
        return pA[:, i * 512:(i + 1) * 512]

    sG = Sem(nc, "sG")
    sS = Sem(nc, "sS")
    sO = Sem(nc, "sO")
    sPE = Sem(nc, "sPE")
    sACT = Sem(nc, "sACT")
    sDVE = Sem(nc, "sDVE")
    sCP = Sem(nc, "sCP")
    dX = [Sem(nc, "dX0"), Sem(nc, "dX1")]
    dQW = [Sem(nc, "dQW0"), Sem(nc, "dQW1")]
    dUW = [Sem(nc, "dUW0"), Sem(nc, "dUW1")]
    dTD = [Sem(nc, "dTD0"), Sem(nc, "dTD1")]
    dWS = Sem(nc, "dWS")

    pl = Plan(nc)
    A = pl.add

    A("sync", [], lambda e: e.dma_start(maskA, maskA_d), inc=sS, k=16)
    A("sync", [], lambda e: e.dma_start(maskB, maskB_d), inc=sS, k=16)
    A("sync", [], lambda e: e.dma_start(identbf, identbf_d), inc=sS, k=16)
    A("sync", [], lambda e: e.dma_start(identf, identf_d), inc=sS, k=16)
    A("sync", [], lambda e: e.dma_start(onescol, onescol_d), inc=sS, k=16)
    A("sync", [], lambda e: e.dma_start(onescolb, onescolb_d), inc=sS, k=16)
    A("sync", [], lambda e: e.dma_start(ones32, ones32_d), inc=sS, k=16)
    A("sync", [], lambda e: e.dma_start(slope_sb[0:1, 0:1], slope_d), inc=sS, k=16)
    A("vector", [], lambda e: e.memset(epsA[:, 0:1], 1e-5))
    A("vector", [], lambda e: e.memset(epsA[:, 1:2], 1e-6))
    A("vector", [], lambda e: e.memset(eps1[0:1, 0:1], 1e-38), inc=sDVE)
    A("tensor", [(sS, sS.n)],
      lambda e: e.matmul(pC[:, 0:1], onescol[0:1, 0:128], slope_sb[0:1, 0:1],
                         start=True, stop=True), inc=sPE)
    A("vector", [(sPE, sPE.n)], lambda e: e.tensor_copy(slope_bc, pC[:, 0:1]), inc=sDVE)
    A("vector", [(sS, sS.n)], lambda e: e.memset(epsA[:, 3:4], 0.0), inc=sDVE)
    pl.flush("setup")

    if upto == "setup":
        A("vector", [], lambda e: e.tensor_copy(outst[:, 0:1], slope_bc), inc=sDVE)
        A("vector", [], lambda e: e.tensor_copy(outst[:, 1:9], epsA), inc=sDVE)
        A("vector", [], lambda e: e.tensor_copy(outst[:, 9:137], maskA), inc=sDVE)
        A("sync", [(sDVE, sDVE.n)], lambda e: e.dma_start(dbg_d[:, 0:137], outst[:, 0:137]),
          inc=sO, k=16)
        pl.flush("dbgsetup")

    for s in range(NSEQ if upto != "setup" else 0):
        A("sync", [], lambda e: e.dma_start(Gfull, rampA_d), inc=sS, k=16)
        A("scalar", [(sS, sS.n), (sDVE, sDVE.n)],
          lambda e: e.activation(Gfull, Gfull, AF.Copy, bias=0.0, scale=slope_bc[:, 0:1]),
          inc=sACT)
        A("gpsimd", [],
          lambda e: e.dma_start(wslot.rearrange("p (c j) -> p c j", c=8),
                                wvT_d.rearrange("(c p) j -> p c j", p=128)),
          inc=dWS, k=16)
        wv_ready = dWS.n

        _tiles = int(os.environ.get("KDBG_TILES", NT))
        _nov = int(os.environ.get("KDBG_NOV", "0"))
        _notr = int(os.environ.get("KDBG_NOTR", "0"))
        trslot = [pA[:, 1024:1152], pA[:, 1536:1664]]
        tr_done = {0: 0, 1: 0}
        vv_free = {0: 0, 1: 0}
        for st in range(_tiles):
            w = [(sPE, tr_done[st % 2])] if st >= 2 else []
            A("gpsimd", w,
              lambda e, s=s, st=st: e.dma_start(xres[st % 2], x_d[s, st * 128:(st + 1) * 128, :]),
              inc=dX[st % 2], k=16)
            xw = dX[st % 2].n
            for c in range(0 if _notr else 8):
                wts = [(sCP, max(0, sCP.n - 1))]
                if c == 0:
                    wts.append((dX[st % 2], xw))
                A("tensor", wts,
                  lambda e, st=st, c=c: e.transpose(
                      trslot[c % 2], xres[st % 2][:, c * 128:(c + 1) * 128], identf),
                  inc=sPE)
                A("vector", [(sPE, sPE.n)],
                  lambda e, st=st, c=c: e.tensor_copy(
                      XT[c][:, st * 128:(st + 1) * 128], trslot[c % 2]),
                  inc=sCP)
            tr_done[st % 2] = sPE.n
            vbank = pB if st % 2 == 0 else pA[:, 0:1024]
            for c in range(0 if _nov else 8):
                wts = []
                if c == 0:
                    wts = [(sCP, sCP.n), (dWS, wv_ready)]
                    if st >= 2:
                        wts.append((sACT, vv_free[st % 2]))
                for half in range(2):
                    A("tensor", wts if half == 0 else [],
                      lambda e, c=c, half=half, st=st, vbank=vbank: e.matmul(
                          vbank[:, half * 512:(half + 1) * 512],
                          XT[c][:, st * 128:(st + 1) * 128],
                          wslot[:, c * D + half * 512: c * D + (half + 1) * 512],
                          start=(c == 0), stop=(c == 7)),
                      inc=(sPE if (c == 7 and half == 1) else None))
            if not _nov:
                A("scalar", [(sPE, sPE.n)],
                  lambda e, st=st, vbank=vbank: e.activation(Vsb[:, st * D:(st + 1) * D], vbank, AF.Copy),
                  inc=sACT)
                vv_free[st % 2] = sACT.n
        pl.flush(f"A01_s{s}")

        if upto == "A01" and s == 0:
            A("vector", [], lambda e: e.tensor_copy(outst[:, 0:1024], Vsb[:, 0:1024]), inc=sDVE)
            A("sync", [(sDVE, sDVE.n)], lambda e: e.dma_start(dbg_d[:, 0:1024], outst[:, 0:1024]),
              inc=sO, k=16)
            A("vector", [], lambda e: e.tensor_copy(outst[:, 1024:2048], XT[0][:, 0:1024]), inc=sDVE)
            A("sync", [(sDVE, sDVE.n)], lambda e: e.dma_start(dbg_d[:, 1024:2048], outst[:, 1024:2048]),
              inc=sO, k=16)
            A("sync", [(sACT, sACT.n)], lambda e: e.dma_start(dbg_d[:, 2048:4096], Gfull), inc=sO, k=16)
            pl.flush("dbgA01")
            break

        A("gpsimd", [],
          lambda e: e.dma_start(wslot.rearrange("p (c j) -> p c j", c=8),
                                wqk_d.rearrange("(c p) j -> p c j", p=128)),
          inc=dWS, k=16)
        wq_ready = dWS.n

        qpt_done = 0
        qp_free = {0: 0, 1: 0}
        sc_free = {0: 0, 1: 0}
        xr_free = {0: 0, 1: 0}
        yv_free = 0
        tr_fin = {0: 0, 1: 0}
        _qis = int(os.environ.get("KDBG_QIS", NT))
        for qi in range(_qis):
            kend = (qi + 1) * 128
            nkc = (kend + 511) // 512
            buf = qi % 2
            A("sync", [(sDVE, xr_free[qi % 2])] if qi >= 2 else [],
              lambda e, s=s, qi=qi: e.dma_start(xres[qi % 2], x_d[s, qi * 128:(qi + 1) * 128, :]),
              inc=dX[qi % 2], k=16)
            xres_rdy = dX[qi % 2].n

            if qi % 2 == 0:
                for m in range(8):
                    bank = a(0) if m % 2 == 0 else a(1)
                    for c in range(8):
                        wts = []
                        if c == 0:
                            wts = [(dWS, wq_ready), (sACT, qp_free[m % 2])]
                        A("tensor", wts,
                          lambda e, m=m, c=c, qi=qi, bank=bank: e.matmul(
                              bank[:, 0:256],
                              wslot[:, c * D + m * 128: c * D + (m + 1) * 128],
                              XT[c][:, qi * 128: qi * 128 + 256],
                              start=(c == 0), stop=(c == 7)),
                          inc=(sPE if c == 7 else None))
                    A("scalar", [(sPE, sPE.n)],
                      lambda e, m=m, bank=bank: e.activation(
                          qpt[m], bank[:, 0:256], AF.Copy, bias=0.0, scale=1.0 / 32.0),
                      inc=sACT)
                    qp_free[m % 2] = sACT.n
                qpt_done = sACT.n

            for kc in range(nkc):
                w = min(512, kend - kc * 512)
                bank = a(2) if kc % 2 == 0 else a(3)
                for c in range(8):
                    wts = []
                    if c == 0:
                        wts = [(sACT, qpt_done), (sDVE, sc_free[kc % 2])]
                    A("tensor", wts,
                      lambda e, c=c, kc=kc, qi=qi, w=w, bank=bank: e.matmul(
                          bank[:, 0:w],
                          qpt[c][:, (qi % 2) * 128:(qi % 2) * 128 + 128],
                          XT[c][:, kc * 512: kc * 512 + w],
                          start=(c == 0), stop=(c == 7)),
                      inc=(sPE if c == 7 else None))
                off = kc * 512 - qi * 128 + C0
                sw = [(sPE, sPE.n)]
                if kc == 0:
                    sw.append((sCP, tr_fin[qi % 2]))
                A("vector", sw,
                  lambda e, kc=kc, w=w, off=off, buf=buf, bank=bank: e.tensor_add(
                      Sstrip[buf][:, kc * 512: kc * 512 + w], bank[:, 0:w],
                      Gfull[:, off: off + w]),
                  inc=sDVE)
                sc_free[kc % 2] = sDVE.n
                if kc == nkc - 1:
                    A("vector", [],
                      lambda e, qi=qi, buf=buf: e.tensor_add(
                          Sstrip[buf][:, qi * 128: qi * 128 + 128],
                          Sstrip[buf][:, qi * 128: qi * 128 + 128], maskA),
                      inc=sDVE)
            A("vector", [(sDVE, sDVE.n)],
              lambda e, buf=buf, kend=kend: e.reduce_max(
                  stat[:, 4:5], Sstrip[buf][:, 0:kend], axis=AX.X, negate=True),
              inc=sDVE)
            A("scalar", [(sDVE, sDVE.n)],
              lambda e, buf=buf, kend=kend: e.activation(
                  Sstrip[buf][:, 0:kend], Sstrip[buf][:, 0:kend], AF.Exp,
                  bias=stat[:, 4:5], accum_out=stat[:, 5:6]),
              inc=sACT)
            exp_done = sACT.n
            A("vector", [(sACT, exp_done)],
              lambda e: e.reciprocal(stat[:, 6:7], stat[:, 5:6]), inc=sDVE)

            ntk = qi + 1
            cps = []

            def emit_tr(kt, buf=buf):
                A("tensor", [(sACT, exp_done), (sCP, sCP.n)],
                  lambda e, kt=kt, buf=buf: e.transpose(
                      pT[:, (kt % 4) * 128:(kt % 4 + 1) * 128],
                      Sstrip[buf][:, kt * 128:(kt + 1) * 128], identf),
                  inc=sPE)
                A("vector", [(sPE, sPE.n)],
                  lambda e, kt=kt: e.tensor_copy(
                      PTring[:, (kt % 4) * 128:(kt % 4 + 1) * 128],
                      pT[:, (kt % 4) * 128:(kt % 4 + 1) * 128]),
                  inc=sCP)
                cps.append(sCP.n)

            def emit_av(kt, qi=qi):
                for half in range(2):
                    wts = []
                    if half == 0:
                        wts = [(sCP, cps[kt])]
                        if kt == 0:
                            wts.append((sDVE, yv_free))
                    A("tensor", wts,
                      lambda e, kt=kt, half=half, qi=qi: e.matmul(
                          pB[:, half * 512:(half + 1) * 512],
                          PTring[:, (kt % 4) * 128:(kt % 4 + 1) * 128],
                          Vsb[:, kt * D + half * 512: kt * D + (half + 1) * 512],
                          start=(kt == 0), stop=(kt == qi)),
                      inc=(sPE if (kt == qi and half == 1) else None))

            for kt in range(ntk):
                emit_tr(kt)
                if kt >= 1:
                    emit_av(kt - 1)
            emit_av(ntk - 1)
            tr_fin[qi % 2] = cps[-1]
            av_done = sPE.n

            A("vector", [(sPE, av_done), (dX[qi % 2], xres_rdy)],
              lambda e, qi=qi: e.scalar_tensor_tensor(
                  xres[qi % 2], pB, stat[:, 6:7], xres[qi % 2], op0=ALU.mult, op1=ALU.add),
              inc=sDVE)
            yv_free = sDVE.n
            ydone = sDVE.n
            A("scalar", [(sDVE, ydone)],
              lambda e, qi=qi: e.activation(sqscrA, xres[qi % 2], AF.Square, accum_out=stat[:, 8:9]),
              inc=sACT)
            A("vector", [],
              lambda e, qi=qi: e.reduce_sum(stat[:, 7:8], xres[qi % 2], axis=AX.X), inc=sDVE)
            A("vector", [(sDVE, sDVE.n)],
              lambda e: e.tensor_scalar(stat[:, 9:10], stat[:, 7:8], 1.0 / D, None, op0=ALU.mult),
              inc=sDVE)
            A("vector", [(sACT, sACT.n), (sDVE, sDVE.n)],
              lambda e: e.tensor_scalar(stat[:, 10:11], stat[:, 8:9], 1.0 / D, None, op0=ALU.mult),
              inc=sDVE)
            A("vector", [(sDVE, sDVE.n)],
              lambda e: e.tensor_mul(stat[:, 11:12], stat[:, 9:10], stat[:, 9:10]),
              inc=sDVE)
            A("vector", [(sDVE, sDVE.n)],
              lambda e: e.tensor_sub(stat[:, 12:13], stat[:, 10:11], stat[:, 11:12]),
              inc=sDVE)
            A("scalar", [(sDVE, sDVE.n)],
              lambda e: e.activation(stat[:, 13:14], stat[:, 12:13], AF.Ln, bias=epsA[:, 0:1]))
            A("scalar", [],
              lambda e: e.activation(stat[:, 14:15], stat[:, 13:14], AF.Exp, bias=0.0, scale=-0.5),
              inc=sACT)
            A("vector", [(sACT, sACT.n), (sDVE, sDVE.n)],
              lambda e, qi=qi: e.tensor_scalar(
                  X1R[:, qi * D:(qi + 1) * D], xres[qi % 2],
                  stat[:, 9:10], stat[:, 14:15], op0=ALU.subtract, op1=ALU.mult),
              inc=sDVE)
            xr_free[qi % 2] = sDVE.n
        pl.flush(f"A2_s{s}")

        if upto == "A2d" and s == 0:
            A("sync", [(sDVE, sDVE.n), (sACT, sACT.n)],
              lambda e: e.dma_start(dbg_d[:, 0:2048], Sstrip[0]), inc=sO, k=16)
            A("vector", [(sO, sO.n)], lambda e: e.tensor_copy(Sstrip[1], Sstrip[0]), inc=sDVE)
            A("vector", [], lambda e: e.tensor_copy(Sstrip[1][:, 0:64], stat), inc=sDVE)
            A("sync", [(sDVE, sDVE.n)], lambda e: e.dma_start(dbg_d[:, 2048:4096], Sstrip[1]), inc=sO, k=16)
            pl.flush("dbgA2")
            break

        if upto == "X1":
            for st in range(NT):
                A("sync", [(sDVE, sDVE.n)],
                  lambda e, s=s, st=st: e.dma_start(
                      y_d[s, st * 128:(st + 1) * 128, :], X1R[:, st * D:(st + 1) * D]),
                  inc=sO, k=16)
            A("vector", [(sO, sO.n)], lambda e: e.memset(epsA[:, 4:5], 0.0), inc=sDVE)
            pl.flush(f"dbgX1_s{s}")
            continue

        A("sync", [], lambda e: e.dma_start(n1row, n1_d), inc=sS, k=16)
        n1_rdy = sS.n
        A("tensor", [(sS, n1_rdy)],
          lambda e: e.matmul(pC, onescol[0:1, 0:128], n1row[0:1, 0:512], start=True, stop=True))
        A("tensor", [],
          lambda e: e.matmul(a(0), onescol[0:1, 0:128], n1row[0:1, 512:1024], start=True, stop=True),
          inc=sPE)
        A("vector", [(sPE, sPE.n)], lambda e: e.tensor_copy(n1rep[:, 0:512], pC), inc=sDVE)
        A("vector", [], lambda e: e.tensor_copy(n1rep[:, 512:1024], a(0)), inc=sDVE)
        n1cp = sDVE.n
        h1ts = []
        h1ds = []

        for st in range(NT):
            A("scalar", [(sDVE, h1ds[st - 1] if st else sDVE.n)],
              lambda e, st=st: e.activation(
                  gbf, X1R[:, st * D:(st + 1) * D], AF.Square, accum_out=stat[:, 16:17]),
              inc=sACT)
            A("scalar", [],
              lambda e: e.activation(stat[:, 17:18], stat[:, 16:17], AF.Ln,
                                     bias=epsA[:, 1:2], scale=1.0 / D))
            A("scalar", [],
              lambda e: e.activation(stat[:, 18:19], stat[:, 17:18], AF.Exp, bias=0.0, scale=-0.5),
              inc=sACT)
            A("vector", [(sACT, sACT.n)],
              lambda e, st=st: e.tensor_scalar(
                  bscr, X1R[:, st * D:(st + 1) * D], stat[:, 18:19], None, op0=ALU.mult),
              inc=sDVE)
            A("vector", [], lambda e: e.tensor_mul(bscr, bscr, n1rep), inc=sDVE)
            h1done = sDVE.n
            h1ds.append(h1done)
            trslot = [pA[:, 1024:1152], pA[:, 1536:1664]]
            for c in range(8):
                wts = [(sCP, max(0, sCP.n - 1))]
                if c == 0:
                    wts.append((sDVE, h1done))
                A("tensor", wts,
                  lambda e, c=c, trslot=trslot: e.transpose(
                      trslot[c % 2], bscr[:, c * 128:(c + 1) * 128], identf),
                  inc=sPE)
                A("vector", [(sPE, sPE.n)],
                  lambda e, st=st, c=c, trslot=trslot: e.tensor_copy(
                      h1T[c][:, st * 128:(st + 1) * 128], trslot[c % 2]),
                  inc=sCP)
            h1ts.append(sCP.n)

        A("gpsimd", [],
          lambda e: e.dma_start(vwslot.rearrange("p (c j) -> p c j", c=8),
                                qkvT_d[:, 2 * D:3 * D].rearrange("(c p) j -> p c j", p=128)),
          inc=dWS, k=16)
        vw_rdy = dWS.n
        b1_free = {0: 0, 1: 0}
        for st in range(NT):
            vbank = pB if st % 2 == 0 else pA[:, 0:1024]
            for c in range(8):
                wts = []
                if c == 0:
                    wts = [(dWS, vw_rdy), (sCP, h1ts[st])]
                    if st == 1:
                        wts.append((sDVE, n1cp))
                    if st >= 2:
                        wts.append((sDVE, b1_free[st % 2]))
                for half in range(2):
                    A("tensor", wts if half == 0 else [],
                      lambda e, c=c, half=half, st=st, vbank=vbank: e.matmul(
                          vbank[:, half * 512:(half + 1) * 512],
                          h1T[c][:, st * 128:(st + 1) * 128],
                          vwslot[:, c * D + half * 512: c * D + (half + 1) * 512],
                          start=(c == 0), stop=(c == 7)),
                      inc=(sPE if (c == 7 and half == 1) else None))
            A("vector", [(sPE, sPE.n)],
              lambda e, st=st, vbank=vbank: e.tensor_copy(
                  Vplus[:, st * G65:(st + 1) * G65].rearrange("p (g w) -> p g w", w=65)[:, :, 0:64],
                  vbank.rearrange("p (g w) -> p g w", w=64)),
              inc=sDVE)
            A("vector", [],
              lambda e, st=st: e.memset(
                  Vplus[:, st * G65:(st + 1) * G65].rearrange("p (g w) -> p g w", w=65)[:, :, 64:65],
                  1.0),
              inc=sDVE)
            b1_free[st % 2] = sDVE.n
        pl.flush(f"B1_s{s}")

        ebuf_ready = [sPE.n, sPE.n, sPE.n]
        stb_free = {0: 0, 1: 0}
        av_free = {0: 0, 1: 0}
        todd_free = {0: 0, 1: 0}
        ps = {"qkw_free": {0: 0, 1: 0}, "qkpT_free": {0: 0, 1: 0},
              "pc_w": [], "pt_w": [], "evac": {}, "wrdy": {}, "ops": {}}

        def emit_qkdma(r):
            pb = r % 2
            for c in range(8):
                wts = [(sPE, ps["qkw_free"][pb])] if c == 0 else []
                A("gpsimd", wts,
                  lambda e, c=c, r=r, pb=pb: e.dma_start(
                      qkvw[pb][:, c * 256: c * 256 + 128],
                      qkvT_d[c * 128:(c + 1) * 128, r * 128:(r + 1) * 128]),
                  inc=dQW[pb], k=16)
                A("gpsimd", [],
                  lambda e, c=c, r=r, pb=pb: e.dma_start(
                      qkvw[pb][:, c * 256 + 128: c * 256 + 256],
                      qkvT_d[c * 128:(c + 1) * 128, D + r * 128: D + (r + 1) * 128]),
                  inc=dQW[pb], k=16)
            ps["wrdy"][r] = dQW[pb].n

        def emit_qkgrp(r, tgt, qq):
            pb = r % 2
            for c in range(8):
                wts = ([(dQW[pb], ps["wrdy"][r])] + ps["pc_w"]) if c == 0 else []
                A("tensor", wts,
                  lambda e, c=c, tgt=tgt, qq=qq, pb=pb: e.matmul(
                      pC[:, 0:512],
                      qkvw[pb][:, c * 256 + tgt * 128: c * 256 + tgt * 128 + 128],
                      h1T[c][:, qq * 512:(qq + 1) * 512],
                      start=(c == 0), stop=(c == 7)),
                  inc=(sPE if c == 7 else None))
            ps["qkw_free"][pb] = sPE.n
            A("scalar", [(sPE, sPE.n), (sPE, ps["qkpT_free"][pb])],
              lambda e, tgt=tgt, qq=qq, pb=pb: e.activation(
                  qkpT[pb][:, tgt * S + qq * 512: tgt * S + (qq + 1) * 512],
                  pC[:, 0:512], AF.Copy, bias=0.0, scale=(0.125 if tgt == 0 else 1.0)),
              inc=sACT)
            ps["evac"][(pb, tgt, qq)] = sACT.n
            ps["pc_w"] = [(sACT, sACT.n)]

        def emit_tmchunk(r, qt, kc):
            pb = r % 2
            kend2 = (qt + 1) * 128
            nkc2 = (kend2 + 511) // 512
            w2 = min(512, kend2 - kc * 512)
            wts = [(sACT, ps["evac"][(pb, 0, qt // 4)]),
                   (sACT, ps["evac"][(pb, 1, kc)])] + ps["pc_w"]
            A("tensor", wts,
              lambda e, qt=qt, kc=kc, w2=w2, pb=pb: e.matmul(
                  pC[:, 0:w2],
                  qkpT[pb][0:64, qt * 128:(qt + 1) * 128],
                  qkpT[pb][0:64, S + kc * 512: S + kc * 512 + w2],
                  start=True, stop=True, tile_position=(0, 0)))
            A("tensor", list(ps["pt_w"]),
              lambda e, qt=qt, kc=kc, w2=w2, pb=pb: e.matmul(
                  pT[:, 0:w2],
                  qkpT[pb][64:128, qt * 128:(qt + 1) * 128],
                  qkpT[pb][64:128, S + kc * 512: S + kc * 512 + w2],
                  start=True, stop=True, tile_position=(64, 0)),
              inc=sPE)
            mm2 = sPE.n
            if kc == nkc2 - 1:
                doff = qt * 128 - kc * 512
                A("vector", [(sPE, mm2)],
                  lambda e, doff=doff: e.tensor_add(
                      pC[:, doff:doff + 128], pC[:, doff:doff + 128], maskA),
                  inc=sDVE)
                A("vector", [],
                  lambda e, doff=doff: e.tensor_add(
                      pT[:, doff:doff + 128], pT[:, doff:doff + 128], maskA),
                  inc=sDVE)
            A("vector", [(sPE, mm2), (sDVE, sDVE.n)],
              lambda e, w2=w2: e.reduce_max(stat[:, 32:33], pC[:, 0:w2], axis=AX.X),
              inc=sDVE)
            ps["pc_w"] = [(sDVE, sDVE.n)]
            A("vector", [],
              lambda e, w2=w2: e.reduce_max(stat[:, 33:34], pT[:, 0:w2], axis=AX.X),
              inc=sDVE)
            ps["pt_w"] = [(sDVE, sDVE.n)]
            if kc == 0:
                A("vector", [(sDVE, sDVE.n)],
                  lambda e: e.tensor_copy(stat[:, 34:36], stat[:, 32:34]), inc=sDVE)
            else:
                A("vector", [(sDVE, sDVE.n)],
                  lambda e: e.tensor_max(stat[:, 34:36], stat[:, 34:36], stat[:, 32:34]),
                  inc=sDVE)
            if kc == nkc2 - 1:
                for hh in range(2):
                    mc = hh * 16 + qt
                    A("vector", [(sDVE, sDVE.n)],
                      lambda e, mc=mc, hh=hh: e.tensor_scalar(
                          mnegbf[:, mc: mc + 1],
                          stat[:, 34 + hh:35 + hh], -1.0, None, op0=ALU.mult),
                      inc=sDVE)

        def emit_dance(r):
            A("tensor", [(sDVE, sDVE.n)] + list(ps["pt_w"]),
              lambda e: e.transpose(pT[0:32, 0:128], mnegbf[:, 0:32], identf),
              inc=sPE)
            tpn = sPE.n
            cw = [(sPE, tpn)]
            if r >= 1:
                cw.append((sS, ps["ops"][r - 1]))
            A("vector", cw,
              lambda e: e.tensor_copy(negrow, pT[0:32, 0:128]), inc=sDVE)
            ps["pt_w"] = [(sDVE, sDVE.n)]
            A("sync", [(sDVE, sDVE.n)],
              lambda e: e.dma_start(negMrowA[0:1, 0:S], negrow[0:16, :]),
              inc=sS, k=16)
            A("sync", [],
              lambda e: e.dma_start(negMrowA[64:65, 0:S], negrow[16:32, :]),
              inc=sS, k=16)
            ps["ops"][r] = sS.n

        def emit_stav(pp, units):
            pb = pp % 2
            ops_rdy = ps["ops"][pp]
            uq = list(units)
            tot = len(uq)
            nslots = 72
            sl = [0]

            def pump():
                sl[0] += 1
                while uq and (tot - len(uq)) * nslots < sl[0] * tot:
                    uq.pop(0)()

            for qc in range(8):
                ntk = 2 * qc + 2
                avb = pB[:, 0:512] if qc % 2 == 0 else pB[:, 512:1024]

                def emit_sc(kt, qc=qc):
                    par = kt % 2
                    half = (kt == 2 * qc + 1)
                    qlo = 128 if half else 0
                    sb = [a(0), a(1)] if par == 0 else [a(2), a(3)]
                    for hh in range(2):
                        wts = [(sS, ops_rdy), (sACT, stb_free[par])] if hh == 0 else []
                        A("tensor", wts,
                          lambda e, hh=hh, kt=kt, qc=qc, qlo=qlo, sb=sb: e.matmul(
                              sb[hh][:, qlo:256],
                              qkpT[pb][hh * 64:(hh + 1) * 64, S + kt * 128:S + (kt + 1) * 128],
                              qkpT[pb][hh * 64:(hh + 1) * 64, qc * 256 + qlo:(qc + 1) * 256],
                              start=True, stop=False,
                              tile_position=(hh * 64, 0)))
                    for hh in range(2):
                        A("tensor", [],
                          lambda e, hh=hh, qc=qc, qlo=qlo, sb=sb: e.matmul(
                              sb[hh][:, qlo:256],
                              onescolb[hh * 64:hh * 64 + 1, 0:128],
                              negMrowA[hh * 64:hh * 64 + 1, qc * 256 + qlo:(qc + 1) * 256],
                              start=False, stop=True,
                              tile_position=(hh * 64, 0)),
                          inc=(sPE if hh == 1 else None))
                    sc_done = sPE.n
                    nmask = 0
                    if kt == 2 * qc:
                        for hh in range(2):
                            A("vector", [(sPE, sc_done)] if hh == 0 else [],
                              lambda e, hh=hh, sb=sb: e.tensor_add(
                                  sb[hh][:, 0:128], sb[hh][:, 0:128], maskB),
                              inc=sDVE)
                        nmask = 2
                    elif half:
                        for hh in range(2):
                            A("vector", [(sPE, sc_done)] if hh == 0 else [],
                              lambda e, hh=hh, sb=sb: e.tensor_add(
                                  sb[hh][:, 128:256], sb[hh][:, 128:256], maskB),
                              inc=sDVE)
                        nmask = 2
                    msk_done = sDVE.n
                    ewts = [(sPE, sc_done), (sPE, ebuf_ready[kt % 3])]
                    if nmask:
                        ewts.append((sDVE, msk_done))
                    for hh in range(2):
                        A("scalar", ewts if hh == 0 else [],
                          lambda e, hh=hh, kt=kt, qlo=qlo, sb=sb: e.activation(
                              Ebuf2[kt % 3][:, hh * 256 + qlo:(hh + 1) * 256],
                              sb[hh][:, qlo:256], AF.Exp),
                          inc=(sACT if hh == 1 else None))
                    return sACT.n

                def emit_avb(kt, qc=qc, exp_n=0, ntk=ntk, avb=avb):
                    half = (kt == 2 * qc + 1)
                    qlo = 128 if half else 0
                    for hh in range(2):
                        wts = []
                        if hh == 0:
                            wts = [(sACT, exp_n)]
                            if kt == 0:
                                wts.append((sDVE, av_free[qc % 2]))
                        A("tensor", wts,
                          lambda e, hh=hh, kt=kt, qc=qc, pp=pp, ntk=ntk, qlo=qlo, avb=avb: e.matmul(
                              avb[0:65, hh * 256 + qlo:(hh + 1) * 256],
                              Vplus[:, kt * G65 + (2 * pp + hh) * 65: kt * G65 + (2 * pp + hh) * 65 + 65],
                              Ebuf2[kt % 3][:, hh * 256 + qlo:(hh + 1) * 256],
                              start=(kt == 0 and hh == 0), stop=(kt == ntk - 1 and hh == 1)),
                          inc=(sPE if hh == 1 else None))
                    ebuf_ready[kt % 3] = sPE.n

                exps = []
                for kt in range(ntk):
                    if kt >= 2:
                        stb_free[kt % 2] = exps[kt - 2]
                    exps.append(emit_sc(kt))
                    pump()
                    if kt >= 1:
                        emit_avb(kt - 1, exp_n=exps[kt - 1])
                emit_avb(ntk - 1, exp_n=exps[ntk - 1])
                av_fin = sPE.n
                stb_free[(ntk - 2) % 2] = exps[ntk - 2]
                stb_free[(ntk - 1) % 2] = exps[ntk - 1]

                A("scalar", [(sPE, av_fin)],
                  lambda e, avb=avb: e.activation(
                      rsrow[0:1, 0:512], avb[64:65, 0:512],
                      AF.Ln, bias=eps1[0:1, 0:1]))
                A("scalar", [],
                  lambda e: e.activation(
                      rsrow[0:1, 0:512], rsrow[0:1, 0:512],
                      AF.Exp, bias=0.0, scale=-1.0),
                  inc=sACT)
                rs_done = sACT.n
                A("tensor", [(sACT, rs_done)],
                  lambda e, avb=avb: e.matmul(avb[64:128, 0:512], onescol[0:1, 0:64],
                                              rsrow, start=True, stop=True,
                                              tile_position=(0, 64)),
                  inc=sPE)
                bc_done = sPE.n
                A("vector", [(sPE, bc_done)],
                  lambda e, avb=avb: e.tensor_copy(rsb, avb[64:128, 0:512]),
                  inc=sDVE)
                A("vector", [],
                  lambda e, pp=pp, qc=qc, avb=avb: e.tensor_mul(
                      attnOT[pp][0:64, qc * 256:(qc + 1) * 256],
                      avb[0:64, 0:256], rsb[0:64, 0:256]),
                  inc=sDVE)
                A("vector", [(dTD[qc % 2], todd_free[qc % 2])],
                  lambda e, qc=qc, avb=avb: e.tensor_mul(
                      todd[0:64, (qc % 2) * 256:(qc % 2) * 256 + 256],
                      avb[0:64, 256:512], rsb[0:64, 256:512]),
                  inc=sDVE)
                av_free[qc % 2] = sDVE.n
                A("sync", [(sDVE, sDVE.n)],
                  lambda e, pp=pp, qc=qc: e.dma_start(
                      attnOT[pp][64:128, qc * 256:(qc + 1) * 256],
                      todd[0:64, (qc % 2) * 256:(qc % 2) * 256 + 256]),
                  inc=dTD[qc % 2], k=16)
                todd_free[qc % 2] = dTD[qc % 2].n
            while uq:
                uq.pop(0)()
            ps["qkpT_free"][pb] = sPE.n

        for r in range(NP + 1):
            units = []
            if r < NP:
                emit_qkdma(r)
                for qq in range(4):
                    units.append(lambda r=r, qq=qq: emit_qkgrp(r, 0, qq))
                    units.append(lambda r=r, qq=qq: emit_qkgrp(r, 1, qq))
                for qt in range(NT):
                    for kc in range((qt // 4) + 1):
                        units.append(lambda r=r, qt=qt, kc=kc: emit_tmchunk(r, qt, kc))
            if r == 0:
                for u in units:
                    u()
            else:
                emit_stav(r - 1, units)
            if r < NP:
                emit_dance(r)
        pl.flush(f"B23_s{s}")

        if upto == "B23d" and s == 0:
            for i in range(2):
                A("vector", [(sO, sO.n)],
                  lambda e, i=i: e.tensor_copy(dbgscr, attnOT[0][:, i * 1024:(i + 1) * 1024]),
                  inc=sDVE)
                A("sync", [(sDVE, sDVE.n)],
                  lambda e, i=i: e.dma_start(dbg_d[:, i * 1024:(i + 1) * 1024], dbgscr), inc=sO, k=16)
            A("vector", [(sO, sO.n)], lambda e: e.tensor_copy(dbgscr[:, 0:32], mnegbf), inc=sDVE)
            A("sync", [(sDVE, sDVE.n)], lambda e: e.dma_start(dbg_d[:, 2048:2112], dbgscr[:, 0:64]),
              inc=sO, k=16)
            A("vector", [(sO, sO.n)], lambda e: e.tensor_copy(dbgscr[:, 0:256], h1T[0][:, 0:256]),
              inc=sDVE)
            A("sync", [(sDVE, sDVE.n)], lambda e: e.dma_start(dbg_d[:, 2304:2560], dbgscr[:, 0:256]),
              inc=sO, k=16)
            A("vector", [(sO, sO.n)], lambda e: e.tensor_copy(dbgscr[:, 0:1024], Vplus[:, 0:1024]),
              inc=sDVE)
            A("sync", [(sDVE, sDVE.n)], lambda e: e.dma_start(dbg_d[:, 2560:3584], dbgscr[:, 0:1024]),
              inc=sO, k=16)
            A("vector", [(sO, sO.n)], lambda e: e.tensor_copy(dbgscr[:, 0:1024], qkpT[0][:, 0:1024]),
              inc=sDVE)
            A("sync", [(sDVE, sDVE.n)], lambda e: e.dma_start(dbg_d[:, 3072:4096], dbgscr[:, 0:1024]),
              inc=sO, k=16)
            pl.flush("dbgB23")
            break

        A("gpsimd", [],
          lambda e: e.dma_start(owT.rearrange("p (c j) -> p c j", c=8),
                                owT_d.rearrange("(c p) j -> p c j", p=128)),
          inc=sG, k=16)
        ow_rdy = sG.n
        attn_fin = [(sS, sS.n), (sDVE, sDVE.n), (dTD[0], dTD[0].n), (dTD[1], dTD[1].n)]
        b4_free = {0: 0, 1: 0}
        for st in range(NT):
            bank = pB if st % 2 == 0 else pA[:, 0:1024]
            for c in range(8):
                wts = []
                if c == 0:
                    wts = [(sG, ow_rdy)] + attn_fin + [(sDVE, b4_free[st % 2])]
                for half in range(2):
                    A("tensor", wts if half == 0 else [],
                      lambda e, c=c, half=half, st=st, bank=bank: e.matmul(
                          bank[:, half * 512:(half + 1) * 512],
                          attnOT[c][:, st * 128:(st + 1) * 128],
                          owT[:, c * D + half * 512: c * D + (half + 1) * 512],
                          start=(c == 0), stop=(c == 7)),
                      inc=(sPE if (c == 7 and half == 1) else None))
            A("vector", [(sPE, sPE.n)],
              lambda e, st=st, bank=bank: e.tensor_add(
                  X2[:, st * D:(st + 1) * D], bank, X1R[:, st * D:(st + 1) * D]),
              inc=sDVE)
            A("vector", [],
              lambda e, st=st: e.tensor_add(
                  X1R[:, st * D:(st + 1) * D], X2[:, st * D:(st + 1) * D],
                  X1R[:, st * D:(st + 1) * D]),
              inc=sDVE)
            b4_free[st % 2] = sDVE.n
        pl.flush(f"B4_s{s}")

        if upto == "X2":
            for st in range(NT):
                A("sync", [(sDVE, sDVE.n)],
                  lambda e, s=s, st=st: e.dma_start(
                      y_d[s, st * 128:(st + 1) * 128, :], X2[:, st * D:(st + 1) * D]),
                  inc=sO, k=16)
            A("vector", [(sO, sO.n)], lambda e: e.memset(epsA[:, 4:5], 0.0), inc=sDVE)
            pl.flush(f"dbgX2_s{s}")
            continue

        A("sync", [], lambda e: e.dma_start(n2row, n2_d), inc=sS, k=16)
        n2_rdy = sS.n
        A("tensor", [(sS, n2_rdy)],
          lambda e: e.matmul(pC, onescol[0:1, 0:128], n2row[0:1, 0:512], start=True, stop=True))
        A("tensor", [],
          lambda e: e.matmul(a(0), onescol[0:1, 0:128], n2row[0:1, 512:1024], start=True, stop=True),
          inc=sPE)
        A("vector", [(sPE, sPE.n)], lambda e: e.tensor_copy(n2rep[:, 0:512], pC), inc=sDVE)
        A("vector", [], lambda e: e.tensor_copy(n2rep[:, 512:1024], a(0)), inc=sDVE)
        n2cp = sDVE.n
        gts = []
        fdone = []
        gd = []
        sqa = []
        A("gpsimd", [],
          lambda e: e.dma_start(gateT_sb.rearrange("p (c j) -> p c j", c=8),
                                gateT_d.rearrange("(c p) j -> p c j", p=128)),
          inc=sG, k=16)
        gate_rdy = sG.n
        ge_n = 0
        rm_n = n2cp
        for st in range(NT):
            A("scalar", [(sDVE, gd[st - 1] if st else sDVE.n)],
              lambda e, st=st: e.activation(
                  gbf, X2[:, st * D:(st + 1) * D], AF.Square, accum_out=stat[:, 20:21]),
              inc=sACT)
            A("scalar", [],
              lambda e: e.activation(stat[:, 21:22], stat[:, 20:21], AF.Ln,
                                     bias=epsA[:, 1:2], scale=1.0 / D))
            A("scalar", [],
              lambda e: e.activation(stat[:, 22:23], stat[:, 21:22], AF.Exp, bias=0.0, scale=-0.5),
              inc=sACT)
            sqa.append(sACT.n)
            A("vector", [(sACT, sACT.n)],
              lambda e, st=st: e.tensor_scalar(
                  bscr2, X2[:, st * D:(st + 1) * D], stat[:, 22:23], None, op0=ALU.mult),
              inc=sDVE)
            A("vector", [], lambda e: e.tensor_mul(bscr2, bscr2, n2rep), inc=sDVE)
            gdone = sDVE.n
            gd.append(gdone)
            trslot = [pA[:, 1024:1152], pA[:, 1536:1664]]
            for c in range(8):
                wts = [(sCP, max(0, sCP.n - 1))]
                if c == 0:
                    wts.append((sDVE, gdone))
                A("tensor", wts,
                  lambda e, c=c, trslot=trslot: e.transpose(
                      trslot[c % 2], bscr2[:, c * 128:(c + 1) * 128], identf),
                  inc=sPE)
                A("vector", [(sPE, sPE.n)],
                  lambda e, st=st, c=c, trslot=trslot: e.tensor_copy(
                      gT[c][:, st * 128:(st + 1) * 128], trslot[c % 2]),
                  inc=sCP)
            gt_done = sCP.n
            gts.append(gt_done)
            for c in range(8):
                wts = [(sCP, gt_done), (sG, gate_rdy), (sDVE, rm_n), (sACT, ge_n)] if c == 0 else []
                A("tensor", wts,
                  lambda e, c=c, st=st: e.matmul(
                      pC[:, 0:NE], gT[c][:, st * 128:(st + 1) * 128],
                      gateT_sb[:, c * NE:(c + 1) * NE], start=(c == 0), stop=(c == 7)),
                  inc=(sPE if c == 7 else None))
            A("vector", [(sPE, sPE.n)],
              lambda e: e.reduce_max(stat[:, 24:25], pC[:, 0:NE], axis=AX.X, negate=True),
              inc=sDVE)
            rm_n = sDVE.n
            A("scalar", [(sDVE, sDVE.n)],
              lambda e: e.activation(stat[:, 32:40], pC[:, 0:NE], AF.Exp,
                                     bias=stat[:, 24:25], accum_out=stat[:, 25:26]),
              inc=sACT)
            ge_n = sACT.n
            A("vector", [(sACT, sACT.n)],
              lambda e: e.reciprocal(stat[:, 26:27], stat[:, 25:26]), inc=sDVE)
            A("vector", [(sDVE, sDVE.n)],
              lambda e: e.tensor_scalar(stat[:, 32:40], stat[:, 32:40], stat[:, 26:27], None,
                                        op0=ALU.mult),
              inc=sDVE)
            A("vector", [(sDVE, sDVE.n)], lambda e: e.reduce_max(stat[:, 27:28], stat[:, 32:40], axis=AX.X),
              inc=sDVE)
            A("vector", [(sDVE, sDVE.n)],
              lambda e: e.tensor_scalar(stat[:, 40:48], stat[:, 32:40], stat[:, 27:28], None,
                                        op0=ALU.subtract),
              inc=sDVE)
            A("vector", [(sDVE, sDVE.n)],
              lambda e: e.tensor_scalar(stat[:, 40:48], stat[:, 40:48], -1e30, 1.0,
                                        op0=ALU.mult, op1=ALU.min),
              inc=sDVE)
            A("vector", [(sDVE, sDVE.n)],
              lambda e: e.tensor_scalar(stat[:, 40:48], stat[:, 40:48], 1.0, 1e30,
                                        op0=ALU.subtract, op1=ALU.mult),
              inc=sDVE)
            A("vector", [(sDVE, sDVE.n)], lambda e: e.tensor_add(stat[:, 40:48], stat[:, 32:40], stat[:, 40:48]),
              inc=sDVE)
            A("vector", [(sDVE, sDVE.n)], lambda e: e.reduce_max(stat[:, 28:29], stat[:, 40:48], axis=AX.X),
              inc=sDVE)
            A("vector", [(sDVE, sDVE.n)], lambda e: e.tensor_add(stat[:, 29:30], stat[:, 27:28], stat[:, 28:29]),
              inc=sDVE)
            A("vector", [(sDVE, sDVE.n)],
              lambda e: e.tensor_scalar(stat[:, 30:31], stat[:, 29:30], 1e-9, None, op0=ALU.add),
              inc=sDVE)
            A("vector", [(sDVE, sDVE.n)], lambda e: e.reciprocal(stat[:, 31:32], stat[:, 30:31]), inc=sDVE)
            A("vector", [(sDVE, sDVE.n)],
              lambda e, st=st: e.tensor_mul(factor[:, st:st + 1], stat[:, 29:30], stat[:, 31:32]),
              inc=sDVE)
            fdone.append(sDVE.n)
        b5cp = sCP.n

        ostore_free = {0: dX[0].n, 1: dX[1].n}

        def emit_b7(st):
            A("scalar", [(sDVE, sDVE.n)],
              lambda e, st=st: e.activation(
                  lnscr, X1R[:, st * D:(st + 1) * D], AF.Square, accum_out=stat[:, 50:51]),
              inc=sACT)
            A("vector", [(sACT, sACT.n)],
              lambda e, st=st: e.reduce_sum(stat[:, 49:50], X1R[:, st * D:(st + 1) * D], axis=AX.X),
              inc=sDVE)
            A("vector", [(sDVE, sDVE.n)],
              lambda e: e.tensor_scalar(stat[:, 51:52], stat[:, 49:50], 1.0 / D, None, op0=ALU.mult),
              inc=sDVE)
            A("vector", [(sDVE, sDVE.n)],
              lambda e: e.tensor_scalar(stat[:, 52:53], stat[:, 50:51], 1.0 / D, None, op0=ALU.mult),
              inc=sDVE)
            A("vector", [(sDVE, sDVE.n)], lambda e: e.tensor_mul(stat[:, 53:54], stat[:, 51:52], stat[:, 51:52]),
              inc=sDVE)
            A("vector", [(sDVE, sDVE.n)], lambda e: e.tensor_sub(stat[:, 54:55], stat[:, 52:53], stat[:, 53:54]),
              inc=sDVE)
            A("scalar", [(sDVE, sDVE.n)],
              lambda e: e.activation(stat[:, 55:56], stat[:, 54:55], AF.Ln, bias=epsA[:, 0:1]))
            A("scalar", [],
              lambda e: e.activation(stat[:, 56:57], stat[:, 55:56], AF.Exp, bias=0.0, scale=-0.5),
              inc=sACT)
            A("vector", [(sACT, sACT.n), (sDVE, sDVE.n), (dX[st % 2], ostore_free[st % 2])],
              lambda e, st=st: e.tensor_scalar(
                  outst[:, (st % 2) * D:(st % 2 + 1) * D], X1R[:, st * D:(st + 1) * D],
                  stat[:, 51:52], stat[:, 56:57], op0=ALU.subtract, op1=ALU.mult),
              inc=sDVE)
            A("sync", [(sDVE, sDVE.n)],
              lambda e, s=s, st=st: e.dma_start(
                  y_d[s, st * 128:(st + 1) * 128, :], outst[:, (st % 2) * D:(st % 2 + 1) * D]),
              inc=dX[st % 2], k=16)
            ostore_free[st % 2] = dX[st % 2].n

        up_free = {0: 0, 1: 0}
        dn_free = {0: 0, 1: 0}
        for ep in range(4):
            for c in range(8):
                if ep >= 2:
                    wts = [(sACT, sACT.n), (sPE, sPE.n)] if c == 0 else []
                else:
                    t = ep * 4 + c // 2
                    wts = [(sDVE, gd[t]), (sACT, sqa[t])]
                A("gpsimd", wts,
                  lambda e, c=c, ep=ep: e.dma_start(
                      upw[ep % 2][:, c * KB:(c + 1) * KB],
                      upT_d[c * 128:(c + 1) * 128, ep * KB:(ep + 1) * KB]),
                  inc=dUW[ep % 2], k=16)
            for cc in range(8):
                if ep >= 2:
                    wts = []
                else:
                    t = 8 + ep * 4 + cc // 2
                    wts = [(sDVE, gd[t]), (sACT, sqa[t])]
                A("gpsimd", wts,
                  lambda e, cc=cc, ep=ep: e.dma_start(
                      dww[ep % 2][:, cc * KB:(cc + 1) * KB],
                      downT_d[ep * KB + cc * 128: ep * KB + (cc + 1) * 128, :]),
                  inc=dUW[ep % 2], k=16)
            w_rdy = dUW[ep % 2].n
            for sc in range(4):
                for et in range(8):
                    bank = a(0) if et % 2 == 0 else a(1)
                    for c in range(8):
                        wts = []
                        if c == 0:
                            wts = [(dUW[ep % 2], w_rdy), (sACT, up_free[et % 2]),
                                   (sCP, gts[4 * sc + 3])]
                            if ep == 0 and sc == 0 and et == 0:
                                wts.append((sDVE, n2cp))
                        A("tensor", wts,
                          lambda e, c=c, et=et, sc=sc, ep=ep, bank=bank: e.matmul(
                              bank, upw[ep % 2][:, c * KB + et * 128: c * KB + (et + 1) * 128],
                              gT[c][:, sc * 512:(sc + 1) * 512],
                              start=(c == 0), stop=(c == 7)),
                          inc=(sPE if c == 7 else None))
                    A("scalar", [(sPE, sPE.n)],
                      lambda e, et=et, sc=sc, bank=bank: e.activation(
                          siluT[sc % 2][:, et * 512:(et + 1) * 512], bank, AF.Silu),
                      inc=sACT)
                    up_free[et % 2] = sACT.n
                silu_done = sACT.n
                for st2 in range(4):
                    st = sc * 4 + st2
                    bank = pB if st2 % 2 == 0 else pA[:, 1024:2048]
                    for et in range(8):
                        wts = []
                        if et == 0:
                            wts = [(sACT, silu_done), (sDVE, dn_free[st2 % 2]),
                                   (dUW[ep % 2], w_rdy)]
                            if ep == 0 and sc == 0 and st2 == 1:
                                wts.append((sCP, b5cp))
                        for half in range(2):
                            A("tensor", wts if half == 0 else [],
                              lambda e, et=et, half=half, st2=st2, sc=sc, ep=ep, bank=bank: e.matmul(
                                  bank[:, half * 512:(half + 1) * 512],
                                  siluT[sc % 2][:, et * 512 + st2 * 128: et * 512 + st2 * 128 + 128],
                                  dww[ep % 2][:, et * KB + half * 512: et * KB + (half + 1) * 512],
                                  start=(et == 0), stop=(et == 7)),
                              inc=(sPE if (et == 7 and half == 1) else None))
                    A("vector", [(sPE, sPE.n), (sDVE, fdone[st])],
                      lambda e, st=st, bank=bank: e.scalar_tensor_tensor(
                          X1R[:, st * D:(st + 1) * D], bank, factor[:, st:st + 1],
                          X1R[:, st * D:(st + 1) * D], op0=ALU.mult, op1=ALU.add),
                      inc=sDVE)
                    dn_free[st2 % 2] = sDVE.n
                    if ep == 3:
                        emit_b7(st)
        pl.flush(f"B6_s{s}")


    with nc.Block("fin") as blk:
        @blk.sync
        def _(e):
            e.wait_ge(sO.h, sO.n)
            e.wait_ge(dX[0].h, dX[0].n)
            e.wait_ge(dX[1].h, dX[1].n)

    nc.finalize()
    return nc


_NC_CACHE = {}


def _get_nc(upto):
    if upto not in _NC_CACHE:
        _NC_CACHE[upto] = build_program(upto)
    return _NC_CACHE[upto]


def make_consts():
    identbf = np.eye(128, dtype=ml_dtypes.bfloat16)
    identf = np.eye(128, dtype=np.float32)
    onescol = np.ones((1, 128), dtype=np.float32)
    onescolb = np.ones((128, 128), dtype=ml_dtypes.bfloat16)
    ones32 = np.ones((128, 32), dtype=ml_dtypes.bfloat16)
    i = np.arange(128).reshape(128, 1)
    c = np.arange(2048).reshape(1, 2048)
    rampA = (c - C0 - i).astype(np.float32)
    j = np.arange(128).reshape(1, 128)
    maskA = np.where(j > i, -1e30, 0.0).astype(np.float32)
    maskB = np.where(i > j, -1e30, 0.0).astype(np.float32)
    return identbf, identf, onescol, onescolb, ones32, rampA, maskA, maskB


def kernel(L0_attn_x, L0_attn_W_qk, L0_attn_W_v, L0_attn_alibi_slope,
           L0_ffn_n1_w, L0_ffn_qkv_w, L0_ffn_o_w, L0_ffn_n2_w,
           L0_ffn_gate_w, L0_ffn_up_w, L0_ffn_down_w, _upto="full", _ncores=8):
    X = np.ascontiguousarray(np.asarray(L0_attn_x, dtype=np.float32))
    wqk = np.ascontiguousarray(np.asarray(L0_attn_W_qk, dtype=np.float32))
    wvT = np.ascontiguousarray(np.asarray(L0_attn_W_v, dtype=np.float32).T)
    slope = np.asarray(L0_attn_alibi_slope, dtype=np.float32).reshape(1, 1)
    n1 = np.asarray(L0_ffn_n1_w, dtype=np.float32).reshape(1, D)
    n2 = np.asarray(L0_ffn_n2_w, dtype=np.float32).reshape(1, D)
    qkvT = np.ascontiguousarray(np.asarray(L0_ffn_qkv_w, dtype=np.float32).T)
    owT = np.ascontiguousarray(np.asarray(L0_ffn_o_w, dtype=np.float32).T)
    gateT = np.ascontiguousarray(np.asarray(L0_ffn_gate_w, dtype=np.float32).T)
    upT = np.ascontiguousarray(np.asarray(L0_ffn_up_w, dtype=np.float32).T)
    downT = np.ascontiguousarray(np.asarray(L0_ffn_down_w, dtype=np.float32).T)
    identbf, identf, onescol, onescolb, ones32, rampA, maskA, maskB = make_consts()

    nc = _get_nc(_upto)
    in_maps = []
    for i in range(_ncores):
        in_maps.append({
            "x": np.ascontiguousarray(X[2 * i:2 * i + 2]),
            "wqk": wqk, "wvT": wvT, "slope": slope, "n1": n1, "n2": n2,
            "qkvT": qkvT, "owT": owT, "gateT": gateT, "upT": upT, "downT": downT,
            "identbf": identbf, "identf": identf, "onescol": onescol,
            "onescolb": onescolb, "ones32": ones32,
            "rampA": rampA, "maskA": maskA, "maskB": maskB,
        })
    res = run_bass_kernel_spmd(nc, in_maps, core_ids=list(range(_ncores)))
    if _upto in ("A01", "setup", "A2d", "B23d"):
        return res.results[0]["dbg"]
    out = np.concatenate([r["y"] for r in res.results], axis=0)
    return out.reshape(2 * _ncores, S, D).astype(np.float32)


# version 19
# speedup vs baseline: 1.1699x; 1.0034x over previous
import os
import numpy as np
import ml_dtypes
import concourse.bass as bass
import concourse.mybir as mybir
from concourse.bass_utils import run_bass_kernel_spmd

F32 = mybir.dt.float32
BF16 = mybir.dt.bfloat16
AF = mybir.ActivationFunctionType
ALU = mybir.AluOpType
AX = mybir.AxisListType

S = 2048
D = 1024
NT = 16
H = 16
HD = 64
NP = 8
ED = 4096
NE = 8
NSEQ = 2
C0 = 1920
KB = 1024
G65 = H * 65


class Sem:
    def __init__(self, nc, name):
        self.cm = nc.semaphore(name)
        self.h = self.cm.__enter__()
        self.n = 0


class Plan:
    ENGS = ("sync", "gpsimd", "tensor", "scalar", "vector")

    def __init__(self, nc):
        self.nc = nc
        self.q = {e: [] for e in self.ENGS}

    def add(self, eng, waits, fn, inc=None, k=1):
        wcap = [(sm.h, v) for (sm, v) in waits]
        if inc is not None:
            inc.n += k
        ih = None if inc is None else inc.h

        def run(e, wcap=wcap, fn=fn, ih=ih, k=k):
            for (h, v) in wcap:
                if v > 0:
                    e.wait_ge(h, v)
            ins = fn(e)
            if ih is not None:
                ins.then_inc(ih, k)
        self.q[eng].append(run)

    def flush(self, name=None):
        q = self.q
        self.q = {e: [] for e in self.ENGS}
        with self.nc.Block(name) as blk:
            for ename in self.ENGS:
                fns = q[ename]
                if not fns:
                    continue

                def body(eng, fns=fns):
                    for f in fns:
                        f(eng)
                getattr(blk, ename)(body)


def build_program(upto="full"):
    nc = bass.Bass("TRN2", target_bir_lowering=False)

    x_d = nc.dram_tensor("x", [NSEQ, S, D], F32, kind="ExternalInput").ap()
    wqk_d = nc.dram_tensor("wqk", [D, D], F32, kind="ExternalInput").ap()
    wvT_d = nc.dram_tensor("wvT", [D, D], F32, kind="ExternalInput").ap()
    slope_d = nc.dram_tensor("slope", [1, 1], F32, kind="ExternalInput").ap()
    n1_d = nc.dram_tensor("n1", [1, D], F32, kind="ExternalInput").ap()
    n2_d = nc.dram_tensor("n2", [1, D], F32, kind="ExternalInput").ap()
    qkvT_d = nc.dram_tensor("qkvT", [D, 3 * D], F32, kind="ExternalInput").ap()
    owT_d = nc.dram_tensor("owT", [D, D], F32, kind="ExternalInput").ap()
    gateT_d = nc.dram_tensor("gateT", [D, NE], F32, kind="ExternalInput").ap()
    upT_d = nc.dram_tensor("upT", [D, ED], F32, kind="ExternalInput").ap()
    downT_d = nc.dram_tensor("downT", [ED, D], F32, kind="ExternalInput").ap()
    identbf_d = nc.dram_tensor("identbf", [128, 128], BF16, kind="ExternalInput").ap()
    identf_d = nc.dram_tensor("identf", [128, 128], F32, kind="ExternalInput").ap()
    onescol_d = nc.dram_tensor("onescol", [1, 128], F32, kind="ExternalInput").ap()
    onescolb_d = nc.dram_tensor("onescolb", [128, 128], BF16, kind="ExternalInput").ap()
    ones32_d = nc.dram_tensor("ones32", [128, 32], BF16, kind="ExternalInput").ap()
    rampA_d = nc.dram_tensor("rampA", [128, 2048], F32, kind="ExternalInput").ap()
    maskA_d = nc.dram_tensor("maskA", [128, 128], F32, kind="ExternalInput").ap()
    maskB_d = nc.dram_tensor("maskB", [128, 128], F32, kind="ExternalInput").ap()
    y_d = nc.dram_tensor("y", [NSEQ, S, D], F32, kind="ExternalOutput").ap()
    dbg_d = nc.dram_tensor("dbg", [128, 4096], F32, kind="ExternalOutput").ap()

    X1R = nc.alloc_sbuf_tensor("X1R", [128, NT * D], F32).ap()
    factor = nc.alloc_sbuf_tensor("factor", [128, NT], F32).ap()
    maskA = nc.alloc_sbuf_tensor("maskA_s", [128, 128], F32).ap()
    maskB = nc.alloc_sbuf_tensor("maskB_s", [128, 128], F32).ap()
    identbf = nc.alloc_sbuf_tensor("identbf_s", [128, 128], BF16).ap()
    identf = nc.alloc_sbuf_tensor("identf_s", [128, 128], F32).ap()
    onescol = nc.alloc_sbuf_tensor("onescol_s", [1, 128], F32).ap()
    onescolb = nc.alloc_sbuf_tensor("onescolb_s", [128, 128], BF16).ap()
    ones32 = nc.alloc_sbuf_tensor("ones32_s", [128, 32], BF16).ap()
    slope_sb = nc.alloc_sbuf_tensor("slope_sb", [1, 32], F32).ap()
    slope_bc = nc.alloc_sbuf_tensor("slope_bc", [128, 1], F32).ap()
    epsA = nc.alloc_sbuf_tensor("epsA", [128, 8], F32).ap()
    eps1 = nc.alloc_sbuf_tensor("eps1", [1, 8], F32).ap()
    stat = nc.alloc_sbuf_tensor("stat", [128, 64], F32).ap()
    mu1 = nc.alloc_sbuf_tensor("mu1", [1, 16], F32).ap()
    mu1b = nc.alloc_sbuf_tensor("mu1b", [8, 8], BF16).ap()
    nbr = nc.alloc_sbuf_tensor("nbr", [128, 8], F32).ap()
    nbrb = nc.alloc_sbuf_tensor("nbrb", [128, 8], BF16).ap()
    biasH = nc.alloc_sbuf_tensor("biasH", [128, 2], F32).ap()

    TOP = 229344
    ARENA = 138 * KB
    AB = TOP - ARENA

    def at(name, off, shape, dtype):
        esz = 4 if dtype == F32 else 2
        assert off + shape[1] * esz <= ARENA, (name, off, shape)
        return nc.alloc_sbuf_tensor_at(name, shape, dtype, offset=AB + off).ap()

    XT = [at(f"XT{c}", c * 4 * KB, [128, S], BF16) for c in range(8)]
    Vsb = at("Vsb", 32 * KB, [128, NT * D], BF16)
    wslot = at("wslot", 64 * KB, [128, 8 * D], BF16)
    Sstrip = [at(f"Sstrip{i}", 80 * KB + i * 8 * KB, [128, S], F32) for i in range(2)]
    PTring = at("PTring", 104 * KB, [128, 4 * 128], BF16)
    qpt = [at(f"qpt{m}", 105 * KB + m * 512, [128, 256], BF16) for m in range(8)]
    xres = [at(f"xres{i}", 109 * KB + i * 4 * KB, [128, D], F32) for i in range(2)]
    xbf = [at(f"xbf{i}", 117 * KB + i * 2 * KB, [128, D], BF16) for i in range(2)]
    Gfull = at("Gfull", 121 * KB, [128, 2048], F32)
    sqscrA = at("sqscrA", 129 * KB, [128, D], BF16)

    h1T = [at(f"h1T{c}", c * 4 * KB, [128, S], BF16) for c in range(8)]
    Vplus = at("Vplus", 32 * KB, [128, NT * G65], BF16)
    qkpT = [at(f"qkpT{i}", 65 * KB + i * 8 * KB, [128, 2 * S], BF16) for i in range(2)]
    negMrowA = at("negMrowA", 81 * KB, [65, S], BF16)
    qkvw = [at(f"qkvw{i}", 89 * KB + i * 4 * KB, [128, 2048], BF16) for i in range(2)]
    Ebuf2 = [at(f"Ebuf2_{i}", 97 * KB + i * KB, [128, 512], BF16) for i in range(3)]
    n1rep = at("n1rep", 100 * KB, [128, D], F32)
    bscr = at("bscr", 104 * KB, [128, D], F32)
    n1row = at("n1row", 108 * KB, [1, D], F32)
    vwslot = at("vwslot", 65 * KB, [128, 8 * D], BF16)
    rsrow = at("rsrow", 100 * KB, [1, 512], F32)
    rsb = at("rsb", 102 * KB, [64, 512], F32)
    todd = at("todd", 104 * KB, [64, 2 * 256], BF16)
    negrow = at("negrow", 105 * KB, [32, 128], BF16)
    mnegbf = at("mnegbf", 105 * KB + 512, [128, 32], F32)
    attnOT = [at(f"attnOT{c}", 106 * KB + c * 4 * KB, [128, S], BF16) for c in range(8)]
    dbgscr = at("dbgscr", 100 * KB, [128, 1024], F32)

    X2 = at("X2", 0, [128, NT * D], F32)
    owT = at("owT", 65 * KB, [128, 8 * D], BF16)
    n2rep = at("n2rep", 81 * KB, [128, D], F32)
    bscr2 = at("bscr2", 85 * KB, [128, D], F32)
    gbf = at("gbf", 89 * KB, [128, D], BF16)
    gateT_sb = at("gateT_sb", 91 * KB, [128, NE * 8], BF16)
    n2row = at("n2row", 92 * KB, [1, D], F32)
    gT = [at(f"gT{c}", 106 * KB + c * 4 * KB, [128, S], BF16) for c in range(8)]

    upw = [at(f"upw{i}", i * 16 * KB, [128, 8 * KB], BF16) for i in range(2)]
    dww = [at(f"dww{i}", 32 * KB + i * 16 * KB, [128, 8 * KB], BF16) for i in range(2)]
    siluT = [at(f"siluT{i}", 65 * KB + i * 8 * KB, [128, 8 * 512], BF16) for i in range(2)]
    lnscr = at("lnscr", 81 * KB, [128, D], BF16)
    outst = at("outst", 85 * KB, [128, 2 * D], F32)

    assert nc.sbuf_base <= AB, (nc.sbuf_base, AB)

    pA = nc.alloc_psum_tensor("pA", [128, 2048], F32).ap()
    pB = nc.alloc_psum_tensor("pB", [128, 1024], F32).ap()
    pC = nc.alloc_psum_tensor("pC", [128, 512], F32).ap()
    pT = nc.alloc_psum_tensor("pT", [128, 512], F32).ap()

    def a(i):
        return pA[:, i * 512:(i + 1) * 512]

    sG = Sem(nc, "sG")
    sS = Sem(nc, "sS")
    sO = Sem(nc, "sO")
    sPE = Sem(nc, "sPE")
    sACT = Sem(nc, "sACT")
    sDVE = Sem(nc, "sDVE")
    sCP = Sem(nc, "sCP")
    dX = [Sem(nc, "dX0"), Sem(nc, "dX1")]
    dQW = [Sem(nc, "dQW0"), Sem(nc, "dQW1")]
    dUW = [Sem(nc, "dUW0"), Sem(nc, "dUW1")]
    dTD = [Sem(nc, "dTD0"), Sem(nc, "dTD1")]
    dY = [Sem(nc, "dY0"), Sem(nc, "dY1")]
    dWS = Sem(nc, "dWS")

    pl = Plan(nc)
    A = pl.add

    A("sync", [], lambda e: e.dma_start(maskA, maskA_d), inc=sS, k=16)
    A("sync", [], lambda e: e.dma_start(maskB, maskB_d), inc=sS, k=16)
    A("sync", [], lambda e: e.dma_start(identbf, identbf_d), inc=sS, k=16)
    A("sync", [], lambda e: e.dma_start(identf, identf_d), inc=sS, k=16)
    A("sync", [], lambda e: e.dma_start(onescol, onescol_d), inc=sS, k=16)
    A("sync", [], lambda e: e.dma_start(onescolb, onescolb_d), inc=sS, k=16)
    A("sync", [], lambda e: e.dma_start(ones32, ones32_d), inc=sS, k=16)
    A("sync", [], lambda e: e.dma_start(slope_sb[0:1, 0:1], slope_d), inc=sS, k=16)
    A("vector", [], lambda e: e.memset(epsA[:, 0:1], 1e-5))
    A("vector", [], lambda e: e.memset(epsA[:, 1:2], 1e-6))
    A("vector", [], lambda e: e.memset(eps1[0:1, 0:1], 1e-38), inc=sDVE)
    A("tensor", [(sS, sS.n)],
      lambda e: e.matmul(pC[:, 0:1], onescol[0:1, 0:128], slope_sb[0:1, 0:1],
                         start=True, stop=True), inc=sPE)
    A("vector", [(sPE, sPE.n)], lambda e: e.tensor_copy(slope_bc, pC[:, 0:1]), inc=sDVE)
    A("vector", [(sS, sS.n)], lambda e: e.memset(epsA[:, 3:4], 0.0), inc=sDVE)
    pl.flush("setup")

    if upto == "setup":
        A("vector", [], lambda e: e.tensor_copy(outst[:, 0:1], slope_bc), inc=sDVE)
        A("vector", [], lambda e: e.tensor_copy(outst[:, 1:9], epsA), inc=sDVE)
        A("vector", [], lambda e: e.tensor_copy(outst[:, 9:137], maskA), inc=sDVE)
        A("sync", [(sDVE, sDVE.n)], lambda e: e.dma_start(dbg_d[:, 0:137], outst[:, 0:137]),
          inc=sO, k=16)
        pl.flush("dbgsetup")

    for s in range(NSEQ if upto != "setup" else 0):
        A("sync", [], lambda e: e.dma_start(Gfull, rampA_d), inc=sS, k=16)
        A("scalar", [(sS, sS.n), (sDVE, sDVE.n)],
          lambda e: e.activation(Gfull, Gfull, AF.Copy, bias=0.0, scale=slope_bc[:, 0:1]),
          inc=sACT)
        A("gpsimd", [],
          lambda e: e.dma_start(wslot.rearrange("p (c j) -> p c j", c=8),
                                wvT_d.rearrange("(c p) j -> p c j", p=128)),
          inc=dWS, k=16)
        wv_ready = dWS.n

        _tiles = int(os.environ.get("KDBG_TILES", NT))
        _nov = int(os.environ.get("KDBG_NOV", "0"))
        _notr = int(os.environ.get("KDBG_NOTR", "0"))
        trslot = [pA[:, 1024:1152], pA[:, 1536:1664]]
        tr_done = {0: 0, 1: 0}
        vv_free = {0: 0, 1: 0}
        for st in range(_tiles):
            w = [(sPE, tr_done[st % 2])] if st >= 2 else []
            A("gpsimd", w,
              lambda e, s=s, st=st: e.dma_start(xres[st % 2], x_d[s, st * 128:(st + 1) * 128, :]),
              inc=dX[st % 2], k=16)
            xw = dX[st % 2].n
            for c in range(0 if _notr else 8):
                wts = [(sCP, max(0, sCP.n - 1))]
                if c == 0:
                    wts.append((dX[st % 2], xw))
                A("tensor", wts,
                  lambda e, st=st, c=c: e.transpose(
                      trslot[c % 2], xres[st % 2][:, c * 128:(c + 1) * 128], identf),
                  inc=sPE)
                A("vector", [(sPE, sPE.n)],
                  lambda e, st=st, c=c: e.tensor_copy(
                      XT[c][:, st * 128:(st + 1) * 128], trslot[c % 2]),
                  inc=sCP)
            tr_done[st % 2] = sPE.n
            vbank = pB if st % 2 == 0 else pA[:, 0:1024]
            for c in range(0 if _nov else 8):
                wts = []
                if c == 0:
                    wts = [(sCP, sCP.n), (dWS, wv_ready)]
                    if st >= 2:
                        wts.append((sACT, vv_free[st % 2]))
                for half in range(2):
                    A("tensor", wts if half == 0 else [],
                      lambda e, c=c, half=half, st=st, vbank=vbank: e.matmul(
                          vbank[:, half * 512:(half + 1) * 512],
                          XT[c][:, st * 128:(st + 1) * 128],
                          wslot[:, c * D + half * 512: c * D + (half + 1) * 512],
                          start=(c == 0), stop=(c == 7)),
                      inc=(sPE if (c == 7 and half == 1) else None))
            if not _nov:
                A("scalar", [(sPE, sPE.n)],
                  lambda e, st=st, vbank=vbank: e.activation(Vsb[:, st * D:(st + 1) * D], vbank, AF.Copy),
                  inc=sACT)
                vv_free[st % 2] = sACT.n
        pl.flush(f"A01_s{s}")

        if upto == "A01" and s == 0:
            A("vector", [], lambda e: e.tensor_copy(outst[:, 0:1024], Vsb[:, 0:1024]), inc=sDVE)
            A("sync", [(sDVE, sDVE.n)], lambda e: e.dma_start(dbg_d[:, 0:1024], outst[:, 0:1024]),
              inc=sO, k=16)
            A("vector", [], lambda e: e.tensor_copy(outst[:, 1024:2048], XT[0][:, 0:1024]), inc=sDVE)
            A("sync", [(sDVE, sDVE.n)], lambda e: e.dma_start(dbg_d[:, 1024:2048], outst[:, 1024:2048]),
              inc=sO, k=16)
            A("sync", [(sACT, sACT.n)], lambda e: e.dma_start(dbg_d[:, 2048:4096], Gfull), inc=sO, k=16)
            pl.flush("dbgA01")
            break

        A("gpsimd", [],
          lambda e: e.dma_start(wslot.rearrange("p (c j) -> p c j", c=8),
                                wqk_d.rearrange("(c p) j -> p c j", p=128)),
          inc=dWS, k=16)
        wq_ready = dWS.n

        qpt_done = 0
        qp_free = {0: 0, 1: 0}
        sc_free = {0: 0, 1: 0}
        xr_free = {0: 0, 1: 0}
        yv_free = 0
        tr_fin = {0: 0, 1: 0}
        _qis = int(os.environ.get("KDBG_QIS", NT))
        for qi in range(_qis):
            kend = (qi + 1) * 128
            nkc = (kend + 511) // 512
            buf = qi % 2
            A("sync", [(sDVE, xr_free[qi % 2])] if qi >= 2 else [],
              lambda e, s=s, qi=qi: e.dma_start(xres[qi % 2], x_d[s, qi * 128:(qi + 1) * 128, :]),
              inc=dX[qi % 2], k=16)
            xres_rdy = dX[qi % 2].n

            if qi % 2 == 0:
                for m in range(8):
                    bank = a(0) if m % 2 == 0 else a(1)
                    for c in range(8):
                        wts = []
                        if c == 0:
                            wts = [(dWS, wq_ready), (sACT, qp_free[m % 2])]
                        A("tensor", wts,
                          lambda e, m=m, c=c, qi=qi, bank=bank: e.matmul(
                              bank[:, 0:256],
                              wslot[:, c * D + m * 128: c * D + (m + 1) * 128],
                              XT[c][:, qi * 128: qi * 128 + 256],
                              start=(c == 0), stop=(c == 7)),
                          inc=(sPE if c == 7 else None))
                    A("scalar", [(sPE, sPE.n)],
                      lambda e, m=m, bank=bank: e.activation(
                          qpt[m], bank[:, 0:256], AF.Copy, bias=0.0, scale=1.0 / 32.0),
                      inc=sACT)
                    qp_free[m % 2] = sACT.n
                qpt_done = sACT.n

            for kc in range(nkc):
                w = min(512, kend - kc * 512)
                bank = a(2) if kc % 2 == 0 else a(3)
                for c in range(8):
                    wts = []
                    if c == 0:
                        wts = [(sACT, qpt_done), (sDVE, sc_free[kc % 2])]
                    A("tensor", wts,
                      lambda e, c=c, kc=kc, qi=qi, w=w, bank=bank: e.matmul(
                          bank[:, 0:w],
                          qpt[c][:, (qi % 2) * 128:(qi % 2) * 128 + 128],
                          XT[c][:, kc * 512: kc * 512 + w],
                          start=(c == 0), stop=(c == 7)),
                      inc=(sPE if c == 7 else None))
                off = kc * 512 - qi * 128 + C0
                sw = [(sPE, sPE.n)]
                if kc == 0:
                    sw.append((sCP, tr_fin[qi % 2]))
                A("vector", sw,
                  lambda e, kc=kc, w=w, off=off, buf=buf, bank=bank: e.tensor_add(
                      Sstrip[buf][:, kc * 512: kc * 512 + w], bank[:, 0:w],
                      Gfull[:, off: off + w]),
                  inc=sDVE)
                sc_free[kc % 2] = sDVE.n
                if kc == nkc - 1:
                    A("vector", [],
                      lambda e, qi=qi, buf=buf: e.tensor_add(
                          Sstrip[buf][:, qi * 128: qi * 128 + 128],
                          Sstrip[buf][:, qi * 128: qi * 128 + 128], maskA),
                      inc=sDVE)
            A("vector", [(sDVE, sDVE.n)],
              lambda e, buf=buf, kend=kend: e.reduce_max(
                  stat[:, 4:5], Sstrip[buf][:, 0:kend], axis=AX.X, negate=True),
              inc=sDVE)
            A("scalar", [(sDVE, sDVE.n)],
              lambda e, buf=buf, kend=kend: e.activation(
                  Sstrip[buf][:, 0:kend], Sstrip[buf][:, 0:kend], AF.Exp,
                  bias=stat[:, 4:5], accum_out=stat[:, 5:6]),
              inc=sACT)
            exp_done = sACT.n
            A("vector", [(sACT, exp_done)],
              lambda e: e.reciprocal(stat[:, 6:7], stat[:, 5:6]), inc=sDVE)

            ntk = qi + 1
            cps = []

            def emit_tr(kt, buf=buf):
                A("tensor", [(sACT, exp_done), (sCP, sCP.n)],
                  lambda e, kt=kt, buf=buf: e.transpose(
                      pT[:, (kt % 4) * 128:(kt % 4 + 1) * 128],
                      Sstrip[buf][:, kt * 128:(kt + 1) * 128], identf),
                  inc=sPE)
                A("vector", [(sPE, sPE.n)],
                  lambda e, kt=kt: e.tensor_copy(
                      PTring[:, (kt % 4) * 128:(kt % 4 + 1) * 128],
                      pT[:, (kt % 4) * 128:(kt % 4 + 1) * 128]),
                  inc=sCP)
                cps.append(sCP.n)

            def emit_av(kt, qi=qi):
                for half in range(2):
                    wts = []
                    if half == 0:
                        wts = [(sCP, cps[kt])]
                        if kt == 0:
                            wts.append((sDVE, yv_free))
                    A("tensor", wts,
                      lambda e, kt=kt, half=half, qi=qi: e.matmul(
                          pB[:, half * 512:(half + 1) * 512],
                          PTring[:, (kt % 4) * 128:(kt % 4 + 1) * 128],
                          Vsb[:, kt * D + half * 512: kt * D + (half + 1) * 512],
                          start=(kt == 0), stop=(kt == qi)),
                      inc=(sPE if (kt == qi and half == 1) else None))

            for kt in range(ntk):
                emit_tr(kt)
                if kt >= 1:
                    emit_av(kt - 1)
            emit_av(ntk - 1)
            tr_fin[qi % 2] = cps[-1]
            av_done = sPE.n

            A("vector", [(sPE, av_done), (dX[qi % 2], xres_rdy)],
              lambda e, qi=qi: e.scalar_tensor_tensor(
                  xres[qi % 2], pB, stat[:, 6:7], xres[qi % 2], op0=ALU.mult, op1=ALU.add),
              inc=sDVE)
            yv_free = sDVE.n
            ydone = sDVE.n
            A("scalar", [(sDVE, ydone)],
              lambda e, qi=qi: e.activation(sqscrA, xres[qi % 2], AF.Square, accum_out=stat[:, 8:9]),
              inc=sACT)
            A("vector", [],
              lambda e, qi=qi: e.reduce_sum(stat[:, 7:8], xres[qi % 2], axis=AX.X), inc=sDVE)
            A("vector", [(sDVE, sDVE.n)],
              lambda e: e.tensor_scalar(stat[:, 9:10], stat[:, 7:8], 1.0 / D, None, op0=ALU.mult),
              inc=sDVE)
            A("vector", [(sACT, sACT.n), (sDVE, sDVE.n)],
              lambda e: e.tensor_scalar(stat[:, 10:11], stat[:, 8:9], 1.0 / D, None, op0=ALU.mult),
              inc=sDVE)
            A("vector", [(sDVE, sDVE.n)],
              lambda e: e.tensor_mul(stat[:, 11:12], stat[:, 9:10], stat[:, 9:10]),
              inc=sDVE)
            A("vector", [(sDVE, sDVE.n)],
              lambda e: e.tensor_sub(stat[:, 12:13], stat[:, 10:11], stat[:, 11:12]),
              inc=sDVE)
            A("scalar", [(sDVE, sDVE.n)],
              lambda e: e.activation(stat[:, 13:14], stat[:, 12:13], AF.Ln, bias=epsA[:, 0:1]))
            A("scalar", [],
              lambda e: e.activation(stat[:, 14:15], stat[:, 13:14], AF.Exp, bias=0.0, scale=-0.5),
              inc=sACT)
            A("vector", [(sACT, sACT.n), (sDVE, sDVE.n)],
              lambda e, qi=qi: e.tensor_scalar(
                  X1R[:, qi * D:(qi + 1) * D], xres[qi % 2],
                  stat[:, 9:10], stat[:, 14:15], op0=ALU.subtract, op1=ALU.mult),
              inc=sDVE)
            xr_free[qi % 2] = sDVE.n
        pl.flush(f"A2_s{s}")

        if upto == "A2d" and s == 0:
            A("sync", [(sDVE, sDVE.n), (sACT, sACT.n)],
              lambda e: e.dma_start(dbg_d[:, 0:2048], Sstrip[0]), inc=sO, k=16)
            A("vector", [(sO, sO.n)], lambda e: e.tensor_copy(Sstrip[1], Sstrip[0]), inc=sDVE)
            A("vector", [], lambda e: e.tensor_copy(Sstrip[1][:, 0:64], stat), inc=sDVE)
            A("sync", [(sDVE, sDVE.n)], lambda e: e.dma_start(dbg_d[:, 2048:4096], Sstrip[1]), inc=sO, k=16)
            pl.flush("dbgA2")
            break

        if upto == "X1":
            for st in range(NT):
                A("sync", [(sDVE, sDVE.n)],
                  lambda e, s=s, st=st: e.dma_start(
                      y_d[s, st * 128:(st + 1) * 128, :], X1R[:, st * D:(st + 1) * D]),
                  inc=sO, k=16)
            A("vector", [(sO, sO.n)], lambda e: e.memset(epsA[:, 4:5], 0.0), inc=sDVE)
            pl.flush(f"dbgX1_s{s}")
            continue

        A("sync", [], lambda e: e.dma_start(n1row, n1_d), inc=sS, k=16)
        n1_rdy = sS.n
        A("tensor", [(sS, n1_rdy)],
          lambda e: e.matmul(pC, onescol[0:1, 0:128], n1row[0:1, 0:512], start=True, stop=True))
        A("tensor", [],
          lambda e: e.matmul(a(0), onescol[0:1, 0:128], n1row[0:1, 512:1024], start=True, stop=True),
          inc=sPE)
        A("vector", [(sPE, sPE.n)], lambda e: e.tensor_copy(n1rep[:, 0:512], pC), inc=sDVE)
        A("vector", [], lambda e: e.tensor_copy(n1rep[:, 512:1024], a(0)), inc=sDVE)
        n1cp = sDVE.n
        h1ts = []
        h1ds = []

        for st in range(NT):
            A("scalar", [(sDVE, h1ds[st - 1] if st else sDVE.n)],
              lambda e, st=st: e.activation(
                  gbf, X1R[:, st * D:(st + 1) * D], AF.Square, accum_out=stat[:, 16:17]),
              inc=sACT)
            A("scalar", [],
              lambda e: e.activation(stat[:, 17:18], stat[:, 16:17], AF.Ln,
                                     bias=epsA[:, 1:2], scale=1.0 / D))
            A("scalar", [],
              lambda e: e.activation(stat[:, 18:19], stat[:, 17:18], AF.Exp, bias=0.0, scale=-0.5),
              inc=sACT)
            A("vector", [(sACT, sACT.n)],
              lambda e, st=st: e.tensor_scalar(
                  bscr, X1R[:, st * D:(st + 1) * D], stat[:, 18:19], None, op0=ALU.mult),
              inc=sDVE)
            A("vector", [], lambda e: e.tensor_mul(bscr, bscr, n1rep), inc=sDVE)
            h1done = sDVE.n
            h1ds.append(h1done)
            trslot = [pA[:, 1024:1152], pA[:, 1536:1664]]
            for c in range(8):
                wts = [(sCP, max(0, sCP.n - 1))]
                if c == 0:
                    wts.append((sDVE, h1done))
                A("tensor", wts,
                  lambda e, c=c, trslot=trslot: e.transpose(
                      trslot[c % 2], bscr[:, c * 128:(c + 1) * 128], identf),
                  inc=sPE)
                A("vector", [(sPE, sPE.n)],
                  lambda e, st=st, c=c, trslot=trslot: e.tensor_copy(
                      h1T[c][:, st * 128:(st + 1) * 128], trslot[c % 2]),
                  inc=sCP)
            h1ts.append(sCP.n)

        A("gpsimd", [],
          lambda e: e.dma_start(vwslot.rearrange("p (c j) -> p c j", c=8),
                                qkvT_d[:, 2 * D:3 * D].rearrange("(c p) j -> p c j", p=128)),
          inc=dWS, k=16)
        vw_rdy = dWS.n
        b1_free = {0: 0, 1: 0}
        for st in range(NT):
            vbank = pB if st % 2 == 0 else pA[:, 0:1024]
            for c in range(8):
                wts = []
                if c == 0:
                    wts = [(dWS, vw_rdy), (sCP, h1ts[st])]
                    if st == 1:
                        wts.append((sDVE, n1cp))
                    if st >= 2:
                        wts.append((sDVE, b1_free[st % 2]))
                for half in range(2):
                    A("tensor", wts if half == 0 else [],
                      lambda e, c=c, half=half, st=st, vbank=vbank: e.matmul(
                          vbank[:, half * 512:(half + 1) * 512],
                          h1T[c][:, st * 128:(st + 1) * 128],
                          vwslot[:, c * D + half * 512: c * D + (half + 1) * 512],
                          start=(c == 0), stop=(c == 7)),
                      inc=(sPE if (c == 7 and half == 1) else None))
            A("vector", [(sPE, sPE.n)],
              lambda e, st=st, vbank=vbank: e.tensor_copy(
                  Vplus[:, st * G65:(st + 1) * G65].rearrange("p (g w) -> p g w", w=65)[:, :, 0:64],
                  vbank.rearrange("p (g w) -> p g w", w=64)),
              inc=sDVE)
            A("vector", [],
              lambda e, st=st: e.memset(
                  Vplus[:, st * G65:(st + 1) * G65].rearrange("p (g w) -> p g w", w=65)[:, :, 64:65],
                  1.0),
              inc=sDVE)
            b1_free[st % 2] = sDVE.n
        pl.flush(f"B1_s{s}")

        ebuf_ready = [sPE.n, sPE.n, sPE.n]
        stb_free = {0: 0, 1: 0}
        av_free = {0: 0, 1: 0}
        todd_free = {0: 0, 1: 0}
        ps = {"qkw_free": {0: 0, 1: 0}, "qkpT_free": {0: 0, 1: 0},
              "pc_w": [], "pt_w": [], "evac": {}, "wrdy": {}, "ops": {}}

        def emit_qkdma(r):
            pb = r % 2
            for c in range(8):
                wts = [(sPE, ps["qkw_free"][pb])] if c == 0 else []
                A("gpsimd", wts,
                  lambda e, c=c, r=r, pb=pb: e.dma_start(
                      qkvw[pb][:, c * 256: c * 256 + 128],
                      qkvT_d[c * 128:(c + 1) * 128, r * 128:(r + 1) * 128]),
                  inc=dQW[pb], k=16)
                A("gpsimd", [],
                  lambda e, c=c, r=r, pb=pb: e.dma_start(
                      qkvw[pb][:, c * 256 + 128: c * 256 + 256],
                      qkvT_d[c * 128:(c + 1) * 128, D + r * 128: D + (r + 1) * 128]),
                  inc=dQW[pb], k=16)
            ps["wrdy"][r] = dQW[pb].n

        def emit_qkgrp(r, tgt, qq):
            pb = r % 2
            for c in range(8):
                wts = ([(dQW[pb], ps["wrdy"][r])] + ps["pc_w"]) if c == 0 else []
                A("tensor", wts,
                  lambda e, c=c, tgt=tgt, qq=qq, pb=pb: e.matmul(
                      pC[:, 0:512],
                      qkvw[pb][:, c * 256 + tgt * 128: c * 256 + tgt * 128 + 128],
                      h1T[c][:, qq * 512:(qq + 1) * 512],
                      start=(c == 0), stop=(c == 7)),
                  inc=(sPE if c == 7 else None))
            ps["qkw_free"][pb] = sPE.n
            A("scalar", [(sPE, sPE.n), (sPE, ps["qkpT_free"][pb])],
              lambda e, tgt=tgt, qq=qq, pb=pb: e.activation(
                  qkpT[pb][:, tgt * S + qq * 512: tgt * S + (qq + 1) * 512],
                  pC[:, 0:512], AF.Copy, bias=0.0, scale=(0.125 if tgt == 0 else 1.0)),
              inc=sACT)
            ps["evac"][(pb, tgt, qq)] = sACT.n
            ps["pc_w"] = [(sACT, sACT.n)]

        def emit_tmchunk(r, qt, kc):
            pb = r % 2
            kend2 = (qt + 1) * 128
            nkc2 = (kend2 + 511) // 512
            w2 = min(512, kend2 - kc * 512)
            wts = [(sACT, ps["evac"][(pb, 0, qt // 4)]),
                   (sACT, ps["evac"][(pb, 1, kc)])] + ps["pc_w"]
            A("tensor", wts,
              lambda e, qt=qt, kc=kc, w2=w2, pb=pb: e.matmul(
                  pC[:, 0:w2],
                  qkpT[pb][0:64, qt * 128:(qt + 1) * 128],
                  qkpT[pb][0:64, S + kc * 512: S + kc * 512 + w2],
                  start=True, stop=True, tile_position=(0, 0)))
            A("tensor", list(ps["pt_w"]),
              lambda e, qt=qt, kc=kc, w2=w2, pb=pb: e.matmul(
                  pT[:, 0:w2],
                  qkpT[pb][64:128, qt * 128:(qt + 1) * 128],
                  qkpT[pb][64:128, S + kc * 512: S + kc * 512 + w2],
                  start=True, stop=True, tile_position=(64, 0)),
              inc=sPE)
            mm2 = sPE.n
            if kc == nkc2 - 1:
                doff = qt * 128 - kc * 512
                A("vector", [(sPE, mm2)],
                  lambda e, doff=doff: e.tensor_add(
                      pC[:, doff:doff + 128], pC[:, doff:doff + 128], maskA),
                  inc=sDVE)
                A("vector", [],
                  lambda e, doff=doff: e.tensor_add(
                      pT[:, doff:doff + 128], pT[:, doff:doff + 128], maskA),
                  inc=sDVE)
            A("vector", [(sPE, mm2), (sDVE, sDVE.n)],
              lambda e, w2=w2: e.reduce_max(stat[:, 32:33], pC[:, 0:w2], axis=AX.X),
              inc=sDVE)
            ps["pc_w"] = [(sDVE, sDVE.n)]
            A("vector", [],
              lambda e, w2=w2: e.reduce_max(stat[:, 33:34], pT[:, 0:w2], axis=AX.X),
              inc=sDVE)
            ps["pt_w"] = [(sDVE, sDVE.n)]
            if kc == 0:
                A("vector", [(sDVE, sDVE.n)],
                  lambda e: e.tensor_copy(stat[:, 34:36], stat[:, 32:34]), inc=sDVE)
            else:
                A("vector", [(sDVE, sDVE.n)],
                  lambda e: e.tensor_max(stat[:, 34:36], stat[:, 34:36], stat[:, 32:34]),
                  inc=sDVE)
            if kc == nkc2 - 1:
                for hh in range(2):
                    mc = hh * 16 + qt
                    A("vector", [(sDVE, sDVE.n)],
                      lambda e, mc=mc, hh=hh: e.tensor_scalar(
                          mnegbf[:, mc: mc + 1],
                          stat[:, 34 + hh:35 + hh], -1.0, None, op0=ALU.mult),
                      inc=sDVE)

        def emit_dance(r):
            A("tensor", [(sDVE, sDVE.n)] + list(ps["pt_w"]),
              lambda e: e.transpose(pT[0:32, 0:128], mnegbf[:, 0:32], identf),
              inc=sPE)
            tpn = sPE.n
            cw = [(sPE, tpn)]
            if r >= 1:
                cw.append((sS, ps["ops"][r - 1]))
            A("vector", cw,
              lambda e: e.tensor_copy(negrow, pT[0:32, 0:128]), inc=sDVE)
            ps["pt_w"] = [(sDVE, sDVE.n)]
            A("sync", [(sDVE, sDVE.n)],
              lambda e: e.dma_start(negMrowA[0:1, 0:S], negrow[0:16, :]),
              inc=sS, k=16)
            A("sync", [],
              lambda e: e.dma_start(negMrowA[64:65, 0:S], negrow[16:32, :]),
              inc=sS, k=16)
            ps["ops"][r] = sS.n

        def emit_stav(pp, units):
            pb = pp % 2
            ops_rdy = ps["ops"][pp]
            uq = list(units)
            tot = len(uq)
            nslots = 72
            sl = [0]

            def pump():
                sl[0] += 1
                while uq and (tot - len(uq)) * nslots < sl[0] * tot:
                    uq.pop(0)()

            for qc in range(8):
                ntk = 2 * qc + 2
                avb = pB[:, 0:512] if qc % 2 == 0 else pB[:, 512:1024]

                def emit_sc(kt, qc=qc):
                    par = kt % 2
                    half = (kt == 2 * qc + 1)
                    qlo = 128 if half else 0
                    sb = [a(0), a(1)] if par == 0 else [a(2), a(3)]
                    for hh in range(2):
                        wts = [(sS, ops_rdy), (sACT, stb_free[par])] if hh == 0 else []
                        A("tensor", wts,
                          lambda e, hh=hh, kt=kt, qc=qc, qlo=qlo, sb=sb: e.matmul(
                              sb[hh][:, qlo:256],
                              qkpT[pb][hh * 64:(hh + 1) * 64, S + kt * 128:S + (kt + 1) * 128],
                              qkpT[pb][hh * 64:(hh + 1) * 64, qc * 256 + qlo:(qc + 1) * 256],
                              start=True, stop=False,
                              tile_position=(hh * 64, 0)))
                    for hh in range(2):
                        A("tensor", [],
                          lambda e, hh=hh, qc=qc, qlo=qlo, sb=sb: e.matmul(
                              sb[hh][:, qlo:256],
                              onescolb[hh * 64:hh * 64 + 1, 0:128],
                              negMrowA[hh * 64:hh * 64 + 1, qc * 256 + qlo:(qc + 1) * 256],
                              start=False, stop=True,
                              tile_position=(hh * 64, 0)),
                          inc=(sPE if hh == 1 else None))
                    sc_done = sPE.n
                    nmask = 0
                    if kt == 2 * qc:
                        for hh in range(2):
                            A("vector", [(sPE, sc_done)] if hh == 0 else [],
                              lambda e, hh=hh, sb=sb: e.tensor_add(
                                  sb[hh][:, 0:128], sb[hh][:, 0:128], maskB),
                              inc=sDVE)
                        nmask = 2
                    elif half:
                        for hh in range(2):
                            A("vector", [(sPE, sc_done)] if hh == 0 else [],
                              lambda e, hh=hh, sb=sb: e.tensor_add(
                                  sb[hh][:, 128:256], sb[hh][:, 128:256], maskB),
                              inc=sDVE)
                        nmask = 2
                    msk_done = sDVE.n
                    ewts = [(sPE, sc_done), (sPE, ebuf_ready[kt % 3])]
                    if nmask:
                        ewts.append((sDVE, msk_done))
                    for hh in range(2):
                        A("scalar", ewts if hh == 0 else [],
                          lambda e, hh=hh, kt=kt, qlo=qlo, sb=sb: e.activation(
                              Ebuf2[kt % 3][:, hh * 256 + qlo:(hh + 1) * 256],
                              sb[hh][:, qlo:256], AF.Exp),
                          inc=(sACT if hh == 1 else None))
                    return sACT.n

                def emit_avb(kt, qc=qc, exp_n=0, ntk=ntk, avb=avb):
                    half = (kt == 2 * qc + 1)
                    qlo = 128 if half else 0
                    for hh in range(2):
                        wts = []
                        if hh == 0:
                            wts = [(sACT, exp_n)]
                            if kt == 0:
                                wts.append((sDVE, av_free[qc % 2]))
                        A("tensor", wts,
                          lambda e, hh=hh, kt=kt, qc=qc, pp=pp, ntk=ntk, qlo=qlo, avb=avb: e.matmul(
                              avb[0:65, hh * 256 + qlo:(hh + 1) * 256],
                              Vplus[:, kt * G65 + (2 * pp + hh) * 65: kt * G65 + (2 * pp + hh) * 65 + 65],
                              Ebuf2[kt % 3][:, hh * 256 + qlo:(hh + 1) * 256],
                              start=(kt == 0 and hh == 0), stop=(kt == ntk - 1 and hh == 1)),
                          inc=(sPE if hh == 1 else None))
                    ebuf_ready[kt % 3] = sPE.n

                exps = []
                for kt in range(ntk):
                    if kt >= 2:
                        stb_free[kt % 2] = exps[kt - 2]
                    exps.append(emit_sc(kt))
                    pump()
                    if kt >= 1:
                        emit_avb(kt - 1, exp_n=exps[kt - 1])
                emit_avb(ntk - 1, exp_n=exps[ntk - 1])
                av_fin = sPE.n
                stb_free[(ntk - 2) % 2] = exps[ntk - 2]
                stb_free[(ntk - 1) % 2] = exps[ntk - 1]

                A("scalar", [(sPE, av_fin)],
                  lambda e, avb=avb: e.activation(
                      rsrow[0:1, 0:512], avb[64:65, 0:512],
                      AF.Ln, bias=eps1[0:1, 0:1]))
                A("scalar", [],
                  lambda e: e.activation(
                      rsrow[0:1, 0:512], rsrow[0:1, 0:512],
                      AF.Exp, bias=0.0, scale=-1.0),
                  inc=sACT)
                rs_done = sACT.n
                A("tensor", [(sACT, rs_done)],
                  lambda e, avb=avb: e.matmul(avb[64:128, 0:512], onescol[0:1, 0:64],
                                              rsrow, start=True, stop=True,
                                              tile_position=(0, 64)),
                  inc=sPE)
                bc_done = sPE.n
                A("vector", [(sPE, bc_done)],
                  lambda e, avb=avb: e.tensor_copy(rsb, avb[64:128, 0:512]),
                  inc=sDVE)
                A("vector", [],
                  lambda e, pp=pp, qc=qc, avb=avb: e.tensor_mul(
                      attnOT[pp][0:64, qc * 256:(qc + 1) * 256],
                      avb[0:64, 0:256], rsb[0:64, 0:256]),
                  inc=sDVE)
                A("vector", [(dTD[qc % 2], todd_free[qc % 2])],
                  lambda e, qc=qc, avb=avb: e.tensor_mul(
                      todd[0:64, (qc % 2) * 256:(qc % 2) * 256 + 256],
                      avb[0:64, 256:512], rsb[0:64, 256:512]),
                  inc=sDVE)
                av_free[qc % 2] = sDVE.n
                A("sync", [(sDVE, sDVE.n)],
                  lambda e, pp=pp, qc=qc: e.dma_start(
                      attnOT[pp][64:128, qc * 256:(qc + 1) * 256],
                      todd[0:64, (qc % 2) * 256:(qc % 2) * 256 + 256]),
                  inc=dTD[qc % 2], k=16)
                todd_free[qc % 2] = dTD[qc % 2].n
            while uq:
                uq.pop(0)()
            ps["qkpT_free"][pb] = sPE.n

        for r in range(NP + 1):
            units = []
            if r < NP:
                emit_qkdma(r)
                for qq in range(4):
                    units.append(lambda r=r, qq=qq: emit_qkgrp(r, 0, qq))
                    units.append(lambda r=r, qq=qq: emit_qkgrp(r, 1, qq))
                for qt in range(NT):
                    for kc in range((qt // 4) + 1):
                        units.append(lambda r=r, qt=qt, kc=kc: emit_tmchunk(r, qt, kc))
            if r == 0:
                for u in units:
                    u()
            else:
                emit_stav(r - 1, units)
            if r < NP:
                emit_dance(r)
        pl.flush(f"B23_s{s}")

        if upto == "B23d" and s == 0:
            for i in range(2):
                A("vector", [(sO, sO.n)],
                  lambda e, i=i: e.tensor_copy(dbgscr, attnOT[0][:, i * 1024:(i + 1) * 1024]),
                  inc=sDVE)
                A("sync", [(sDVE, sDVE.n)],
                  lambda e, i=i: e.dma_start(dbg_d[:, i * 1024:(i + 1) * 1024], dbgscr), inc=sO, k=16)
            A("vector", [(sO, sO.n)], lambda e: e.tensor_copy(dbgscr[:, 0:32], mnegbf), inc=sDVE)
            A("sync", [(sDVE, sDVE.n)], lambda e: e.dma_start(dbg_d[:, 2048:2112], dbgscr[:, 0:64]),
              inc=sO, k=16)
            A("vector", [(sO, sO.n)], lambda e: e.tensor_copy(dbgscr[:, 0:256], h1T[0][:, 0:256]),
              inc=sDVE)
            A("sync", [(sDVE, sDVE.n)], lambda e: e.dma_start(dbg_d[:, 2304:2560], dbgscr[:, 0:256]),
              inc=sO, k=16)
            A("vector", [(sO, sO.n)], lambda e: e.tensor_copy(dbgscr[:, 0:1024], Vplus[:, 0:1024]),
              inc=sDVE)
            A("sync", [(sDVE, sDVE.n)], lambda e: e.dma_start(dbg_d[:, 2560:3584], dbgscr[:, 0:1024]),
              inc=sO, k=16)
            A("vector", [(sO, sO.n)], lambda e: e.tensor_copy(dbgscr[:, 0:1024], qkpT[0][:, 0:1024]),
              inc=sDVE)
            A("sync", [(sDVE, sDVE.n)], lambda e: e.dma_start(dbg_d[:, 3072:4096], dbgscr[:, 0:1024]),
              inc=sO, k=16)
            pl.flush("dbgB23")
            break

        A("gpsimd", [],
          lambda e: e.dma_start(owT.rearrange("p (c j) -> p c j", c=8),
                                owT_d.rearrange("(c p) j -> p c j", p=128)),
          inc=sG, k=16)
        ow_rdy = sG.n
        attn_fin = [(sS, sS.n), (sDVE, sDVE.n), (dTD[0], dTD[0].n), (dTD[1], dTD[1].n)]
        b4_free = {0: 0, 1: 0}
        for st in range(NT):
            bank = pB if st % 2 == 0 else pA[:, 0:1024]
            for c in range(8):
                wts = []
                if c == 0:
                    wts = [(sG, ow_rdy)] + attn_fin + [(sDVE, b4_free[st % 2])]
                for half in range(2):
                    A("tensor", wts if half == 0 else [],
                      lambda e, c=c, half=half, st=st, bank=bank: e.matmul(
                          bank[:, half * 512:(half + 1) * 512],
                          attnOT[c][:, st * 128:(st + 1) * 128],
                          owT[:, c * D + half * 512: c * D + (half + 1) * 512],
                          start=(c == 0), stop=(c == 7)),
                      inc=(sPE if (c == 7 and half == 1) else None))
            A("vector", [(sPE, sPE.n)],
              lambda e, st=st, bank=bank: e.tensor_add(
                  X2[:, st * D:(st + 1) * D], bank, X1R[:, st * D:(st + 1) * D]),
              inc=sDVE)
            A("vector", [],
              lambda e, st=st: e.tensor_add(
                  X1R[:, st * D:(st + 1) * D], X2[:, st * D:(st + 1) * D],
                  X1R[:, st * D:(st + 1) * D]),
              inc=sDVE)
            b4_free[st % 2] = sDVE.n
        pl.flush(f"B4_s{s}")

        if upto == "X2":
            for st in range(NT):
                A("sync", [(sDVE, sDVE.n)],
                  lambda e, s=s, st=st: e.dma_start(
                      y_d[s, st * 128:(st + 1) * 128, :], X2[:, st * D:(st + 1) * D]),
                  inc=sO, k=16)
            A("vector", [(sO, sO.n)], lambda e: e.memset(epsA[:, 4:5], 0.0), inc=sDVE)
            pl.flush(f"dbgX2_s{s}")
            continue

        A("sync", [], lambda e: e.dma_start(n2row, n2_d), inc=sS, k=16)
        n2_rdy = sS.n
        A("tensor", [(sS, n2_rdy)],
          lambda e: e.matmul(pC, onescol[0:1, 0:128], n2row[0:1, 0:512], start=True, stop=True))
        A("tensor", [],
          lambda e: e.matmul(a(0), onescol[0:1, 0:128], n2row[0:1, 512:1024], start=True, stop=True),
          inc=sPE)
        A("vector", [(sPE, sPE.n)], lambda e: e.tensor_copy(n2rep[:, 0:512], pC), inc=sDVE)
        A("vector", [], lambda e: e.tensor_copy(n2rep[:, 512:1024], a(0)), inc=sDVE)
        n2cp = sDVE.n
        gts = []
        fdone = []
        gd = []
        sqa = []
        A("gpsimd", [],
          lambda e: e.dma_start(gateT_sb.rearrange("p (c j) -> p c j", c=8),
                                gateT_d.rearrange("(c p) j -> p c j", p=128)),
          inc=sG, k=16)
        gate_rdy = sG.n
        ge_n = 0
        rm_n = n2cp
        for st in range(NT):
            A("scalar", [(sDVE, gd[st - 1] if st else sDVE.n)],
              lambda e, st=st: e.activation(
                  gbf, X2[:, st * D:(st + 1) * D], AF.Square, accum_out=stat[:, 20:21]),
              inc=sACT)
            A("scalar", [],
              lambda e: e.activation(stat[:, 21:22], stat[:, 20:21], AF.Ln,
                                     bias=epsA[:, 1:2], scale=1.0 / D))
            A("scalar", [],
              lambda e: e.activation(stat[:, 22:23], stat[:, 21:22], AF.Exp, bias=0.0, scale=-0.5),
              inc=sACT)
            sqa.append(sACT.n)
            A("vector", [(sACT, sACT.n)],
              lambda e, st=st: e.tensor_scalar(
                  bscr2, X2[:, st * D:(st + 1) * D], stat[:, 22:23], None, op0=ALU.mult),
              inc=sDVE)
            A("vector", [], lambda e: e.tensor_mul(bscr2, bscr2, n2rep), inc=sDVE)
            gdone = sDVE.n
            gd.append(gdone)
            trslot = [pA[:, 1024:1152], pA[:, 1536:1664]]
            for c in range(8):
                wts = [(sCP, max(0, sCP.n - 1))]
                if c == 0:
                    wts.append((sDVE, gdone))
                A("tensor", wts,
                  lambda e, c=c, trslot=trslot: e.transpose(
                      trslot[c % 2], bscr2[:, c * 128:(c + 1) * 128], identf),
                  inc=sPE)
                A("vector", [(sPE, sPE.n)],
                  lambda e, st=st, c=c, trslot=trslot: e.tensor_copy(
                      gT[c][:, st * 128:(st + 1) * 128], trslot[c % 2]),
                  inc=sCP)
            gt_done = sCP.n
            gts.append(gt_done)
            for c in range(8):
                wts = [(sCP, gt_done), (sG, gate_rdy), (sDVE, rm_n), (sACT, ge_n)] if c == 0 else []
                A("tensor", wts,
                  lambda e, c=c, st=st: e.matmul(
                      pC[:, 0:NE], gT[c][:, st * 128:(st + 1) * 128],
                      gateT_sb[:, c * NE:(c + 1) * NE], start=(c == 0), stop=(c == 7)),
                  inc=(sPE if c == 7 else None))
            A("vector", [(sPE, sPE.n)],
              lambda e: e.reduce_max(stat[:, 24:25], pC[:, 0:NE], axis=AX.X, negate=True),
              inc=sDVE)
            rm_n = sDVE.n
            A("scalar", [(sDVE, sDVE.n)],
              lambda e: e.activation(stat[:, 32:40], pC[:, 0:NE], AF.Exp,
                                     bias=stat[:, 24:25], accum_out=stat[:, 25:26]),
              inc=sACT)
            ge_n = sACT.n
            A("vector", [(sACT, sACT.n)],
              lambda e: e.reciprocal(stat[:, 26:27], stat[:, 25:26]), inc=sDVE)
            A("vector", [(sDVE, sDVE.n)],
              lambda e: e.tensor_scalar(stat[:, 32:40], stat[:, 32:40], stat[:, 26:27], None,
                                        op0=ALU.mult),
              inc=sDVE)
            A("vector", [(sDVE, sDVE.n)], lambda e: e.reduce_max(stat[:, 27:28], stat[:, 32:40], axis=AX.X),
              inc=sDVE)
            A("vector", [(sDVE, sDVE.n)],
              lambda e: e.tensor_scalar(stat[:, 40:48], stat[:, 32:40], stat[:, 27:28], None,
                                        op0=ALU.subtract),
              inc=sDVE)
            A("vector", [(sDVE, sDVE.n)],
              lambda e: e.tensor_scalar(stat[:, 40:48], stat[:, 40:48], -1e30, 1.0,
                                        op0=ALU.mult, op1=ALU.min),
              inc=sDVE)
            A("vector", [(sDVE, sDVE.n)],
              lambda e: e.tensor_scalar(stat[:, 40:48], stat[:, 40:48], 1.0, 1e30,
                                        op0=ALU.subtract, op1=ALU.mult),
              inc=sDVE)
            A("vector", [(sDVE, sDVE.n)], lambda e: e.tensor_add(stat[:, 40:48], stat[:, 32:40], stat[:, 40:48]),
              inc=sDVE)
            A("vector", [(sDVE, sDVE.n)], lambda e: e.reduce_max(stat[:, 28:29], stat[:, 40:48], axis=AX.X),
              inc=sDVE)
            A("vector", [(sDVE, sDVE.n)], lambda e: e.tensor_add(stat[:, 29:30], stat[:, 27:28], stat[:, 28:29]),
              inc=sDVE)
            A("vector", [(sDVE, sDVE.n)],
              lambda e: e.tensor_scalar(stat[:, 30:31], stat[:, 29:30], 1e-9, None, op0=ALU.add),
              inc=sDVE)
            A("vector", [(sDVE, sDVE.n)], lambda e: e.reciprocal(stat[:, 31:32], stat[:, 30:31]), inc=sDVE)
            A("vector", [(sDVE, sDVE.n)],
              lambda e, st=st: e.tensor_mul(factor[:, st:st + 1], stat[:, 29:30], stat[:, 31:32]),
              inc=sDVE)
            fdone.append(sDVE.n)
        b5cp = sCP.n

        ostore_free = {0: dY[0].n, 1: dY[1].n}

        def emit_b7(st):
            A("scalar", [(sDVE, sDVE.n)],
              lambda e, st=st: e.activation(
                  lnscr, X1R[:, st * D:(st + 1) * D], AF.Square, accum_out=stat[:, 50:51]),
              inc=sACT)
            A("vector", [(sACT, sACT.n)],
              lambda e, st=st: e.reduce_sum(stat[:, 49:50], X1R[:, st * D:(st + 1) * D], axis=AX.X),
              inc=sDVE)
            A("vector", [(sDVE, sDVE.n)],
              lambda e: e.tensor_scalar(stat[:, 51:52], stat[:, 49:50], 1.0 / D, None, op0=ALU.mult),
              inc=sDVE)
            A("vector", [(sDVE, sDVE.n)],
              lambda e: e.tensor_scalar(stat[:, 52:53], stat[:, 50:51], 1.0 / D, None, op0=ALU.mult),
              inc=sDVE)
            A("vector", [(sDVE, sDVE.n)], lambda e: e.tensor_mul(stat[:, 53:54], stat[:, 51:52], stat[:, 51:52]),
              inc=sDVE)
            A("vector", [(sDVE, sDVE.n)], lambda e: e.tensor_sub(stat[:, 54:55], stat[:, 52:53], stat[:, 53:54]),
              inc=sDVE)
            A("scalar", [(sDVE, sDVE.n)],
              lambda e: e.activation(stat[:, 55:56], stat[:, 54:55], AF.Ln, bias=epsA[:, 0:1]))
            A("scalar", [],
              lambda e: e.activation(stat[:, 56:57], stat[:, 55:56], AF.Exp, bias=0.0, scale=-0.5),
              inc=sACT)
            A("vector", [(sACT, sACT.n), (sDVE, sDVE.n), (dY[st % 2], ostore_free[st % 2])],
              lambda e, st=st: e.tensor_scalar(
                  outst[:, (st % 2) * D:(st % 2 + 1) * D], X1R[:, st * D:(st + 1) * D],
                  stat[:, 51:52], stat[:, 56:57], op0=ALU.subtract, op1=ALU.mult),
              inc=sDVE)
            A("sync", [(sDVE, sDVE.n)],
              lambda e, s=s, st=st: e.dma_start(
                  y_d[s, st * 128:(st + 1) * 128, :], outst[:, (st % 2) * D:(st % 2 + 1) * D]),
              inc=dY[st % 2], k=16)
            ostore_free[st % 2] = dY[st % 2].n

        up_free = {0: 0, 1: 0}
        dn_free = {0: 0, 1: 0}
        for ep in range(4):
            for c in range(8):
                if ep >= 2:
                    wts = [(sACT, sACT.n), (sPE, sPE.n)] if c == 0 else []
                else:
                    t = ep * 4 + c // 2
                    wts = [(sDVE, gd[t]), (sACT, sqa[t])]
                A("gpsimd", wts,
                  lambda e, c=c, ep=ep: e.dma_start(
                      upw[ep % 2][:, c * KB:(c + 1) * KB],
                      upT_d[c * 128:(c + 1) * 128, ep * KB:(ep + 1) * KB]),
                  inc=dUW[ep % 2], k=16)
            for cc in range(8):
                if ep >= 2:
                    wts = []
                else:
                    t = 8 + ep * 4 + cc // 2
                    wts = [(sDVE, gd[t]), (sACT, sqa[t])]
                A("gpsimd", wts,
                  lambda e, cc=cc, ep=ep: e.dma_start(
                      dww[ep % 2][:, cc * KB:(cc + 1) * KB],
                      downT_d[ep * KB + cc * 128: ep * KB + (cc + 1) * 128, :]),
                  inc=dUW[ep % 2], k=16)
            w_rdy = dUW[ep % 2].n
            for sc in range(4):
                for et in range(8):
                    bank = a(0) if et % 2 == 0 else a(1)
                    for c in range(8):
                        wts = []
                        if c == 0:
                            wts = [(dUW[ep % 2], w_rdy), (sACT, up_free[et % 2]),
                                   (sCP, gts[4 * sc + 3])]
                            if ep == 0 and sc == 0 and et == 0:
                                wts.append((sDVE, n2cp))
                        A("tensor", wts,
                          lambda e, c=c, et=et, sc=sc, ep=ep, bank=bank: e.matmul(
                              bank, upw[ep % 2][:, c * KB + et * 128: c * KB + (et + 1) * 128],
                              gT[c][:, sc * 512:(sc + 1) * 512],
                              start=(c == 0), stop=(c == 7)),
                          inc=(sPE if c == 7 else None))
                    A("scalar", [(sPE, sPE.n)],
                      lambda e, et=et, sc=sc, bank=bank: e.activation(
                          siluT[sc % 2][:, et * 512:(et + 1) * 512], bank, AF.Silu),
                      inc=sACT)
                    up_free[et % 2] = sACT.n
                silu_done = sACT.n
                for st2 in range(4):
                    st = sc * 4 + st2
                    bank = pB if st2 % 2 == 0 else pA[:, 1024:2048]
                    for et in range(8):
                        wts = []
                        if et == 0:
                            wts = [(sACT, silu_done), (sDVE, dn_free[st2 % 2]),
                                   (dUW[ep % 2], w_rdy)]
                            if ep == 0 and sc == 0 and st2 == 1:
                                wts.append((sCP, b5cp))
                        for half in range(2):
                            A("tensor", wts if half == 0 else [],
                              lambda e, et=et, half=half, st2=st2, sc=sc, ep=ep, bank=bank: e.matmul(
                                  bank[:, half * 512:(half + 1) * 512],
                                  siluT[sc % 2][:, et * 512 + st2 * 128: et * 512 + st2 * 128 + 128],
                                  dww[ep % 2][:, et * KB + half * 512: et * KB + (half + 1) * 512],
                                  start=(et == 0), stop=(et == 7)),
                              inc=(sPE if (et == 7 and half == 1) else None))
                    A("vector", [(sPE, sPE.n), (sDVE, fdone[st])],
                      lambda e, st=st, bank=bank: e.scalar_tensor_tensor(
                          X1R[:, st * D:(st + 1) * D], bank, factor[:, st:st + 1],
                          X1R[:, st * D:(st + 1) * D], op0=ALU.mult, op1=ALU.add),
                      inc=sDVE)
                    dn_free[st2 % 2] = sDVE.n
                    if ep == 3:
                        emit_b7(st)
        pl.flush(f"B6_s{s}")


    with nc.Block("fin") as blk:
        @blk.sync
        def _(e):
            e.wait_ge(sO.h, sO.n)
            e.wait_ge(dX[0].h, dX[0].n)
            e.wait_ge(dX[1].h, dX[1].n)
            e.wait_ge(dY[0].h, dY[0].n)
            e.wait_ge(dY[1].h, dY[1].n)

    nc.finalize()
    return nc


_NC_CACHE = {}


def _get_nc(upto):
    if upto not in _NC_CACHE:
        _NC_CACHE[upto] = build_program(upto)
    return _NC_CACHE[upto]


def make_consts():
    identbf = np.eye(128, dtype=ml_dtypes.bfloat16)
    identf = np.eye(128, dtype=np.float32)
    onescol = np.ones((1, 128), dtype=np.float32)
    onescolb = np.ones((128, 128), dtype=ml_dtypes.bfloat16)
    ones32 = np.ones((128, 32), dtype=ml_dtypes.bfloat16)
    i = np.arange(128).reshape(128, 1)
    c = np.arange(2048).reshape(1, 2048)
    rampA = (c - C0 - i).astype(np.float32)
    j = np.arange(128).reshape(1, 128)
    maskA = np.where(j > i, -1e30, 0.0).astype(np.float32)
    maskB = np.where(i > j, -1e30, 0.0).astype(np.float32)
    return identbf, identf, onescol, onescolb, ones32, rampA, maskA, maskB


def kernel(L0_attn_x, L0_attn_W_qk, L0_attn_W_v, L0_attn_alibi_slope,
           L0_ffn_n1_w, L0_ffn_qkv_w, L0_ffn_o_w, L0_ffn_n2_w,
           L0_ffn_gate_w, L0_ffn_up_w, L0_ffn_down_w, _upto="full", _ncores=8):
    X = np.ascontiguousarray(np.asarray(L0_attn_x, dtype=np.float32))
    wqk = np.ascontiguousarray(np.asarray(L0_attn_W_qk, dtype=np.float32))
    wvT = np.ascontiguousarray(np.asarray(L0_attn_W_v, dtype=np.float32).T)
    slope = np.asarray(L0_attn_alibi_slope, dtype=np.float32).reshape(1, 1)
    n1 = np.asarray(L0_ffn_n1_w, dtype=np.float32).reshape(1, D)
    n2 = np.asarray(L0_ffn_n2_w, dtype=np.float32).reshape(1, D)
    qkvT = np.ascontiguousarray(np.asarray(L0_ffn_qkv_w, dtype=np.float32).T)
    owT = np.ascontiguousarray(np.asarray(L0_ffn_o_w, dtype=np.float32).T)
    gateT = np.ascontiguousarray(np.asarray(L0_ffn_gate_w, dtype=np.float32).T)
    upT = np.ascontiguousarray(np.asarray(L0_ffn_up_w, dtype=np.float32).T)
    downT = np.ascontiguousarray(np.asarray(L0_ffn_down_w, dtype=np.float32).T)
    identbf, identf, onescol, onescolb, ones32, rampA, maskA, maskB = make_consts()

    nc = _get_nc(_upto)
    in_maps = []
    for i in range(_ncores):
        in_maps.append({
            "x": np.ascontiguousarray(X[2 * i:2 * i + 2]),
            "wqk": wqk, "wvT": wvT, "slope": slope, "n1": n1, "n2": n2,
            "qkvT": qkvT, "owT": owT, "gateT": gateT, "upT": upT, "downT": downT,
            "identbf": identbf, "identf": identf, "onescol": onescol,
            "onescolb": onescolb, "ones32": ones32,
            "rampA": rampA, "maskA": maskA, "maskB": maskB,
        })
    res = run_bass_kernel_spmd(nc, in_maps, core_ids=list(range(_ncores)))
    if _upto in ("A01", "setup", "A2d", "B23d"):
        return res.results[0]["dbg"]
    out = np.concatenate([r["y"] for r in res.results], axis=0)
    return out.reshape(2 * _ncores, S, D).astype(np.float32)


# version 20 (final)
# speedup vs baseline: 1.1739x; 1.0034x over previous
import os
import numpy as np
import ml_dtypes
import concourse.bass as bass
import concourse.mybir as mybir
from concourse.bass_utils import run_bass_kernel_spmd

F32 = mybir.dt.float32
BF16 = mybir.dt.bfloat16
AF = mybir.ActivationFunctionType
ALU = mybir.AluOpType
AX = mybir.AxisListType

S = 2048
D = 1024
NT = 16
H = 16
HD = 64
NP = 8
ED = 4096
NE = 8
NSEQ = 2
C0 = 1920
KB = 1024
G65 = H * 65


class Sem:
    def __init__(self, nc, name):
        self.cm = nc.semaphore(name)
        self.h = self.cm.__enter__()
        self.n = 0


class Plan:
    ENGS = ("sync", "gpsimd", "tensor", "scalar", "vector")

    def __init__(self, nc):
        self.nc = nc
        self.q = {e: [] for e in self.ENGS}

    def add(self, eng, waits, fn, inc=None, k=1):
        wcap = [(sm.h, v) for (sm, v) in waits]
        if inc is not None:
            inc.n += k
        ih = None if inc is None else inc.h

        def run(e, wcap=wcap, fn=fn, ih=ih, k=k):
            for (h, v) in wcap:
                if v > 0:
                    e.wait_ge(h, v)
            ins = fn(e)
            if ih is not None:
                ins.then_inc(ih, k)
        self.q[eng].append(run)

    def flush(self, name=None):
        q = self.q
        self.q = {e: [] for e in self.ENGS}
        with self.nc.Block(name) as blk:
            for ename in self.ENGS:
                fns = q[ename]
                if not fns:
                    continue

                def body(eng, fns=fns):
                    for f in fns:
                        f(eng)
                getattr(blk, ename)(body)


def build_program(upto="full"):
    nc = bass.Bass("TRN2", target_bir_lowering=False)

    x_d = nc.dram_tensor("x", [NSEQ, S, D], F32, kind="ExternalInput").ap()
    wqk_d = nc.dram_tensor("wqk", [D, D], F32, kind="ExternalInput").ap()
    wvT_d = nc.dram_tensor("wvT", [D, D], F32, kind="ExternalInput").ap()
    slope_d = nc.dram_tensor("slope", [1, 1], F32, kind="ExternalInput").ap()
    n1_d = nc.dram_tensor("n1", [1, D], F32, kind="ExternalInput").ap()
    n2_d = nc.dram_tensor("n2", [1, D], F32, kind="ExternalInput").ap()
    qkvT_d = nc.dram_tensor("qkvT", [D, 3 * D], F32, kind="ExternalInput").ap()
    owT_d = nc.dram_tensor("owT", [D, D], F32, kind="ExternalInput").ap()
    gateT_d = nc.dram_tensor("gateT", [D, NE], F32, kind="ExternalInput").ap()
    upT_d = nc.dram_tensor("upT", [D, ED], F32, kind="ExternalInput").ap()
    downT_d = nc.dram_tensor("downT", [ED, D], F32, kind="ExternalInput").ap()
    identbf_d = nc.dram_tensor("identbf", [128, 128], BF16, kind="ExternalInput").ap()
    identf_d = nc.dram_tensor("identf", [128, 128], F32, kind="ExternalInput").ap()
    onescol_d = nc.dram_tensor("onescol", [1, 128], F32, kind="ExternalInput").ap()
    onescolb_d = nc.dram_tensor("onescolb", [128, 128], BF16, kind="ExternalInput").ap()
    ones32_d = nc.dram_tensor("ones32", [128, 32], BF16, kind="ExternalInput").ap()
    rampA_d = nc.dram_tensor("rampA", [128, 2048], F32, kind="ExternalInput").ap()
    maskA_d = nc.dram_tensor("maskA", [128, 128], F32, kind="ExternalInput").ap()
    maskB_d = nc.dram_tensor("maskB", [128, 128], F32, kind="ExternalInput").ap()
    y_d = nc.dram_tensor("y", [NSEQ, S, D], F32, kind="ExternalOutput").ap()
    dbg_d = nc.dram_tensor("dbg", [128, 4096], F32, kind="ExternalOutput").ap()

    X1R = nc.alloc_sbuf_tensor("X1R", [128, NT * D], F32).ap()
    factor = nc.alloc_sbuf_tensor("factor", [128, NT], F32).ap()
    maskA = nc.alloc_sbuf_tensor("maskA_s", [128, 128], F32).ap()
    maskB = nc.alloc_sbuf_tensor("maskB_s", [128, 128], F32).ap()
    identbf = nc.alloc_sbuf_tensor("identbf_s", [128, 128], BF16).ap()
    identf = nc.alloc_sbuf_tensor("identf_s", [128, 128], F32).ap()
    onescol = nc.alloc_sbuf_tensor("onescol_s", [1, 128], F32).ap()
    onescolb = nc.alloc_sbuf_tensor("onescolb_s", [128, 128], BF16).ap()
    ones32 = nc.alloc_sbuf_tensor("ones32_s", [128, 32], BF16).ap()
    slope_sb = nc.alloc_sbuf_tensor("slope_sb", [1, 32], F32).ap()
    slope_bc = nc.alloc_sbuf_tensor("slope_bc", [128, 1], F32).ap()
    epsA = nc.alloc_sbuf_tensor("epsA", [128, 8], F32).ap()
    eps1 = nc.alloc_sbuf_tensor("eps1", [1, 8], F32).ap()
    stat = nc.alloc_sbuf_tensor("stat", [128, 64], F32).ap()
    mu1 = nc.alloc_sbuf_tensor("mu1", [1, 16], F32).ap()
    mu1b = nc.alloc_sbuf_tensor("mu1b", [8, 8], BF16).ap()
    nbr = nc.alloc_sbuf_tensor("nbr", [128, 8], F32).ap()
    nbrb = nc.alloc_sbuf_tensor("nbrb", [128, 8], BF16).ap()
    biasH = nc.alloc_sbuf_tensor("biasH", [128, 2], F32).ap()

    TOP = 229344
    ARENA = 138 * KB
    AB = TOP - ARENA

    def at(name, off, shape, dtype):
        esz = 4 if dtype == F32 else 2
        assert off + shape[1] * esz <= ARENA, (name, off, shape)
        return nc.alloc_sbuf_tensor_at(name, shape, dtype, offset=AB + off).ap()

    XT = [at(f"XT{c}", c * 4 * KB, [128, S], BF16) for c in range(8)]
    Vsb = at("Vsb", 32 * KB, [128, NT * D], BF16)
    wslot = at("wslot", 64 * KB, [128, 8 * D], BF16)
    Sstrip = [at(f"Sstrip{i}", 80 * KB + i * 8 * KB, [128, S], F32) for i in range(2)]
    PTring = at("PTring", 104 * KB, [128, 4 * 128], BF16)
    qpt = [at(f"qpt{m}", 105 * KB + m * 512, [128, 256], BF16) for m in range(8)]
    xres = [at(f"xres{i}", 109 * KB + i * 4 * KB, [128, D], F32) for i in range(2)]
    xbf = [at(f"xbf{i}", 117 * KB + i * 2 * KB, [128, D], BF16) for i in range(2)]
    Gfull = at("Gfull", 121 * KB, [128, 2048], F32)
    sqscrA = at("sqscrA", 129 * KB, [128, D], BF16)

    h1T = [at(f"h1T{c}", c * 4 * KB, [128, S], BF16) for c in range(8)]
    Vplus = at("Vplus", 32 * KB, [128, NT * G65], BF16)
    qkpT = [at(f"qkpT{i}", 65 * KB + i * 8 * KB, [128, 2 * S], BF16) for i in range(2)]
    negMrowA = at("negMrowA", 81 * KB, [65, S], BF16)
    qkvw = [at(f"qkvw{i}", 89 * KB + i * 4 * KB, [128, 2048], BF16) for i in range(2)]
    Ebuf2 = [at(f"Ebuf2_{i}", 97 * KB + i * KB, [128, 512], BF16) for i in range(3)]
    n1rep = at("n1rep", 100 * KB, [128, D], F32)
    bscr = at("bscr", 104 * KB, [128, D], F32)
    n1row = at("n1row", 108 * KB, [1, D], F32)
    vwslot = at("vwslot", 65 * KB, [128, 8 * D], BF16)
    rsrow = at("rsrow", 100 * KB, [1, 512], F32)
    rsb = at("rsb", 102 * KB, [64, 512], F32)
    todd = at("todd", 104 * KB, [64, 2 * 256], BF16)
    negrow = at("negrow", 105 * KB, [32, 128], BF16)
    mnegbf = at("mnegbf", 105 * KB + 512, [128, 32], F32)
    attnOT = [at(f"attnOT{c}", 106 * KB + c * 4 * KB, [128, S], BF16) for c in range(8)]
    dbgscr = at("dbgscr", 100 * KB, [128, 1024], F32)

    X2 = at("X2", 0, [128, NT * D], F32)
    owT = at("owT", 65 * KB, [128, 8 * D], BF16)
    n2rep = at("n2rep", 81 * KB, [128, D], F32)
    bscr2 = at("bscr2", 85 * KB, [128, D], F32)
    gbf = at("gbf", 89 * KB, [128, D], BF16)
    gateT_sb = at("gateT_sb", 91 * KB, [128, NE * 8], BF16)
    n2row = at("n2row", 92 * KB, [1, D], F32)
    gT = [at(f"gT{c}", 106 * KB + c * 4 * KB, [128, S], BF16) for c in range(8)]

    upw = [at(f"upw{i}", i * 16 * KB, [128, 8 * KB], BF16) for i in range(2)]
    dww = [at(f"dww{i}", 32 * KB + i * 16 * KB, [128, 8 * KB], BF16) for i in range(2)]
    siluT = [at(f"siluT{i}", 65 * KB + i * 8 * KB, [128, 8 * 512], BF16) for i in range(2)]
    lnscr = at("lnscr", 81 * KB, [128, D], BF16)
    outst = at("outst", 85 * KB, [128, 2 * D], F32)

    assert nc.sbuf_base <= AB, (nc.sbuf_base, AB)

    pA = nc.alloc_psum_tensor("pA", [128, 2048], F32).ap()
    pB = nc.alloc_psum_tensor("pB", [128, 1024], F32).ap()
    pC = nc.alloc_psum_tensor("pC", [128, 512], F32).ap()
    pT = nc.alloc_psum_tensor("pT", [128, 512], F32).ap()

    def a(i):
        return pA[:, i * 512:(i + 1) * 512]

    sG = Sem(nc, "sG")
    sS = Sem(nc, "sS")
    sO = Sem(nc, "sO")
    sPE = Sem(nc, "sPE")
    sACT = Sem(nc, "sACT")
    sDVE = Sem(nc, "sDVE")
    sCP = Sem(nc, "sCP")
    dX = [Sem(nc, "dX0"), Sem(nc, "dX1")]
    dQW = [Sem(nc, "dQW0"), Sem(nc, "dQW1")]
    dUW = [Sem(nc, "dUW0"), Sem(nc, "dUW1")]
    dTD = [Sem(nc, "dTD0"), Sem(nc, "dTD1")]
    dY = [Sem(nc, "dY0"), Sem(nc, "dY1")]
    dWS = Sem(nc, "dWS")

    pl = Plan(nc)
    A = pl.add

    A("sync", [], lambda e: e.dma_start(maskA, maskA_d), inc=sS, k=16)
    A("sync", [], lambda e: e.dma_start(maskB, maskB_d), inc=sS, k=16)
    A("sync", [], lambda e: e.dma_start(identbf, identbf_d), inc=sS, k=16)
    A("sync", [], lambda e: e.dma_start(identf, identf_d), inc=sS, k=16)
    A("sync", [], lambda e: e.dma_start(onescol, onescol_d), inc=sS, k=16)
    A("sync", [], lambda e: e.dma_start(onescolb, onescolb_d), inc=sS, k=16)
    A("sync", [], lambda e: e.dma_start(ones32, ones32_d), inc=sS, k=16)
    A("sync", [], lambda e: e.dma_start(slope_sb[0:1, 0:1], slope_d), inc=sS, k=16)
    A("vector", [], lambda e: e.memset(epsA[:, 0:1], 1e-5))
    A("vector", [], lambda e: e.memset(epsA[:, 1:2], 1e-6))
    A("vector", [], lambda e: e.memset(eps1[0:1, 0:1], 1e-38), inc=sDVE)
    A("tensor", [(sS, sS.n)],
      lambda e: e.matmul(pC[:, 0:1], onescol[0:1, 0:128], slope_sb[0:1, 0:1],
                         start=True, stop=True), inc=sPE)
    A("vector", [(sPE, sPE.n)], lambda e: e.tensor_copy(slope_bc, pC[:, 0:1]), inc=sDVE)
    A("vector", [(sS, sS.n)], lambda e: e.memset(epsA[:, 3:4], 0.0), inc=sDVE)
    pl.flush("setup")

    if upto == "setup":
        A("vector", [], lambda e: e.tensor_copy(outst[:, 0:1], slope_bc), inc=sDVE)
        A("vector", [], lambda e: e.tensor_copy(outst[:, 1:9], epsA), inc=sDVE)
        A("vector", [], lambda e: e.tensor_copy(outst[:, 9:137], maskA), inc=sDVE)
        A("sync", [(sDVE, sDVE.n)], lambda e: e.dma_start(dbg_d[:, 0:137], outst[:, 0:137]),
          inc=sO, k=16)
        pl.flush("dbgsetup")

    for s in range(NSEQ if upto != "setup" else 0):
        A("sync", [], lambda e: e.dma_start(Gfull, rampA_d), inc=sS, k=16)
        A("scalar", [(sS, sS.n), (sDVE, sDVE.n)],
          lambda e: e.activation(Gfull, Gfull, AF.Copy, bias=0.0, scale=slope_bc[:, 0:1]),
          inc=sACT)
        A("gpsimd", [],
          lambda e: e.dma_start(wslot.rearrange("p (c j) -> p c j", c=8),
                                wvT_d.rearrange("(c p) j -> p c j", p=128)),
          inc=dWS, k=16)
        wv_ready = dWS.n

        _tiles = int(os.environ.get("KDBG_TILES", NT))
        _nov = int(os.environ.get("KDBG_NOV", "0"))
        _notr = int(os.environ.get("KDBG_NOTR", "0"))
        trslot = [pA[:, 1024:1152], pA[:, 1536:1664]]
        tr_done = {0: 0, 1: 0}
        vv_free = {0: 0, 1: 0}
        for st in range(_tiles):
            w = [(sPE, tr_done[st % 2])] if st >= 2 else []
            A("gpsimd", w,
              lambda e, s=s, st=st: e.dma_start(xres[st % 2], x_d[s, st * 128:(st + 1) * 128, :]),
              inc=dX[st % 2], k=16)
            xw = dX[st % 2].n
            for c in range(0 if _notr else 8):
                wts = [(sCP, max(0, sCP.n - 1))]
                if c == 0:
                    wts.append((dX[st % 2], xw))
                A("tensor", wts,
                  lambda e, st=st, c=c: e.transpose(
                      trslot[c % 2], xres[st % 2][:, c * 128:(c + 1) * 128], identf),
                  inc=sPE)
                A("vector", [(sPE, sPE.n)],
                  lambda e, st=st, c=c: e.tensor_copy(
                      XT[c][:, st * 128:(st + 1) * 128], trslot[c % 2]),
                  inc=sCP)
            tr_done[st % 2] = sPE.n
            vbank = pB if st % 2 == 0 else pA[:, 0:1024]
            for c in range(0 if _nov else 8):
                wts = []
                if c == 0:
                    wts = [(sCP, sCP.n), (dWS, wv_ready)]
                    if st >= 2:
                        wts.append((sACT, vv_free[st % 2]))
                for half in range(2):
                    A("tensor", wts if half == 0 else [],
                      lambda e, c=c, half=half, st=st, vbank=vbank: e.matmul(
                          vbank[:, half * 512:(half + 1) * 512],
                          XT[c][:, st * 128:(st + 1) * 128],
                          wslot[:, c * D + half * 512: c * D + (half + 1) * 512],
                          start=(c == 0), stop=(c == 7)),
                      inc=(sPE if (c == 7 and half == 1) else None))
            if not _nov:
                A("scalar", [(sPE, sPE.n)],
                  lambda e, st=st, vbank=vbank: e.activation(Vsb[:, st * D:(st + 1) * D], vbank, AF.Copy),
                  inc=sACT)
                vv_free[st % 2] = sACT.n
        pl.flush(f"A01_s{s}")

        if upto == "A01" and s == 0:
            A("vector", [], lambda e: e.tensor_copy(outst[:, 0:1024], Vsb[:, 0:1024]), inc=sDVE)
            A("sync", [(sDVE, sDVE.n)], lambda e: e.dma_start(dbg_d[:, 0:1024], outst[:, 0:1024]),
              inc=sO, k=16)
            A("vector", [], lambda e: e.tensor_copy(outst[:, 1024:2048], XT[0][:, 0:1024]), inc=sDVE)
            A("sync", [(sDVE, sDVE.n)], lambda e: e.dma_start(dbg_d[:, 1024:2048], outst[:, 1024:2048]),
              inc=sO, k=16)
            A("sync", [(sACT, sACT.n)], lambda e: e.dma_start(dbg_d[:, 2048:4096], Gfull), inc=sO, k=16)
            pl.flush("dbgA01")
            break

        A("gpsimd", [],
          lambda e: e.dma_start(wslot.rearrange("p (c j) -> p c j", c=8),
                                wqk_d.rearrange("(c p) j -> p c j", p=128)),
          inc=dWS, k=16)
        wq_ready = dWS.n

        qpt_done = 0
        qp_free = {0: 0, 1: 0}
        sc_free = {0: 0, 1: 0}
        xr_free = {0: 0, 1: 0}
        yv_free = 0
        tr_fin = {0: 0, 1: 0}
        _qis = int(os.environ.get("KDBG_QIS", NT))
        for qi in range(_qis):
            kend = (qi + 1) * 128
            nkc = (kend + 511) // 512
            buf = qi % 2
            A("gpsimd", [(sDVE, xr_free[qi % 2])] if qi >= 2 else [],
              lambda e, s=s, qi=qi: e.dma_start(xres[qi % 2], x_d[s, qi * 128:(qi + 1) * 128, :]),
              inc=dX[qi % 2], k=16)
            xres_rdy = dX[qi % 2].n

            if qi % 2 == 0:
                for m in range(8):
                    bank = a(0) if m % 2 == 0 else a(1)
                    for c in range(8):
                        wts = []
                        if c == 0:
                            wts = [(dWS, wq_ready), (sACT, qp_free[m % 2])]
                        A("tensor", wts,
                          lambda e, m=m, c=c, qi=qi, bank=bank: e.matmul(
                              bank[:, 0:256],
                              wslot[:, c * D + m * 128: c * D + (m + 1) * 128],
                              XT[c][:, qi * 128: qi * 128 + 256],
                              start=(c == 0), stop=(c == 7)),
                          inc=(sPE if c == 7 else None))
                    A("scalar", [(sPE, sPE.n)],
                      lambda e, m=m, bank=bank: e.activation(
                          qpt[m], bank[:, 0:256], AF.Copy, bias=0.0, scale=1.0 / 32.0),
                      inc=sACT)
                    qp_free[m % 2] = sACT.n
                qpt_done = sACT.n

            for kc in range(nkc):
                w = min(512, kend - kc * 512)
                bank = a(2) if kc % 2 == 0 else a(3)
                for c in range(8):
                    wts = []
                    if c == 0:
                        wts = [(sACT, qpt_done), (sDVE, sc_free[kc % 2])]
                    A("tensor", wts,
                      lambda e, c=c, kc=kc, qi=qi, w=w, bank=bank: e.matmul(
                          bank[:, 0:w],
                          qpt[c][:, (qi % 2) * 128:(qi % 2) * 128 + 128],
                          XT[c][:, kc * 512: kc * 512 + w],
                          start=(c == 0), stop=(c == 7)),
                      inc=(sPE if c == 7 else None))
                off = kc * 512 - qi * 128 + C0
                sw = [(sPE, sPE.n)]
                if kc == 0:
                    sw.append((sCP, tr_fin[qi % 2]))
                A("vector", sw,
                  lambda e, kc=kc, w=w, off=off, buf=buf, bank=bank: e.tensor_add(
                      Sstrip[buf][:, kc * 512: kc * 512 + w], bank[:, 0:w],
                      Gfull[:, off: off + w]),
                  inc=sDVE)
                sc_free[kc % 2] = sDVE.n
                if kc == nkc - 1:
                    A("vector", [],
                      lambda e, qi=qi, buf=buf: e.tensor_add(
                          Sstrip[buf][:, qi * 128: qi * 128 + 128],
                          Sstrip[buf][:, qi * 128: qi * 128 + 128], maskA),
                      inc=sDVE)
            A("vector", [(sDVE, sDVE.n)],
              lambda e, buf=buf, kend=kend: e.reduce_max(
                  stat[:, 4:5], Sstrip[buf][:, 0:kend], axis=AX.X, negate=True),
              inc=sDVE)
            A("scalar", [(sDVE, sDVE.n)],
              lambda e, buf=buf, kend=kend: e.activation(
                  Sstrip[buf][:, 0:kend], Sstrip[buf][:, 0:kend], AF.Exp,
                  bias=stat[:, 4:5], accum_out=stat[:, 5:6]),
              inc=sACT)
            exp_done = sACT.n
            A("vector", [(sACT, exp_done)],
              lambda e: e.reciprocal(stat[:, 6:7], stat[:, 5:6]), inc=sDVE)

            ntk = qi + 1
            cps = []

            def emit_tr(kt, buf=buf):
                A("tensor", [(sACT, exp_done), (sCP, sCP.n)],
                  lambda e, kt=kt, buf=buf: e.transpose(
                      pT[:, (kt % 4) * 128:(kt % 4 + 1) * 128],
                      Sstrip[buf][:, kt * 128:(kt + 1) * 128], identf),
                  inc=sPE)
                A("vector", [(sPE, sPE.n)],
                  lambda e, kt=kt: e.tensor_copy(
                      PTring[:, (kt % 4) * 128:(kt % 4 + 1) * 128],
                      pT[:, (kt % 4) * 128:(kt % 4 + 1) * 128]),
                  inc=sCP)
                cps.append(sCP.n)

            def emit_av(kt, qi=qi):
                for half in range(2):
                    wts = []
                    if half == 0:
                        wts = [(sCP, cps[kt])]
                        if kt == 0:
                            wts.append((sDVE, yv_free))
                    A("tensor", wts,
                      lambda e, kt=kt, half=half, qi=qi: e.matmul(
                          pB[:, half * 512:(half + 1) * 512],
                          PTring[:, (kt % 4) * 128:(kt % 4 + 1) * 128],
                          Vsb[:, kt * D + half * 512: kt * D + (half + 1) * 512],
                          start=(kt == 0), stop=(kt == qi)),
                      inc=(sPE if (kt == qi and half == 1) else None))

            for kt in range(ntk):
                emit_tr(kt)
                if kt >= 1:
                    emit_av(kt - 1)
            emit_av(ntk - 1)
            tr_fin[qi % 2] = cps[-1]
            av_done = sPE.n

            A("vector", [(sPE, av_done), (dX[qi % 2], xres_rdy)],
              lambda e, qi=qi: e.scalar_tensor_tensor(
                  xres[qi % 2], pB, stat[:, 6:7], xres[qi % 2], op0=ALU.mult, op1=ALU.add),
              inc=sDVE)
            yv_free = sDVE.n
            ydone = sDVE.n
            A("scalar", [(sDVE, ydone)],
              lambda e, qi=qi: e.activation(sqscrA, xres[qi % 2], AF.Square, accum_out=stat[:, 8:9]),
              inc=sACT)
            A("vector", [],
              lambda e, qi=qi: e.reduce_sum(stat[:, 7:8], xres[qi % 2], axis=AX.X), inc=sDVE)
            A("vector", [(sDVE, sDVE.n)],
              lambda e: e.tensor_scalar(stat[:, 9:10], stat[:, 7:8], 1.0 / D, None, op0=ALU.mult),
              inc=sDVE)
            A("vector", [(sACT, sACT.n), (sDVE, sDVE.n)],
              lambda e: e.tensor_scalar(stat[:, 10:11], stat[:, 8:9], 1.0 / D, None, op0=ALU.mult),
              inc=sDVE)
            A("vector", [(sDVE, sDVE.n)],
              lambda e: e.tensor_mul(stat[:, 11:12], stat[:, 9:10], stat[:, 9:10]),
              inc=sDVE)
            A("vector", [(sDVE, sDVE.n)],
              lambda e: e.tensor_sub(stat[:, 12:13], stat[:, 10:11], stat[:, 11:12]),
              inc=sDVE)
            A("scalar", [(sDVE, sDVE.n)],
              lambda e: e.activation(stat[:, 13:14], stat[:, 12:13], AF.Ln, bias=epsA[:, 0:1]))
            A("scalar", [],
              lambda e: e.activation(stat[:, 14:15], stat[:, 13:14], AF.Exp, bias=0.0, scale=-0.5),
              inc=sACT)
            A("vector", [(sACT, sACT.n), (sDVE, sDVE.n)],
              lambda e, qi=qi: e.tensor_scalar(
                  X1R[:, qi * D:(qi + 1) * D], xres[qi % 2],
                  stat[:, 9:10], stat[:, 14:15], op0=ALU.subtract, op1=ALU.mult),
              inc=sDVE)
            xr_free[qi % 2] = sDVE.n
        pl.flush(f"A2_s{s}")

        if upto == "A2d" and s == 0:
            A("sync", [(sDVE, sDVE.n), (sACT, sACT.n)],
              lambda e: e.dma_start(dbg_d[:, 0:2048], Sstrip[0]), inc=sO, k=16)
            A("vector", [(sO, sO.n)], lambda e: e.tensor_copy(Sstrip[1], Sstrip[0]), inc=sDVE)
            A("vector", [], lambda e: e.tensor_copy(Sstrip[1][:, 0:64], stat), inc=sDVE)
            A("sync", [(sDVE, sDVE.n)], lambda e: e.dma_start(dbg_d[:, 2048:4096], Sstrip[1]), inc=sO, k=16)
            pl.flush("dbgA2")
            break

        if upto == "X1":
            for st in range(NT):
                A("sync", [(sDVE, sDVE.n)],
                  lambda e, s=s, st=st: e.dma_start(
                      y_d[s, st * 128:(st + 1) * 128, :], X1R[:, st * D:(st + 1) * D]),
                  inc=sO, k=16)
            A("vector", [(sO, sO.n)], lambda e: e.memset(epsA[:, 4:5], 0.0), inc=sDVE)
            pl.flush(f"dbgX1_s{s}")
            continue

        A("sync", [], lambda e: e.dma_start(n1row, n1_d), inc=sS, k=16)
        n1_rdy = sS.n
        A("tensor", [(sS, n1_rdy)],
          lambda e: e.matmul(pC, onescol[0:1, 0:128], n1row[0:1, 0:512], start=True, stop=True))
        A("tensor", [],
          lambda e: e.matmul(a(0), onescol[0:1, 0:128], n1row[0:1, 512:1024], start=True, stop=True),
          inc=sPE)
        A("vector", [(sPE, sPE.n)], lambda e: e.tensor_copy(n1rep[:, 0:512], pC), inc=sDVE)
        A("vector", [], lambda e: e.tensor_copy(n1rep[:, 512:1024], a(0)), inc=sDVE)
        n1cp = sDVE.n
        h1ts = []
        h1ds = []

        for st in range(NT):
            A("scalar", [(sDVE, h1ds[st - 1] if st else sDVE.n)],
              lambda e, st=st: e.activation(
                  gbf, X1R[:, st * D:(st + 1) * D], AF.Square, accum_out=stat[:, 16:17]),
              inc=sACT)
            A("scalar", [],
              lambda e: e.activation(stat[:, 17:18], stat[:, 16:17], AF.Ln,
                                     bias=epsA[:, 1:2], scale=1.0 / D))
            A("scalar", [],
              lambda e: e.activation(stat[:, 18:19], stat[:, 17:18], AF.Exp, bias=0.0, scale=-0.5),
              inc=sACT)
            A("vector", [(sACT, sACT.n)],
              lambda e, st=st: e.tensor_scalar(
                  bscr, X1R[:, st * D:(st + 1) * D], stat[:, 18:19], None, op0=ALU.mult),
              inc=sDVE)
            A("vector", [], lambda e: e.tensor_mul(bscr, bscr, n1rep), inc=sDVE)
            h1done = sDVE.n
            h1ds.append(h1done)
            trslot = [pA[:, 1024:1152], pA[:, 1536:1664]]
            for c in range(8):
                wts = [(sCP, max(0, sCP.n - 1))]
                if c == 0:
                    wts.append((sDVE, h1done))
                A("tensor", wts,
                  lambda e, c=c, trslot=trslot: e.transpose(
                      trslot[c % 2], bscr[:, c * 128:(c + 1) * 128], identf),
                  inc=sPE)
                A("vector", [(sPE, sPE.n)],
                  lambda e, st=st, c=c, trslot=trslot: e.tensor_copy(
                      h1T[c][:, st * 128:(st + 1) * 128], trslot[c % 2]),
                  inc=sCP)
            h1ts.append(sCP.n)

        A("gpsimd", [],
          lambda e: e.dma_start(vwslot.rearrange("p (c j) -> p c j", c=8),
                                qkvT_d[:, 2 * D:3 * D].rearrange("(c p) j -> p c j", p=128)),
          inc=dWS, k=16)
        vw_rdy = dWS.n
        b1_free = {0: 0, 1: 0}
        for st in range(NT):
            vbank = pB if st % 2 == 0 else pA[:, 0:1024]
            for c in range(8):
                wts = []
                if c == 0:
                    wts = [(dWS, vw_rdy), (sCP, h1ts[st])]
                    if st == 1:
                        wts.append((sDVE, n1cp))
                    if st >= 2:
                        wts.append((sDVE, b1_free[st % 2]))
                for half in range(2):
                    A("tensor", wts if half == 0 else [],
                      lambda e, c=c, half=half, st=st, vbank=vbank: e.matmul(
                          vbank[:, half * 512:(half + 1) * 512],
                          h1T[c][:, st * 128:(st + 1) * 128],
                          vwslot[:, c * D + half * 512: c * D + (half + 1) * 512],
                          start=(c == 0), stop=(c == 7)),
                      inc=(sPE if (c == 7 and half == 1) else None))
            A("vector", [(sPE, sPE.n)],
              lambda e, st=st, vbank=vbank: e.tensor_copy(
                  Vplus[:, st * G65:(st + 1) * G65].rearrange("p (g w) -> p g w", w=65)[:, :, 0:64],
                  vbank.rearrange("p (g w) -> p g w", w=64)),
              inc=sDVE)
            A("vector", [],
              lambda e, st=st: e.memset(
                  Vplus[:, st * G65:(st + 1) * G65].rearrange("p (g w) -> p g w", w=65)[:, :, 64:65],
                  1.0),
              inc=sDVE)
            b1_free[st % 2] = sDVE.n
        pl.flush(f"B1_s{s}")

        ebuf_ready = [sPE.n, sPE.n, sPE.n]
        stb_free = {0: 0, 1: 0}
        av_free = {0: 0, 1: 0}
        todd_free = {0: 0, 1: 0}
        ps = {"qkw_free": {0: 0, 1: 0}, "qkpT_free": {0: 0, 1: 0},
              "pc_w": [], "pt_w": [], "evac": {}, "wrdy": {}, "ops": {}}

        def emit_qkdma(r):
            pb = r % 2
            for c in range(8):
                wts = [(sPE, ps["qkw_free"][pb])] if c == 0 else []
                A("gpsimd", wts,
                  lambda e, c=c, r=r, pb=pb: e.dma_start(
                      qkvw[pb][:, c * 256: c * 256 + 128],
                      qkvT_d[c * 128:(c + 1) * 128, r * 128:(r + 1) * 128]),
                  inc=dQW[pb], k=16)
                A("gpsimd", [],
                  lambda e, c=c, r=r, pb=pb: e.dma_start(
                      qkvw[pb][:, c * 256 + 128: c * 256 + 256],
                      qkvT_d[c * 128:(c + 1) * 128, D + r * 128: D + (r + 1) * 128]),
                  inc=dQW[pb], k=16)
            ps["wrdy"][r] = dQW[pb].n

        def emit_qkgrp(r, tgt, qq):
            pb = r % 2
            for c in range(8):
                wts = ([(dQW[pb], ps["wrdy"][r])] + ps["pc_w"]) if c == 0 else []
                A("tensor", wts,
                  lambda e, c=c, tgt=tgt, qq=qq, pb=pb: e.matmul(
                      pC[:, 0:512],
                      qkvw[pb][:, c * 256 + tgt * 128: c * 256 + tgt * 128 + 128],
                      h1T[c][:, qq * 512:(qq + 1) * 512],
                      start=(c == 0), stop=(c == 7)),
                  inc=(sPE if c == 7 else None))
            ps["qkw_free"][pb] = sPE.n
            A("scalar", [(sPE, sPE.n), (sPE, ps["qkpT_free"][pb])],
              lambda e, tgt=tgt, qq=qq, pb=pb: e.activation(
                  qkpT[pb][:, tgt * S + qq * 512: tgt * S + (qq + 1) * 512],
                  pC[:, 0:512], AF.Copy, bias=0.0, scale=(0.125 if tgt == 0 else 1.0)),
              inc=sACT)
            ps["evac"][(pb, tgt, qq)] = sACT.n
            ps["pc_w"] = [(sACT, sACT.n)]

        def emit_tmchunk(r, qt, kc):
            pb = r % 2
            kend2 = (qt + 1) * 128
            nkc2 = (kend2 + 511) // 512
            w2 = min(512, kend2 - kc * 512)
            wts = [(sACT, ps["evac"][(pb, 0, qt // 4)]),
                   (sACT, ps["evac"][(pb, 1, kc)])] + ps["pc_w"]
            A("tensor", wts,
              lambda e, qt=qt, kc=kc, w2=w2, pb=pb: e.matmul(
                  pC[:, 0:w2],
                  qkpT[pb][0:64, qt * 128:(qt + 1) * 128],
                  qkpT[pb][0:64, S + kc * 512: S + kc * 512 + w2],
                  start=True, stop=True, tile_position=(0, 0)))
            A("tensor", list(ps["pt_w"]),
              lambda e, qt=qt, kc=kc, w2=w2, pb=pb: e.matmul(
                  pT[:, 0:w2],
                  qkpT[pb][64:128, qt * 128:(qt + 1) * 128],
                  qkpT[pb][64:128, S + kc * 512: S + kc * 512 + w2],
                  start=True, stop=True, tile_position=(64, 0)),
              inc=sPE)
            mm2 = sPE.n
            if kc == nkc2 - 1:
                doff = qt * 128 - kc * 512
                A("vector", [(sPE, mm2)],
                  lambda e, doff=doff: e.tensor_add(
                      pC[:, doff:doff + 128], pC[:, doff:doff + 128], maskA),
                  inc=sDVE)
                A("vector", [],
                  lambda e, doff=doff: e.tensor_add(
                      pT[:, doff:doff + 128], pT[:, doff:doff + 128], maskA),
                  inc=sDVE)
            A("vector", [(sPE, mm2), (sDVE, sDVE.n)],
              lambda e, w2=w2: e.reduce_max(stat[:, 32:33], pC[:, 0:w2], axis=AX.X),
              inc=sDVE)
            ps["pc_w"] = [(sDVE, sDVE.n)]
            A("vector", [],
              lambda e, w2=w2: e.reduce_max(stat[:, 33:34], pT[:, 0:w2], axis=AX.X),
              inc=sDVE)
            ps["pt_w"] = [(sDVE, sDVE.n)]
            if kc == 0:
                A("vector", [(sDVE, sDVE.n)],
                  lambda e: e.tensor_copy(stat[:, 34:36], stat[:, 32:34]), inc=sDVE)
            else:
                A("vector", [(sDVE, sDVE.n)],
                  lambda e: e.tensor_max(stat[:, 34:36], stat[:, 34:36], stat[:, 32:34]),
                  inc=sDVE)
            if kc == nkc2 - 1:
                for hh in range(2):
                    mc = hh * 16 + qt
                    A("vector", [(sDVE, sDVE.n)],
                      lambda e, mc=mc, hh=hh: e.tensor_scalar(
                          mnegbf[:, mc: mc + 1],
                          stat[:, 34 + hh:35 + hh], -1.0, None, op0=ALU.mult),
                      inc=sDVE)

        def emit_dance(r):
            A("tensor", [(sDVE, sDVE.n)] + list(ps["pt_w"]),
              lambda e: e.transpose(pT[0:32, 0:128], mnegbf[:, 0:32], identf),
              inc=sPE)
            tpn = sPE.n
            cw = [(sPE, tpn)]
            if r >= 1:
                cw.append((sS, ps["ops"][r - 1]))
            A("vector", cw,
              lambda e: e.tensor_copy(negrow, pT[0:32, 0:128]), inc=sDVE)
            ps["pt_w"] = [(sDVE, sDVE.n)]
            A("sync", [(sDVE, sDVE.n)],
              lambda e: e.dma_start(negMrowA[0:1, 0:S], negrow[0:16, :]),
              inc=sS, k=16)
            A("sync", [],
              lambda e: e.dma_start(negMrowA[64:65, 0:S], negrow[16:32, :]),
              inc=sS, k=16)
            ps["ops"][r] = sS.n

        def emit_stav(pp, units):
            pb = pp % 2
            ops_rdy = ps["ops"][pp]
            uq = list(units)
            tot = len(uq)
            nslots = 72
            sl = [0]

            def pump():
                sl[0] += 1
                while uq and (tot - len(uq)) * nslots < sl[0] * tot:
                    uq.pop(0)()

            for qc in range(8):
                ntk = 2 * qc + 2
                avb = pB[:, 0:512] if qc % 2 == 0 else pB[:, 512:1024]

                def emit_sc(kt, qc=qc):
                    par = kt % 2
                    half = (kt == 2 * qc + 1)
                    qlo = 128 if half else 0
                    sb = [a(0), a(1)] if par == 0 else [a(2), a(3)]
                    for hh in range(2):
                        wts = [(sS, ops_rdy), (sACT, stb_free[par])] if hh == 0 else []
                        A("tensor", wts,
                          lambda e, hh=hh, kt=kt, qc=qc, qlo=qlo, sb=sb: e.matmul(
                              sb[hh][:, qlo:256],
                              qkpT[pb][hh * 64:(hh + 1) * 64, S + kt * 128:S + (kt + 1) * 128],
                              qkpT[pb][hh * 64:(hh + 1) * 64, qc * 256 + qlo:(qc + 1) * 256],
                              start=True, stop=False,
                              tile_position=(hh * 64, 0)))
                    for hh in range(2):
                        A("tensor", [],
                          lambda e, hh=hh, qc=qc, qlo=qlo, sb=sb: e.matmul(
                              sb[hh][:, qlo:256],
                              onescolb[hh * 64:hh * 64 + 1, 0:128],
                              negMrowA[hh * 64:hh * 64 + 1, qc * 256 + qlo:(qc + 1) * 256],
                              start=False, stop=True,
                              tile_position=(hh * 64, 0)),
                          inc=(sPE if hh == 1 else None))
                    sc_done = sPE.n
                    nmask = 0
                    if kt == 2 * qc:
                        for hh in range(2):
                            A("vector", [(sPE, sc_done)] if hh == 0 else [],
                              lambda e, hh=hh, sb=sb: e.tensor_add(
                                  sb[hh][:, 0:128], sb[hh][:, 0:128], maskB),
                              inc=sDVE)
                        nmask = 2
                    elif half:
                        for hh in range(2):
                            A("vector", [(sPE, sc_done)] if hh == 0 else [],
                              lambda e, hh=hh, sb=sb: e.tensor_add(
                                  sb[hh][:, 128:256], sb[hh][:, 128:256], maskB),
                              inc=sDVE)
                        nmask = 2
                    msk_done = sDVE.n
                    ewts = [(sPE, sc_done), (sPE, ebuf_ready[kt % 3])]
                    if nmask:
                        ewts.append((sDVE, msk_done))
                    for hh in range(2):
                        A("scalar", ewts if hh == 0 else [],
                          lambda e, hh=hh, kt=kt, qlo=qlo, sb=sb: e.activation(
                              Ebuf2[kt % 3][:, hh * 256 + qlo:(hh + 1) * 256],
                              sb[hh][:, qlo:256], AF.Exp),
                          inc=(sACT if hh == 1 else None))
                    return sACT.n

                def emit_avb(kt, qc=qc, exp_n=0, ntk=ntk, avb=avb):
                    half = (kt == 2 * qc + 1)
                    qlo = 128 if half else 0
                    for hh in range(2):
                        wts = []
                        if hh == 0:
                            wts = [(sACT, exp_n)]
                            if kt == 0:
                                wts.append((sDVE, av_free[qc % 2]))
                        A("tensor", wts,
                          lambda e, hh=hh, kt=kt, qc=qc, pp=pp, ntk=ntk, qlo=qlo, avb=avb: e.matmul(
                              avb[0:65, hh * 256 + qlo:(hh + 1) * 256],
                              Vplus[:, kt * G65 + (2 * pp + hh) * 65: kt * G65 + (2 * pp + hh) * 65 + 65],
                              Ebuf2[kt % 3][:, hh * 256 + qlo:(hh + 1) * 256],
                              start=(kt == 0 and hh == 0), stop=(kt == ntk - 1 and hh == 1)),
                          inc=(sPE if hh == 1 else None))
                    ebuf_ready[kt % 3] = sPE.n

                exps = []
                for kt in range(ntk):
                    if kt >= 2:
                        stb_free[kt % 2] = exps[kt - 2]
                    exps.append(emit_sc(kt))
                    pump()
                    if kt >= 1:
                        emit_avb(kt - 1, exp_n=exps[kt - 1])
                emit_avb(ntk - 1, exp_n=exps[ntk - 1])
                av_fin = sPE.n
                stb_free[(ntk - 2) % 2] = exps[ntk - 2]
                stb_free[(ntk - 1) % 2] = exps[ntk - 1]

                A("scalar", [(sPE, av_fin)],
                  lambda e, avb=avb: e.activation(
                      rsrow[0:1, 0:512], avb[64:65, 0:512],
                      AF.Ln, bias=eps1[0:1, 0:1]))
                A("scalar", [],
                  lambda e: e.activation(
                      rsrow[0:1, 0:512], rsrow[0:1, 0:512],
                      AF.Exp, bias=0.0, scale=-1.0),
                  inc=sACT)
                rs_done = sACT.n
                A("tensor", [(sACT, rs_done)],
                  lambda e, avb=avb: e.matmul(avb[64:128, 0:512], onescol[0:1, 0:64],
                                              rsrow, start=True, stop=True,
                                              tile_position=(0, 64)),
                  inc=sPE)
                bc_done = sPE.n
                A("vector", [(sPE, bc_done)],
                  lambda e, avb=avb: e.tensor_copy(rsb, avb[64:128, 0:512]),
                  inc=sDVE)
                A("vector", [],
                  lambda e, pp=pp, qc=qc, avb=avb: e.tensor_mul(
                      attnOT[pp][0:64, qc * 256:(qc + 1) * 256],
                      avb[0:64, 0:256], rsb[0:64, 0:256]),
                  inc=sDVE)
                A("vector", [(dTD[qc % 2], todd_free[qc % 2])],
                  lambda e, qc=qc, avb=avb: e.tensor_mul(
                      todd[0:64, (qc % 2) * 256:(qc % 2) * 256 + 256],
                      avb[0:64, 256:512], rsb[0:64, 256:512]),
                  inc=sDVE)
                av_free[qc % 2] = sDVE.n
                A("sync", [(sDVE, sDVE.n)],
                  lambda e, pp=pp, qc=qc: e.dma_start(
                      attnOT[pp][64:128, qc * 256:(qc + 1) * 256],
                      todd[0:64, (qc % 2) * 256:(qc % 2) * 256 + 256]),
                  inc=dTD[qc % 2], k=16)
                todd_free[qc % 2] = dTD[qc % 2].n
            while uq:
                uq.pop(0)()
            ps["qkpT_free"][pb] = sPE.n

        for r in range(NP + 1):
            units = []
            if r < NP:
                emit_qkdma(r)
                for qq in range(4):
                    units.append(lambda r=r, qq=qq: emit_qkgrp(r, 0, qq))
                    units.append(lambda r=r, qq=qq: emit_qkgrp(r, 1, qq))
                for qt in range(NT):
                    for kc in range((qt // 4) + 1):
                        units.append(lambda r=r, qt=qt, kc=kc: emit_tmchunk(r, qt, kc))
            if r == 0:
                for u in units:
                    u()
            else:
                emit_stav(r - 1, units)
            if r < NP:
                emit_dance(r)
        pl.flush(f"B23_s{s}")

        if upto == "B23d" and s == 0:
            for i in range(2):
                A("vector", [(sO, sO.n)],
                  lambda e, i=i: e.tensor_copy(dbgscr, attnOT[0][:, i * 1024:(i + 1) * 1024]),
                  inc=sDVE)
                A("sync", [(sDVE, sDVE.n)],
                  lambda e, i=i: e.dma_start(dbg_d[:, i * 1024:(i + 1) * 1024], dbgscr), inc=sO, k=16)
            A("vector", [(sO, sO.n)], lambda e: e.tensor_copy(dbgscr[:, 0:32], mnegbf), inc=sDVE)
            A("sync", [(sDVE, sDVE.n)], lambda e: e.dma_start(dbg_d[:, 2048:2112], dbgscr[:, 0:64]),
              inc=sO, k=16)
            A("vector", [(sO, sO.n)], lambda e: e.tensor_copy(dbgscr[:, 0:256], h1T[0][:, 0:256]),
              inc=sDVE)
            A("sync", [(sDVE, sDVE.n)], lambda e: e.dma_start(dbg_d[:, 2304:2560], dbgscr[:, 0:256]),
              inc=sO, k=16)
            A("vector", [(sO, sO.n)], lambda e: e.tensor_copy(dbgscr[:, 0:1024], Vplus[:, 0:1024]),
              inc=sDVE)
            A("sync", [(sDVE, sDVE.n)], lambda e: e.dma_start(dbg_d[:, 2560:3584], dbgscr[:, 0:1024]),
              inc=sO, k=16)
            A("vector", [(sO, sO.n)], lambda e: e.tensor_copy(dbgscr[:, 0:1024], qkpT[0][:, 0:1024]),
              inc=sDVE)
            A("sync", [(sDVE, sDVE.n)], lambda e: e.dma_start(dbg_d[:, 3072:4096], dbgscr[:, 0:1024]),
              inc=sO, k=16)
            pl.flush("dbgB23")
            break

        A("gpsimd", [],
          lambda e: e.dma_start(owT.rearrange("p (c j) -> p c j", c=8),
                                owT_d.rearrange("(c p) j -> p c j", p=128)),
          inc=sG, k=16)
        ow_rdy = sG.n
        attn_fin = [(sS, sS.n), (sDVE, sDVE.n), (dTD[0], dTD[0].n), (dTD[1], dTD[1].n)]
        b4_free = {0: 0, 1: 0}
        for st in range(NT):
            bank = pB if st % 2 == 0 else pA[:, 0:1024]
            for c in range(8):
                wts = []
                if c == 0:
                    wts = [(sG, ow_rdy)] + attn_fin + [(sDVE, b4_free[st % 2])]
                for half in range(2):
                    A("tensor", wts if half == 0 else [],
                      lambda e, c=c, half=half, st=st, bank=bank: e.matmul(
                          bank[:, half * 512:(half + 1) * 512],
                          attnOT[c][:, st * 128:(st + 1) * 128],
                          owT[:, c * D + half * 512: c * D + (half + 1) * 512],
                          start=(c == 0), stop=(c == 7)),
                      inc=(sPE if (c == 7 and half == 1) else None))
            A("vector", [(sPE, sPE.n)],
              lambda e, st=st, bank=bank: e.tensor_add(
                  X2[:, st * D:(st + 1) * D], bank, X1R[:, st * D:(st + 1) * D]),
              inc=sDVE)
            A("vector", [],
              lambda e, st=st: e.tensor_add(
                  X1R[:, st * D:(st + 1) * D], X2[:, st * D:(st + 1) * D],
                  X1R[:, st * D:(st + 1) * D]),
              inc=sDVE)
            b4_free[st % 2] = sDVE.n
        pl.flush(f"B4_s{s}")

        if upto == "X2":
            for st in range(NT):
                A("sync", [(sDVE, sDVE.n)],
                  lambda e, s=s, st=st: e.dma_start(
                      y_d[s, st * 128:(st + 1) * 128, :], X2[:, st * D:(st + 1) * D]),
                  inc=sO, k=16)
            A("vector", [(sO, sO.n)], lambda e: e.memset(epsA[:, 4:5], 0.0), inc=sDVE)
            pl.flush(f"dbgX2_s{s}")
            continue

        A("sync", [], lambda e: e.dma_start(n2row, n2_d), inc=sS, k=16)
        n2_rdy = sS.n
        A("tensor", [(sS, n2_rdy)],
          lambda e: e.matmul(pC, onescol[0:1, 0:128], n2row[0:1, 0:512], start=True, stop=True))
        A("tensor", [],
          lambda e: e.matmul(a(0), onescol[0:1, 0:128], n2row[0:1, 512:1024], start=True, stop=True),
          inc=sPE)
        A("vector", [(sPE, sPE.n)], lambda e: e.tensor_copy(n2rep[:, 0:512], pC), inc=sDVE)
        A("vector", [], lambda e: e.tensor_copy(n2rep[:, 512:1024], a(0)), inc=sDVE)
        n2cp = sDVE.n
        gts = []
        fdone = []
        gd = []
        sqa = []
        A("gpsimd", [],
          lambda e: e.dma_start(gateT_sb.rearrange("p (c j) -> p c j", c=8),
                                gateT_d.rearrange("(c p) j -> p c j", p=128)),
          inc=sG, k=16)
        gate_rdy = sG.n
        ge_n = 0
        rm_n = n2cp
        for st in range(NT):
            A("scalar", [(sDVE, gd[st - 1] if st else sDVE.n)],
              lambda e, st=st: e.activation(
                  gbf, X2[:, st * D:(st + 1) * D], AF.Square, accum_out=stat[:, 20:21]),
              inc=sACT)
            A("scalar", [],
              lambda e: e.activation(stat[:, 21:22], stat[:, 20:21], AF.Ln,
                                     bias=epsA[:, 1:2], scale=1.0 / D))
            A("scalar", [],
              lambda e: e.activation(stat[:, 22:23], stat[:, 21:22], AF.Exp, bias=0.0, scale=-0.5),
              inc=sACT)
            sqa.append(sACT.n)
            A("vector", [(sACT, sACT.n)],
              lambda e, st=st: e.tensor_scalar(
                  bscr2, X2[:, st * D:(st + 1) * D], stat[:, 22:23], None, op0=ALU.mult),
              inc=sDVE)
            A("vector", [], lambda e: e.tensor_mul(bscr2, bscr2, n2rep), inc=sDVE)
            gdone = sDVE.n
            gd.append(gdone)
            trslot = [pA[:, 1024:1152], pA[:, 1536:1664]]
            for c in range(8):
                wts = [(sCP, max(0, sCP.n - 1))]
                if c == 0:
                    wts.append((sDVE, gdone))
                A("tensor", wts,
                  lambda e, c=c, trslot=trslot: e.transpose(
                      trslot[c % 2], bscr2[:, c * 128:(c + 1) * 128], identf),
                  inc=sPE)
                A("vector", [(sPE, sPE.n)],
                  lambda e, st=st, c=c, trslot=trslot: e.tensor_copy(
                      gT[c][:, st * 128:(st + 1) * 128], trslot[c % 2]),
                  inc=sCP)
            gt_done = sCP.n
            gts.append(gt_done)
            for c in range(8):
                wts = [(sCP, gt_done), (sG, gate_rdy), (sDVE, rm_n), (sACT, ge_n)] if c == 0 else []
                A("tensor", wts,
                  lambda e, c=c, st=st: e.matmul(
                      pC[:, 0:NE], gT[c][:, st * 128:(st + 1) * 128],
                      gateT_sb[:, c * NE:(c + 1) * NE], start=(c == 0), stop=(c == 7)),
                  inc=(sPE if c == 7 else None))
            A("vector", [(sPE, sPE.n)],
              lambda e: e.reduce_max(stat[:, 24:25], pC[:, 0:NE], axis=AX.X, negate=True),
              inc=sDVE)
            rm_n = sDVE.n
            A("scalar", [(sDVE, sDVE.n)],
              lambda e: e.activation(stat[:, 32:40], pC[:, 0:NE], AF.Exp,
                                     bias=stat[:, 24:25], accum_out=stat[:, 25:26]),
              inc=sACT)
            ge_n = sACT.n
            A("vector", [(sACT, sACT.n)],
              lambda e: e.reciprocal(stat[:, 26:27], stat[:, 25:26]), inc=sDVE)
            A("vector", [(sDVE, sDVE.n)],
              lambda e: e.tensor_scalar(stat[:, 32:40], stat[:, 32:40], stat[:, 26:27], None,
                                        op0=ALU.mult),
              inc=sDVE)
            A("vector", [(sDVE, sDVE.n)], lambda e: e.reduce_max(stat[:, 27:28], stat[:, 32:40], axis=AX.X),
              inc=sDVE)
            A("vector", [(sDVE, sDVE.n)],
              lambda e: e.tensor_scalar(stat[:, 40:48], stat[:, 32:40], stat[:, 27:28], None,
                                        op0=ALU.subtract),
              inc=sDVE)
            A("vector", [(sDVE, sDVE.n)],
              lambda e: e.tensor_scalar(stat[:, 40:48], stat[:, 40:48], -1e30, 1.0,
                                        op0=ALU.mult, op1=ALU.min),
              inc=sDVE)
            A("vector", [(sDVE, sDVE.n)],
              lambda e: e.tensor_scalar(stat[:, 40:48], stat[:, 40:48], 1.0, 1e30,
                                        op0=ALU.subtract, op1=ALU.mult),
              inc=sDVE)
            A("vector", [(sDVE, sDVE.n)], lambda e: e.tensor_add(stat[:, 40:48], stat[:, 32:40], stat[:, 40:48]),
              inc=sDVE)
            A("vector", [(sDVE, sDVE.n)], lambda e: e.reduce_max(stat[:, 28:29], stat[:, 40:48], axis=AX.X),
              inc=sDVE)
            A("vector", [(sDVE, sDVE.n)], lambda e: e.tensor_add(stat[:, 29:30], stat[:, 27:28], stat[:, 28:29]),
              inc=sDVE)
            A("vector", [(sDVE, sDVE.n)],
              lambda e: e.tensor_scalar(stat[:, 30:31], stat[:, 29:30], 1e-9, None, op0=ALU.add),
              inc=sDVE)
            A("vector", [(sDVE, sDVE.n)], lambda e: e.reciprocal(stat[:, 31:32], stat[:, 30:31]), inc=sDVE)
            A("vector", [(sDVE, sDVE.n)],
              lambda e, st=st: e.tensor_mul(factor[:, st:st + 1], stat[:, 29:30], stat[:, 31:32]),
              inc=sDVE)
            fdone.append(sDVE.n)
        b5cp = sCP.n

        ostore_free = {0: dY[0].n, 1: dY[1].n}

        def emit_b7(st):
            A("scalar", [(sDVE, sDVE.n)],
              lambda e, st=st: e.activation(
                  lnscr, X1R[:, st * D:(st + 1) * D], AF.Square, accum_out=stat[:, 50:51]),
              inc=sACT)
            A("vector", [(sACT, sACT.n)],
              lambda e, st=st: e.reduce_sum(stat[:, 49:50], X1R[:, st * D:(st + 1) * D], axis=AX.X),
              inc=sDVE)
            A("vector", [(sDVE, sDVE.n)],
              lambda e: e.tensor_scalar(stat[:, 51:52], stat[:, 49:50], 1.0 / D, None, op0=ALU.mult),
              inc=sDVE)
            A("vector", [(sDVE, sDVE.n)],
              lambda e: e.tensor_scalar(stat[:, 52:53], stat[:, 50:51], 1.0 / D, None, op0=ALU.mult),
              inc=sDVE)
            A("vector", [(sDVE, sDVE.n)], lambda e: e.tensor_mul(stat[:, 53:54], stat[:, 51:52], stat[:, 51:52]),
              inc=sDVE)
            A("vector", [(sDVE, sDVE.n)], lambda e: e.tensor_sub(stat[:, 54:55], stat[:, 52:53], stat[:, 53:54]),
              inc=sDVE)
            A("scalar", [(sDVE, sDVE.n)],
              lambda e: e.activation(stat[:, 55:56], stat[:, 54:55], AF.Ln, bias=epsA[:, 0:1]))
            A("scalar", [],
              lambda e: e.activation(stat[:, 56:57], stat[:, 55:56], AF.Exp, bias=0.0, scale=-0.5),
              inc=sACT)
            A("vector", [(sACT, sACT.n), (sDVE, sDVE.n), (dY[st % 2], ostore_free[st % 2])],
              lambda e, st=st: e.tensor_scalar(
                  outst[:, (st % 2) * D:(st % 2 + 1) * D], X1R[:, st * D:(st + 1) * D],
                  stat[:, 51:52], stat[:, 56:57], op0=ALU.subtract, op1=ALU.mult),
              inc=sDVE)
            A("sync", [(sDVE, sDVE.n)],
              lambda e, s=s, st=st: e.dma_start(
                  y_d[s, st * 128:(st + 1) * 128, :], outst[:, (st % 2) * D:(st % 2 + 1) * D]),
              inc=dY[st % 2], k=16)
            ostore_free[st % 2] = dY[st % 2].n

        up_free = {0: 0, 1: 0}
        dn_free = {0: 0, 1: 0}
        for ep in range(4):
            for c in range(8):
                if ep >= 2:
                    wts = [(sACT, sACT.n), (sPE, sPE.n)] if c == 0 else []
                else:
                    t = ep * 4 + c // 2
                    wts = [(sDVE, gd[t]), (sACT, sqa[t])]
                A("gpsimd", wts,
                  lambda e, c=c, ep=ep: e.dma_start(
                      upw[ep % 2][:, c * KB:(c + 1) * KB],
                      upT_d[c * 128:(c + 1) * 128, ep * KB:(ep + 1) * KB]),
                  inc=dUW[ep % 2], k=16)
            for cc in range(8):
                if ep >= 2:
                    wts = []
                else:
                    t = 8 + ep * 4 + cc // 2
                    wts = [(sDVE, gd[t]), (sACT, sqa[t])]
                A("gpsimd", wts,
                  lambda e, cc=cc, ep=ep: e.dma_start(
                      dww[ep % 2][:, cc * KB:(cc + 1) * KB],
                      downT_d[ep * KB + cc * 128: ep * KB + (cc + 1) * 128, :]),
                  inc=dUW[ep % 2], k=16)
            w_rdy = dUW[ep % 2].n
            for sc in range(4):
                for et in range(8):
                    bank = a(0) if et % 2 == 0 else a(1)
                    for c in range(8):
                        wts = []
                        if c == 0:
                            wts = [(dUW[ep % 2], w_rdy), (sACT, up_free[et % 2]),
                                   (sCP, gts[4 * sc + 3])]
                            if ep == 0 and sc == 0 and et == 0:
                                wts.append((sDVE, n2cp))
                        A("tensor", wts,
                          lambda e, c=c, et=et, sc=sc, ep=ep, bank=bank: e.matmul(
                              bank, upw[ep % 2][:, c * KB + et * 128: c * KB + (et + 1) * 128],
                              gT[c][:, sc * 512:(sc + 1) * 512],
                              start=(c == 0), stop=(c == 7)),
                          inc=(sPE if c == 7 else None))
                    A("scalar", [(sPE, sPE.n)],
                      lambda e, et=et, sc=sc, bank=bank: e.activation(
                          siluT[sc % 2][:, et * 512:(et + 1) * 512], bank, AF.Silu),
                      inc=sACT)
                    up_free[et % 2] = sACT.n
                silu_done = sACT.n
                for st2 in range(4):
                    st = sc * 4 + st2
                    bank = pB if st2 % 2 == 0 else pA[:, 1024:2048]
                    for et in range(8):
                        wts = []
                        if et == 0:
                            wts = [(sACT, silu_done), (sDVE, dn_free[st2 % 2]),
                                   (dUW[ep % 2], w_rdy)]
                            if ep == 0 and sc == 0 and st2 == 1:
                                wts.append((sCP, b5cp))
                        for half in range(2):
                            A("tensor", wts if half == 0 else [],
                              lambda e, et=et, half=half, st2=st2, sc=sc, ep=ep, bank=bank: e.matmul(
                                  bank[:, half * 512:(half + 1) * 512],
                                  siluT[sc % 2][:, et * 512 + st2 * 128: et * 512 + st2 * 128 + 128],
                                  dww[ep % 2][:, et * KB + half * 512: et * KB + (half + 1) * 512],
                                  start=(et == 0), stop=(et == 7)),
                              inc=(sPE if (et == 7 and half == 1) else None))
                    A("vector", [(sPE, sPE.n), (sDVE, fdone[st])],
                      lambda e, st=st, bank=bank: e.scalar_tensor_tensor(
                          X1R[:, st * D:(st + 1) * D], bank, factor[:, st:st + 1],
                          X1R[:, st * D:(st + 1) * D], op0=ALU.mult, op1=ALU.add),
                      inc=sDVE)
                    dn_free[st2 % 2] = sDVE.n
                    if ep == 3:
                        emit_b7(st)
        pl.flush(f"B6_s{s}")


    with nc.Block("fin") as blk:
        @blk.sync
        def _(e):
            e.wait_ge(sO.h, sO.n)
            e.wait_ge(dX[0].h, dX[0].n)
            e.wait_ge(dX[1].h, dX[1].n)
            e.wait_ge(dY[0].h, dY[0].n)
            e.wait_ge(dY[1].h, dY[1].n)

    nc.finalize()
    return nc


_NC_CACHE = {}


def _get_nc(upto):
    if upto not in _NC_CACHE:
        _NC_CACHE[upto] = build_program(upto)
    return _NC_CACHE[upto]


def make_consts():
    identbf = np.eye(128, dtype=ml_dtypes.bfloat16)
    identf = np.eye(128, dtype=np.float32)
    onescol = np.ones((1, 128), dtype=np.float32)
    onescolb = np.ones((128, 128), dtype=ml_dtypes.bfloat16)
    ones32 = np.ones((128, 32), dtype=ml_dtypes.bfloat16)
    i = np.arange(128).reshape(128, 1)
    c = np.arange(2048).reshape(1, 2048)
    rampA = (c - C0 - i).astype(np.float32)
    j = np.arange(128).reshape(1, 128)
    maskA = np.where(j > i, -1e30, 0.0).astype(np.float32)
    maskB = np.where(i > j, -1e30, 0.0).astype(np.float32)
    return identbf, identf, onescol, onescolb, ones32, rampA, maskA, maskB


def kernel(L0_attn_x, L0_attn_W_qk, L0_attn_W_v, L0_attn_alibi_slope,
           L0_ffn_n1_w, L0_ffn_qkv_w, L0_ffn_o_w, L0_ffn_n2_w,
           L0_ffn_gate_w, L0_ffn_up_w, L0_ffn_down_w, _upto="full", _ncores=8):
    X = np.ascontiguousarray(np.asarray(L0_attn_x, dtype=np.float32))
    wqk = np.ascontiguousarray(np.asarray(L0_attn_W_qk, dtype=np.float32))
    wvT = np.ascontiguousarray(np.asarray(L0_attn_W_v, dtype=np.float32).T)
    slope = np.asarray(L0_attn_alibi_slope, dtype=np.float32).reshape(1, 1)
    n1 = np.asarray(L0_ffn_n1_w, dtype=np.float32).reshape(1, D)
    n2 = np.asarray(L0_ffn_n2_w, dtype=np.float32).reshape(1, D)
    qkvT = np.ascontiguousarray(np.asarray(L0_ffn_qkv_w, dtype=np.float32).T)
    owT = np.ascontiguousarray(np.asarray(L0_ffn_o_w, dtype=np.float32).T)
    gateT = np.ascontiguousarray(np.asarray(L0_ffn_gate_w, dtype=np.float32).T)
    upT = np.ascontiguousarray(np.asarray(L0_ffn_up_w, dtype=np.float32).T)
    downT = np.ascontiguousarray(np.asarray(L0_ffn_down_w, dtype=np.float32).T)
    identbf, identf, onescol, onescolb, ones32, rampA, maskA, maskB = make_consts()

    nc = _get_nc(_upto)
    in_maps = []
    for i in range(_ncores):
        in_maps.append({
            "x": np.ascontiguousarray(X[2 * i:2 * i + 2]),
            "wqk": wqk, "wvT": wvT, "slope": slope, "n1": n1, "n2": n2,
            "qkvT": qkvT, "owT": owT, "gateT": gateT, "upT": upT, "downT": downT,
            "identbf": identbf, "identf": identf, "onescol": onescol,
            "onescolb": onescolb, "ones32": ones32,
            "rampA": rampA, "maskA": maskA, "maskB": maskB,
        })
    res = run_bass_kernel_spmd(nc, in_maps, core_ids=list(range(_ncores)))
    if _upto in ("A01", "setup", "A2d", "B23d"):
        return res.results[0]["dbg"]
    out = np.concatenate([r["y"] for r in res.results], axis=0)
    return out.reshape(2 * _ncores, S, D).astype(np.float32)
